# version 02
# baseline (speedup 1.0000x reference)
import contextlib
import math
import numpy as np
import ml_dtypes

import concourse.bass as bass
import concourse.mybir as mybir
from concourse.bass_utils import run_bass_kernel_spmd

F32 = mybir.dt.float32
BF16 = mybir.dt.bfloat16
AF = mybir.ActivationFunctionType
ALU = mybir.AluOpType
AX = mybir.AxisListType
BF = ml_dtypes.bfloat16

NCORE = 8
B = 2
T = 2048
D = 1024
NT = T // 128
NC = D // 128

ENGS = ("tensor", "vector", "scalar", "gpsimd", "sync")


class Sem:
    __slots__ = ("h", "n", "name")

    def __init__(self, h, name):
        self.h = h
        self.n = 0
        self.name = name


class Prog:

    def __init__(self, name="mk"):
        self.nc = bass.Bass("TRN2", target_bir_lowering=False)
        self.es = contextlib.ExitStack()
        self.q = {e: [] for e in ENGS}
        self.pipe = {e: self.sem("pipe_" + e)
                     for e in ("vector", "scalar", "gpsimd", "tensor")}
        self.DPOOL = 10
        self.dpool = {e: [self.sem(f"dma_{e}{i}") for i in range(self.DPOOL)]
                      for e in ("sync", "scalar", "gpsimd")}
        self.didx = {e: 0 for e in ("sync", "scalar", "gpsimd")}
        self.dlast = {e: [None] * self.DPOOL for e in ("sync", "scalar", "gpsimd")}
        self.waited = {e: {} for e in ENGS}
        self._names = set()
        self.init_toks = []

    def sem(self, name):
        h = self.es.enter_context(self.nc.semaphore(name))
        return Sem(h, name)

    def push_scope(self):
        self._scopes = getattr(self, "_scopes", [])
        self._scopes.append(contextlib.ExitStack())

    def pop_scope(self):
        self._scopes.pop().close()

    def _alloc_stack(self):
        scopes = getattr(self, "_scopes", None)
        return scopes[-1] if scopes else self.es

    def _nm(self, name):
        base, i = name, 0
        while name in self._names:
            i += 1
            name = f"{base}_{i}"
        self._names.add(name)
        return name

    def sb(self, name, shape, dtype):
        st_ = self._alloc_stack()
        if st_ is self.es:
            return self.nc.alloc_sbuf_tensor(self._nm(name), shape, dtype)
        return st_.enter_context(
            self.nc.sbuf_tensor(self._nm(name), shape, dtype))

    def ps(self, name, shape, dtype=F32):
        st_ = self._alloc_stack()
        if st_ is self.es:
            return self.nc.alloc_psum_tensor(self._nm(name), shape, dtype)
        return st_.enter_context(
            self.nc.psum_tensor(self._nm(name), shape, dtype))

    def dram(self, name, shape, dtype, kind="Internal"):
        if kind == "Internal":
            name = self._nm("dr_" + name)
        return self.nc.dram_tensor(name, shape, dtype, kind=kind).ap()

    def constcol(self, value):
        if not hasattr(self, "_ccol"):
            self._ccol = {}
        if value not in self._ccol:
            t = self.nc.alloc_sbuf_tensor(self._nm(f"ccol_{len(self._ccol)}"),
                                          [128, 1], F32)
            tok = self.v(lambda v, t=t, value=value: v.memset(t[:], value))
            self.init_toks.append(tok)
            self._ccol[value] = t[:, 0:1]
        return self._ccol[value]

    def _emit(self, eng, fn, after, incs):
        waits = []
        wl = self.waited[eng]
        for tok in list(self.init_toks) + list(after):
            if tok is None:
                continue
            s, v = tok
            if v > 0 and wl.get(s.name, -1) < v:
                waits.append((s, v))
                wl[s.name] = v

        def thunk(e, waits=waits, incs=incs, fn=fn):
            for (s, v) in waits:
                e.wait_ge(s.h, v)
            inst = fn(e)
            for (s, k) in incs:
                inst.then_inc(s.h, k)

        self.q[eng].append(thunk)

    def _compute(self, eng, fn, after=(), sync=True):
        incs = []
        if sync:
            p = self.pipe[eng]
            p.n += 1
            incs.append((p, 1))
        self._emit(eng, fn, after, incs)
        return (self.pipe[eng], self.pipe[eng].n) if sync else None

    def v(self, fn, after=(), sync=True):
        return self._compute("vector", fn, after, sync)

    def s(self, fn, after=(), sync=True):
        return self._compute("scalar", fn, after, sync)

    def g(self, fn, after=(), sync=True):
        return self._compute("gpsimd", fn, after, sync)

    def t(self, fn, after=(), sync=True):
        return self._compute("tensor", fn, after, sync)

    def dma(self, queue, out, in_, after=(), transpose=False):
        i = self.didx[queue] % self.DPOOL
        self.didx[queue] += 1
        ds = self.dpool[queue][i]
        prev = self.dlast[queue][i]
        ds.n += 16

        def fn(e, out=out, in_=in_, transpose=transpose):
            return e.dma_start(out=out, in_=in_, transpose=transpose)

        self._emit(queue, fn, list(after) + ([prev] if prev else []), [(ds, 16)])
        tok = (ds, ds.n)
        self.dlast[queue][i] = tok
        return tok

    def wait(self, eng, after):
        self._emit(eng, lambda e: e.nop(), after, [])

    def barrier(self):
        if not hasattr(self, "_barsem"):
            self._barsem = self.sem("bar")
        bs = self._barsem
        dtoks = []
        for q in self.dpool:
            dtoks += [(s, s.n) for s in self.dpool[q] if s.n > 0]
        for e in ("tensor", "vector", "scalar", "gpsimd", "sync"):
            self._emit(e, lambda en: en.nop(), dtoks, [(bs, 1)])
            bs.n += 1
        for e in ("tensor", "vector", "scalar", "gpsimd", "sync"):
            self.wait(e, [(bs, bs.n)])

    def finalize(self):
        with self.nc.Block() as block:
            for eng in ENGS:
                if self.q[eng]:
                    def body(e, eng=eng):
                        for t in self.q[eng]:
                            t(e)
                    getattr(block, eng)(body)
        self.es.close()
        return self.nc


class Ring:

    def __init__(self, p, name, shape, dtype, n=2):
        self.bufs = [p.sb(f"{name}{i}", shape, dtype) for i in range(n)]
        self.guards = [[] for _ in range(n)]
        self.i = -1

    def next(self):
        self.i = (self.i + 1) % len(self.bufs)
        return self.bufs[self.i]

    @property
    def guard(self):
        return self.guards[self.i]

    def set_guard(self, toks):
        self.guards[self.i] = list(toks)



def host_consts():
    c = {}
    c["ones1_f32"] = np.ones((1, 128), dtype=np.float32)
    c["ones1_bf"] = np.ones((1, 128), dtype=np.float32).astype(BF)
    iu = np.triu(np.ones((128, 128), dtype=np.float32), 1)
    c["maskbias_causal"] = (-1e30 * iu).astype(np.float32)
    c["ident2_bf"] = (2.0 * np.eye(128, dtype=np.float32)).astype(BF)
    c["ident_bf"] = np.eye(128, dtype=np.float32).astype(BF)
    half = 32
    inv = 1.0 / (10000.0 ** (np.arange(half, dtype=np.float32) / half))
    ang = np.arange(T, dtype=np.float32)[:, None] * inv[None, :]
    cos1, sin1 = np.cos(ang), np.sin(ang)
    c["rope0_cos"] = np.tile(np.concatenate([cos1, cos1], 1), (1, 16)).astype(np.float32)
    c["rope0_sin"] = np.tile(np.concatenate([-sin1, sin1], 1), (1, 16)).astype(np.float32)
    c["ident_f32"] = np.eye(128, dtype=np.float32)
    c["iota512_row"] = np.arange(512, dtype=np.float32)[None, :]
    c["iota128_col"] = np.arange(128, dtype=np.float32)[:, None]
    c["ropeb_cos"] = np.concatenate([cos1, cos1], 1).astype(np.float32)
    c["ropeb_sin"] = np.concatenate([-sin1, sin1], 1).astype(np.float32)
    c["ropeb_cos_bf"] = c["ropeb_cos"].astype(BF)
    c["ropeb_sin_bf"] = c["ropeb_sin"].astype(BF)
    c["maskbias_anticausal"] = (-1e30 * np.tril(
        np.ones((128, 128), dtype=np.float32), -1)).astype(np.float32)
    half_idx = np.arange(0, 64, 2, dtype=np.float32)
    inv_freq = 1.0 / 10000.0 ** (half_idx / 64)
    tpos = np.arange(T, dtype=np.float32)
    freqs = tpos[:, None] * inv_freq[None, :]
    base = (half_idx + 0.4 * 64) / (1.4 * 64)
    power = (tpos - T // 2) / 256.0
    xsc = base[None, :] ** power[:, None]
    cosf, sinf = np.cos(freqs), np.sin(freqs)
    c["xpos_qc"] = np.concatenate([cosf * xsc, cosf * xsc], 1).astype(np.float32)
    c["xpos_qs"] = np.concatenate([-sinf * xsc, sinf * xsc], 1).astype(np.float32)
    c["xpos_kc"] = np.concatenate([cosf / xsc, cosf / xsc], 1).astype(np.float32)
    c["xpos_ks"] = np.concatenate([-sinf / xsc, sinf / xsc], 1).astype(np.float32)
    for k2 in ("xpos_qc", "xpos_qs", "xpos_kc", "xpos_ks"):
        c[k2 + "_bf"] = c[k2].astype(BF)
    c["rope1c"] = np.ascontiguousarray(
        np.broadcast_to(c["ropeb_cos"][None], (B, T, 64))).astype(np.float32)
    c["rope1s"] = np.ascontiguousarray(
        np.broadcast_to(c["ropeb_sin"][None], (B, T, 64))).astype(np.float32)
    return c


_CONSTS = host_consts()


def const_ap(p, name):
    arr = _CONSTS[name]
    dt = BF16 if arr.dtype == BF else F32
    return p.dram("c_" + name, list(arr.shape), dt, kind="ExternalInput")


def const_feed(names):
    return {"c_" + n: np.ascontiguousarray(_CONSTS[n]) for n in names}




def ln_tile(p, xs, stb, sqb, after, kind="layernorm", eps=1e-5, nfeat=D):
    if kind == "layernorm":
        t1 = p.v(lambda v: v.reduce_sum(stb[:, 0:1], xs[:], axis=AX.X, negate=True),
                 after=after)
        t2 = p.v(lambda v: v.tensor_scalar_mul(stb[:, 1:2], stb[:, 0:1], 1.0 / nfeat),
                 after=[t1])
        t3 = p.v(lambda v: v.tensor_scalar(
            out=xs[:], in0=xs[:], scalar1=stb[:, 1:2], scalar2=None, op0=ALU.add),
            after=[t2])
        base = [t3]
    else:
        base = list(after)
    cc = p.constcol(eps)
    t5 = p.s(lambda s: s.activation(sqb[:], xs[:], AF.Square), after=base)
    t6 = p.v(lambda v: v.reduce_sum(stb[:, 2:3], sqb[:], axis=AX.X), after=[t5])
    t7 = p.s(lambda s, cc=cc: s.activation(stb[:, 3:4], stb[:, 2:3], AF.Sqrt,
                                           scale=1.0 / nfeat, bias=cc), after=[t6])
    t8 = p.v(lambda v: v.reciprocal(stb[:, 3:4], stb[:, 3:4]), after=[t7])
    t9 = p.v(lambda v: v.tensor_scalar(
        out=xs[:], in0=xs[:], scalar1=stb[:, 3:4], scalar2=None, op0=ALU.mult),
        after=[t8])
    return t9, stb[:, 3:4]


def store_fm(p, xnb, dst_fm_b, ti, after, tmp, nch=NC):
    tt = p.dma("sync",
               tmp[:, :nch * 128].rearrange("p (c t) -> p c t", c=nch),
               xnb[:].rearrange("p (c t) -> p c t", c=nch),
               after=after, transpose=True)
    o2 = p.dma("sync",
               dst_fm_b[:, :, ti * 128:(ti + 1) * 128].rearrange("c d t -> d c t"),
               tmp[:, :nch * 128].rearrange("d (c t) -> d c t", c=nch),
               after=[tt])
    return o2


class FMWeight:

    def __init__(self, p, tag, w_dram, din, e, gamma=None, ring=None, slots=None):
        self.chunks = []
        self.tok = []
        nch = din // 128
        f32b = ring or Ring(p, tag + "_wf32", [128, e], F32, n=2)
        guard = slots[1]["toks"] if slots else []
        for c in range(nch):
            wb = (slots[0][c][:, 0:e] if slots
                  else p.sb(f"{tag}_w{c}", [128, e], BF16)[:, 0:e])
            dst = f32b.next()
            ld = p.dma("scalar", dst[:, 0:e], w_dram[c * 128:(c + 1) * 128, :],
                       after=f32b.guard)
            src = dst[:, 0:e]
            if gamma is not None:
                tk = p.v(lambda v, wb=wb, src=src, c=c: v.tensor_scalar(
                    out=wb, in0=src, scalar1=gamma[:, c:c + 1], scalar2=None,
                    op0=ALU.mult), after=[ld] + guard)
            else:
                tk = p.v(lambda v, wb=wb, src=src: v.tensor_copy(wb, src),
                         after=[ld] + guard)
            f32b.set_guard([tk])
            self.chunks.append(wb)
            self.tok.append(tk)


def load_vec_fm(p, tag, vec_dram, n):
    vb = p.sb(tag, [128, n // 128], F32)
    tok = None
    for c in range(n // 128):
        tok = p.dma("sync", vb[:, c:c + 1],
                    vec_dram[c * 128:(c + 1) * 128].unsqueeze(1),
                    after=[tok] if tok else ())
    return vb, tok



def emit_s0(p, x, xn_tm, xn_fm):
    st = Ring(p, "s0_st", [128, 4], F32, n=3)
    sq = Ring(p, "s0_sq", [128, D], F32, n=2)
    xr = Ring(p, "s0_x", [128, D], F32, n=3)
    cb = Ring(p, "s0_cb", [128, D], BF16, n=2)
    tpb = Ring(p, "s0_tp", [128, D], BF16, n=2)
    for b in range(B):
        for i in range(NT):
            xs = xr.next()
            ld = p.dma("sync", xs[:], x[b, i * 128:(i + 1) * 128, :], after=xr.guard)
            stb = st.next()
            sqb = sq.next()
            p.wait("vector", st.guard + sq.guard)
            tok, _ = ln_tile(p, xs, stb, sqb, after=[ld])
            st.set_guard([tok])
            sq.set_guard([tok])
            o1 = p.dma("gpsimd", xn_tm[b, i * 128:(i + 1) * 128, :], xs[:], after=[tok])
            xnb = cb.next()
            tcast = p.s(lambda s, xnb=xnb, xs=xs: s.activation(
                xnb[:], xs[:], AF.Copy), after=[tok] + cb.guard)
            tmp = tpb.next()
            p.wait("sync", tpb.guard)
            o2 = store_fm(p, xnb, xn_fm[b], i, [tcast], tmp)
            cb.set_guard([o2])
            tpb.set_guard([o2])
            xr.set_guard([o1, tcast])



def finish(p):
    toks = []
    for q in p.dpool:
        toks += [(s, s.n) for s in p.dpool[q] if s.n > 0]
    for e in ENGS:
        p.wait(e, toks)


def run_stage(build_fn, in_maps, core_ids=None, **kw):
    p, names = build_fn()
    finish(p)
    nc = p.finalize()
    res = run_bass_kernel_spmd(nc, in_maps, core_ids=core_ids or list(range(len(in_maps))), **kw)
    return res


HD = 128
NH = 8
SC0 = HD ** -0.5


def _load_xnfm(p, tag, xn_fm_b):
    xsb = p.sb(tag, [128, NC * T], BF16)
    tok = p.dma("sync", xsb[:].rearrange("p (c t) -> p c t", c=NC),
                xn_fm_b.rearrange("c p t -> p c t"))
    return xsb, tok


def _vec_row(p, tag, vec_dram, reps, n):
    row = p.sb(tag, [1, reps * n], F32)
    toks = [p.dma("sync", row[0:1, r * n:(r + 1) * n], vec_dram.unsqueeze(0))
            for r in range(reps)]
    return row, toks


def _bcast_tile(p, tag, row_ap, width, after, ones_f32, psum, scale_row=None):
    out = p.sb(tag, [128, width], F32)
    toks = []
    for s0 in range(0, width, 512):
        s1 = min(s0 + 512, width)
        tm = p.t(lambda t, s0=s0, s1=s1: t.matmul(
            psum[:, s0:s1], ones_f32[:, :], row_ap[:, s0:s1],
            start=True, stop=True), after=after)
        toks.append(p.s(lambda s, s0=s0, s1=s1: s.activation(
            out[:, s0:s1], psum[:, s0:s1], AF.Copy), after=[tm]))
    return out, toks


def emit_l0a_proj(p, aps):
    xn_fm, q_fm, k_fm, w_fm = aps["xn_fm"], aps["q_fm"], aps["k_fm"], aps["w_fm"]
    v_tm, w_tm, beta_tm, negG_hm = aps["v_tm"], aps["w_tm"], aps["beta_tm"], aps["negG_hm"]
    ones1_f32 = p.sb("ones1f", [1, 128], F32)
    onescol_f32 = p.sb("onescolf", [128, 1], F32)
    t_ones = p.v(lambda v: v.memset(ones1_f32[:], 1.0))
    t_ones2 = p.v(lambda v: v.memset(onescol_f32[:], 1.0))
    psA = p.ps("l0p_psA", [128, 2048])
    psB = p.ps("l0p_psB", [128, 2048])

    wring = Ring(p, "l0p_wld", [128, 1024], F32, n=2)
    sring = Ring(p, "l0p_sld", [128, 8], F32, n=2)
    wslots = ([p.sb(f"l0p_wslot{i}", [128, 1024], BF16) for i in range(NC)],
              {"toks": []})
    Wbeta = FMWeight(p, "Wbeta", aps["Wbeta"], D, 8, ring=sring)
    Wg = FMWeight(p, "Wg", aps["Wg"], D, 8, ring=sring)

    qn_row, t1 = _vec_row(p, "qn_row", aps["qn_w"], NH, HD)
    kn_row, t2 = _vec_row(p, "kn_row", aps["kn_w"], NH, HD)
    sc_row = p.sb("sc_row", [1, 128], F32)
    t3 = p.v(lambda v: v.memset(sc_row[:], SC0))
    qn_bc, tq = _bcast_tile(p, "qn_bc", qn_row, D, t1 + [t3], sc_row, psA)
    kn_bc, tk = _bcast_tile(p, "kn_bc", kn_row, D, tq + t2, ones1_f32, psA)
    cw = p.sb("cw", [128, NC * 4], F32)
    t_cw = p.dma("sync", cw[:].rearrange("p (c k) -> p c k", c=NC),
                 aps["conv_w"].rearrange("(c p) k -> p c k", p=128))

    S = [p.sb(f"l0p_S{i}", [128, T], F32) for i in range(3)]
    wpad = p.sb("l0p_wpad", [128, T + 4], F32)
    srow = p.sb("l0p_srow", [1, T], F32)
    wcb = p.sb("l0p_wcb", [128, T], BF16)
    wT = p.sb("l0p_wT", [128, T], BF16)
    st = Ring(p, "l0p_st", [128, 10], F32, n=3)
    qt = Ring(p, "l0p_qt", [128, 1024], F32, n=2)
    sqb = Ring(p, "l0p_sq", [128, 1024], F32, n=2)
    bb = Ring(p, "l0p_bf", [128, 1024], BF16, n=3)
    tpb = Ring(p, "l0p_tp", [128, 1024], BF16, n=2)
    xsb = p.sb("l0p_xn", [128, NC * T], BF16)

    wg_s = []
    prev_xload = []
    for b in range(B):
        t_x = p.dma("sync", xsb[:].rearrange("p (c t) -> p c t", c=NC),
                    xn_fm[b].rearrange("c p t -> p c t"), after=prev_xload)

        def lhs(c, ti, xsb=xsb):
            return xsb[:, c * T + ti * 128:c * T + ti * 128 + 128]

        batch_toks = []

        Wv = FMWeight(p, "Wv", aps["Wv"], D, D, ring=wring, slots=wslots)
        for ti in range(NT):
            for half in range(2):
                for c in range(NC):
                    mm = p.t(lambda t, la=lhs(c, ti), c=c, half=half, Wv=Wv: t.matmul(
                        psA[:, half * 512:(half + 1) * 512], la,
                        Wv.chunks[c][:, half * 512:(half + 1) * 512],
                        start=(c == 0), stop=(c == NC - 1)),
                        after=[t_x] + (Wv.tok if ti == 0 else []),
                        sync=(c == NC - 1))
            mmv = mm
            for c in range(NC):
                mmb = p.t(lambda t, la=lhs(c, ti), c=c: t.matmul(
                    psB[:, 0:8], la, Wbeta.chunks[c][:, 0:8],
                    start=(c == 0), stop=(c == NC - 1)),
                    after=(Wbeta.tok if ti == 0 else []), sync=(c == NC - 1))
            vb = bb.next()
            ev = p.s(lambda s, vb=vb: s.activation(vb[:], psA[:, 0:1024], AF.Copy),
                     after=[mmv] + bb.guard)
            so = p.dma("gpsimd", v_tm[b, ti * 128:(ti + 1) * 128, :], vb[:],
                       after=[ev])
            bb.set_guard([so])
            stb = st.next()
            e1 = p.s(lambda s, stb=stb: s.activation(stb[:, 0:8], psB[:, 0:8],
                                                     AF.Sigmoid),
                     after=[mmb] + st.guard)
            e2 = p.v(lambda v, stb=stb: v.tensor_scalar_mul(stb[:, 0:8],
                                                            stb[:, 0:8], 2.0),
                     after=[e1])
            so2 = p.dma("gpsimd", beta_tm[b, ti * 128:(ti + 1) * 128, :],
                        stb[:, 0:8], after=[e2])
            st.set_guard([so2])
            p.wait("tensor", [ev, e1])
            batch_toks += [so, so2]
            wslots[1]["toks"] = [mmv]

        for (name, Wdr, bcast, bctok, dst_fm) in (
                ("q", aps["Wq"], qn_bc, tq, q_fm), ("k", aps["Wk"], kn_bc, tk, k_fm)):
            W = FMWeight(p, name + "W", Wdr, D, D, ring=wring, slots=wslots)
            for ti in range(NT):
                mms = []
                for half in range(2):
                    for c in range(NC):
                        mm = p.t(lambda t, la=lhs(c, ti), c=c, half=half, W=W: t.matmul(
                            psA[:, half * 512:(half + 1) * 512], la,
                            W.chunks[c][:, half * 512:(half + 1) * 512],
                            start=(c == 0), stop=(c == NC - 1)),
                            after=[t_x] + (W.tok if ti == 0 else []),
                            sync=(c == NC - 1))
                    mms.append(mm)
                qtb = qt.next()
                ev = p.s(lambda s, qtb=qtb: s.activation(qtb[:], psA[:, 0:1024], AF.Copy),
                         after=mms + qt.guard)
                p.wait("tensor", [ev])
                sb_ = sqb.next()
                s1 = p.s(lambda s, qtb=qtb, sb_=sb_: s.activation(
                    sb_[:], qtb[:], AF.Square), after=[ev] + sqb.guard)
                stb = st.next()
                r1 = p.v(lambda v, sb_=sb_, stb=stb: v.reduce_sum(
                    stb[:, 0:8], sb_[:].rearrange("p (h d) -> p h d", h=NH),
                    axis=AX.X), after=[s1] + st.guard)
                sqb.set_guard([r1])
                cc6 = p.constcol(1e-6)
                r2 = p.s(lambda s, stb=stb, cc6=cc6: s.activation(
                    stb[:, 0:8], stb[:, 0:8], AF.Sqrt, scale=1.0 / HD, bias=cc6),
                    after=[r1])
                r3 = p.v(lambda v, stb=stb: v.reciprocal(stb[:, 0:8], stb[:, 0:8]),
                         after=[r2])
                a1 = p.v(lambda v, qtb=qtb, stb=stb: v.tensor_mul(
                    qtb[:].rearrange("p (h d) -> p h d", h=NH),
                    qtb[:].rearrange("p (h d) -> p h d", h=NH),
                    stb[:, 0:8].unsqueeze(2).to_broadcast([128, NH, HD])),
                    after=[r3])
                st.set_guard([a1])
                qb2 = bb.next()
                a2 = p.v(lambda v, qtb=qtb, qb2=qb2, bcast=bcast: v.tensor_mul(
                    qb2[:], qtb[:], bcast[:]), after=[a1] + bctok + bb.guard)
                qt.set_guard([a2])
                tmp = tpb.next()
                p.wait("sync", tpb.guard)
                o2 = store_fm(p, qb2, dst_fm[b], ti, [a2], tmp, nch=NC)
                bb.set_guard([o2])
                tpb.set_guard([o2])
                batch_toks.append(o2)
                wslots[1]["toks"] = mms

        Ww = FMWeight(p, "Ww", aps["Ww"], D, D, ring=wring, slots=wslots)
        z3 = p.v(lambda v: v.memset(wpad[:, 0:3], 0.0), after=wg_s)
        prev_c = list(wg_s)
        for c in range(NC):
            mms = []
            for s0 in range(0, T, 512):
                for dc in range(NC):
                    mm = p.t(lambda t, c=c, s0=s0, dc=dc, xsb=xsb, Ww=Ww: t.matmul(
                        psB[:, s0:s0 + 512],
                        Ww.chunks[dc][:, c * 128:(c + 1) * 128],
                        xsb[:, dc * T + s0:dc * T + s0 + 512],
                        start=(dc == 0), stop=(dc == NC - 1)),
                        after=[t_x] + (Ww.tok if c == 0 and s0 == 0 else []) + prev_c,
                        sync=(dc == NC - 1))
                mms.append(mm)
            ev = p.s(lambda s: s.activation(wpad[:, 3:3 + T], psB[:, 0:T], AF.Copy),
                     after=mms + [z3])
            a0 = p.v(lambda v, c=c: v.tensor_scalar(
                out=S[0][:], in0=wpad[:, 0:T], scalar1=cw[:, c * 4:c * 4 + 1],
                scalar2=None, op0=ALU.mult), after=[ev, t_cw])
            cur, nxt = S[0], S[1]
            tok = a0
            for i in range(1, 4):
                tok = p.v(lambda v, i=i, c=c, cur=cur, nxt=nxt: v.scalar_tensor_tensor(
                    out=nxt[:], in0=wpad[:, i:i + T],
                    scalar=cw[:, c * 4 + i:c * 4 + i + 1], in1=cur[:],
                    op0=ALU.mult, op1=ALU.add), after=[tok])
                cur, nxt = nxt, cur
            sil = p.s(lambda s, cur=cur: s.activation(cur[:], cur[:], AF.Silu),
                      after=[tok])
            sq = p.s(lambda s, cur=cur: s.activation(S[2][:], cur[:], AF.Square),
                     after=[sil])
            rmm = []
            for s0 in range(0, T, 512):
                rm = p.t(lambda t, s0=s0: t.matmul(
                    psA[0:1, s0:s0 + 512], onescol_f32[:], S[2][:, s0:s0 + 512],
                    start=True, stop=True), after=[sq, t_ones2, ev])
                rmm.append(rm)
            cc6b = p.constcol(1e-6)
            sr = p.s(lambda s, cc6b=cc6b: s.activation(
                srow[0:1, 0:T], psA[0:1, 0:T], AF.Sqrt,
                bias=cc6b[0:1, :]), after=rmm)
            rr = p.v(lambda v: v.reciprocal(srow[0:1, 0:T], srow[0:1, 0:T]),
                     after=[sr])
            bmm = []
            for s0 in range(0, T, 512):
                bm = p.t(lambda t, s0=s0: t.matmul(
                    psA[:, s0:s0 + 512], ones1_f32[:], srow[0:1, s0:s0 + 512],
                    start=True, stop=True), after=[rr, t_ones])
                bmm.append(bm)
            wn = p.v(lambda v, cur=cur: v.tensor_mul(wcb[:], cur[:], psA[:, 0:T]),
                     after=bmm + [sil])
            of = p.dma("sync", w_fm[b][c], wcb[:], after=[wn])
            tt = p.dma("sync", wT[:].rearrange("p (i t) -> p i t", i=NT),
                       wcb[:].rearrange("p (i t) -> p i t", i=NT),
                       after=[wn], transpose=True)
            ot = p.dma("sync",
                       w_tm[b][:, c * 128:(c + 1) * 128].rearrange(
                           "(i p) d -> p i d", p=128),
                       wT[:].rearrange("p (i d) -> p i d", i=NT), after=[tt])
            prev_c = [of, ot, wn]
            batch_toks += [of, ot]
            wslots[1]["toks"] = mms

        zb, gb, g1 = S[0], S[1], S[2]
        gones = p.sb("l0p_gones", [8, T], F32)
        t_go = p.v(lambda v, gones=gones: v.memset(gones[:], 1.0))
        mms = []
        for s0 in range(0, T, 512):
            for dc in range(NC):
                mm = p.t(lambda t, s0=s0, dc=dc, xsb=xsb: t.matmul(
                    psB[0:8, s0:s0 + 512], Wg.chunks[dc][:, 0:8],
                    xsb[:, dc * T + s0:dc * T + s0 + 512],
                    start=(dc == 0), stop=(dc == NC - 1)),
                    after=[t_x] + (Wg.tok if s0 == 0 else []) + prev_c,
                    sync=(dc == NC - 1))
            mms.append(mm)
        ge = p.s(lambda s, zb=zb: s.activation(zb[0:8, 0:T], psB[0:8, 0:T], AF.Copy),
                 after=mms + prev_c)
        gtmp = p.sb("l0p_gtmp", [8, T], F32) if b == 0 else gtmp
        ga = p.s(lambda s, gtmp=gtmp, zb=zb: s.activation(
            gtmp[0:8, 0:T], zb[0:8, 0:T], AF.Abs), after=[ge] + wg_s)
        gx = p.s(lambda s, gb=gb, gtmp=gtmp: s.activation(
            gb[0:8, 0:T], gtmp[0:8, 0:T], AF.Exp, scale=-1.0), after=[ga])
        cc1 = p.constcol(1.0)
        gl = p.s(lambda s, gtmp=gtmp, gb=gb, cc1=cc1: s.activation(
            gtmp[0:8, 0:T], gb[0:8, 0:T], AF.Ln,
            bias=cc1[0:8, :]), after=[gx])
        gm = p.v(lambda v, g1=g1, zb=zb: v.tensor_scalar_min(g1[0:8, 0:T],
                                                             zb[0:8, 0:T], 0.0),
                 after=[ge])
        gn = p.v(lambda v, gb=gb, gtmp=gtmp, g1=g1: v.tensor_sub(
            gb[0:8, 0:T], gtmp[0:8, 0:T], g1[0:8, 0:T]), after=[gl, gm])
        gs = p.v(lambda v, g1=g1, gones=gones, gb=gb: v.tensor_tensor_scan(
            out=g1[0:8, 0:T], data0=gones[0:8, 0:T],
            data1=gb[0:8, 0:T], initial=0.0,
            op0=ALU.mult, op1=ALU.add), after=[gn, t_go])
        sg = p.dma("gpsimd", negG_hm[b], g1[0:8, 0:T], after=[gs])
        wg_s = [sg, gs, gn]
        prev_xload = batch_toks + [sg]




DEBUG_A2_BARRIER = False


def emit_l0a_inner(p, aps, pairs=None):
    q_fm, k_fm, w_fm = aps["q_fm"], aps["k_fm"], aps["w_fm"]
    v_tm, w_tm, beta_tm, negG_hm = aps["v_tm"], aps["w_tm"], aps["beta_tm"], aps["negG_hm"]
    o_fm = aps["o_fm"]
    identrep = p.sb("identrep", [128, T], BF16)
    ident2rep = p.sb("ident2rep", [128, T], BF16)
    maskb = p.sb("maskb", [128, 128], F32)
    ones1f = p.sb("a2_ones1f", [1, 128], F32)
    c1 = p.dma("sync", identrep[:].rearrange("p (i t) -> p i t", i=NT),
               aps["c_ident_bf"].unsqueeze(0).to_broadcast([NT, 128, 128]).rearrange(
                   "i p t -> p i t"))
    c2 = p.dma("sync", ident2rep[:].rearrange("p (i t) -> p i t", i=NT),
               aps["c_ident2_bf"].unsqueeze(0).to_broadcast([NT, 128, 128]).rearrange(
                   "i p t -> p i t"))
    c3 = p.dma("sync", maskb[:], aps["c_maskbias_causal"])
    c4 = p.v(lambda v: v.memset(ones1f[:], 1.0))
    consts = [c1, c2, c3, c4]

    psA = p.ps("a2_psA", [128, 2048])
    psB = p.ps("a2_psB", [128, 2048])

    qf = p.sb("a2_qf", [128, T], BF16)
    kf = p.sb("a2_kf", [128, T], BF16)
    wf = p.sb("a2_wf", [128, T], BF16)
    wtm = p.sb("a2_wtm", [128, T], BF16)
    vsb = [p.sb(f"a2_vsb{b}", [128, NT * D], BF16) for b in range(B)]
    betaf = p.sb("a2_betaf", [128, NT * 8], F32)
    negGr = p.sb("a2_negG", [1, T], F32)
    MT = p.sb("a2_MT", [128, T], BF16)
    Xb = p.sb("a2_X", [128, T], BF16)
    Zb = p.sb("a2_Z", [128, T], BF16)
    Rb = p.sb("a2_R", [128, T], BF16)
    Ef = p.sb("a2_Ef", [128, T], F32)
    Eb = p.sb("a2_Eb", [128, T], BF16)
    Tbf = p.sb("a2_Tbf", [128, T], BF16)
    Vsb = p.sb("a2_Vsb", [128, T], BF16)
    Eexp = p.sb("a2_Eexp", [128, T], BF16)
    Fsb = p.sb("a2_F", [128, T], BF16)
    negGm = p.sb("a2_negGm", [128, 128], BF16)
    ocoll = p.sb("a2_ocoll", [128, T], BF16)
    ofmb = p.sb("a2_ofmb", [128, T], BF16)
    stt_ = p.sb("a2_stats", [128, 8], F32)
    mtt = p.sb("a2_mtt", [128, 128], BF16)

    vload = [None, None]
    if pairs is None:
        pairs = [(b, h) for b in range(B) for h in range(NH)]
    prev_done = list(consts)
    prev_ost = None

    for (b, h) in pairs:
        if vload[b] is None:
            vload[b] = p.dma("scalar", vsb[b][:].rearrange("p (i d) -> p i d", i=NT),
                             v_tm[b].rearrange("(i p) d -> p i d", p=128))
        lq = p.dma("sync", qf[:], q_fm[b][h], after=prev_done)
        lk = p.dma("sync", kf[:], k_fm[b][h], after=prev_done)
        lw = p.dma("sync", wf[:], w_fm[b][h], after=prev_done)
        lwt = p.dma("sync", wtm[:].rearrange("p (i d) -> p i d", i=NT),
                    w_tm[b][:, h * 128:(h + 1) * 128].rearrange(
                        "(i p) d -> p i d", p=128), after=prev_done)
        lbe = p.dma("sync", betaf[:].rearrange("p (i e) -> p i e", i=NT),
                    beta_tm[b].rearrange("(i p) e -> p i e", p=128), after=prev_done)
        lg = p.dma("sync", negGr[:], negG_hm[b, h].unsqueeze(0), after=prev_done)

        mt_toks = []
        pw_mms = []
        for I in range(NT):
            sl = slice(I * 128, (I + 1) * 128)
            mm = p.t(lambda t, sl=sl: t.matmul(
                psA[:, sl], wf[:, sl], wf[:, sl], start=True, stop=True),
                after=[lw] + prev_done)
            pw_mms.append(mm)
            st1 = p.v(lambda v, sl=sl, I=I, h=h: v.tensor_scalar(
                out=MT[:, sl], in0=psA[:, sl], scalar1=betaf[:, I * 8 + h:I * 8 + h + 1],
                scalar2=None, op0=ALU.mult), after=[mm, lbe])
            mt_toks.append(st1)
        af = p.g(lambda g: g.affine_select(
            out=MT[:].rearrange("p (i t) -> p i t", i=NT),
            in_=MT[:].rearrange("p (i t) -> p i t", i=NT),
            pattern=[[0, NT], [1, 128]], compare_op=ALU.is_gt, fill=0.0, base=0,
            channel_multiplier=-1), after=mt_toks)
        mtf = p.v(lambda v: v.tensor_add(MT[:], MT[:], identrep[:]), after=[af])

        xi = p.g(lambda g: g.tensor_copy(Xb[:], identrep[:]), after=[af] + prev_done)
        zi = p.g(lambda g: g.tensor_copy(Zb[:], identrep[:]), after=[af] + prev_done)
        tX, tZ = xi, zi
        tR = None
        t_evX = t_evZ = mtf
        for it in range(7):
            t1s = []
            for I in range(NT):
                sl = slice(I * 128, (I + 1) * 128)
                mm = p.t(lambda t, sl=sl: t.matmul(
                    psB[:, sl], MT[:, sl], Xb[:, sl], start=True, stop=True),
                    after=[mtf, tX, t_evZ], sync=(I == NT - 1))
                t1s.append(mm)
            tR = p.v(lambda v: v.tensor_sub(Rb[:], ident2rep[:], psB[:]),
                     after=[t1s[-1], tR] if tR else [t1s[-1]])
            xms = []
            for I in range(NT):
                sl = slice(I * 128, (I + 1) * 128)
                mm = p.t(lambda t, sl=sl: t.matmul(
                    psA[:, sl], Zb[:, sl], Rb[:, sl], start=True, stop=True),
                    after=[tR, t_evX, tZ], sync=(I == NT - 1))
                xms.append(mm)
            zms = []
            for I in range(NT):
                sl = slice(I * 128, (I + 1) * 128)
                mm = p.t(lambda t, sl=sl: t.matmul(
                    psB[:, sl], Rb[:, sl], Zb[:, sl], start=True, stop=True),
                    after=[tR, xms[-1], tZ], sync=(I == NT - 1))
                zms.append(mm)
            tX = p.s(lambda s: s.activation(Xb[:], psA[:], AF.Copy),
                     after=[xms[-1]])
            tZ = p.s(lambda s: s.activation(Zb[:], psB[:], AF.Copy),
                     after=[zms[-1]])
            t_evX, t_evZ = tX, tZ

        e1 = p.s(lambda s: s.activation(Ef[:], kf[:], AF.Copy), after=[lk] + prev_done)
        e2 = p.g(lambda g: g.tensor_copy(Eb[:], kf[:]), after=[lk] + prev_done)
        tEup, tEcast = e1, e2
        tVread = tZ
        tTread = tZ
        tOev = tZ
        tAread = tZ
        tUp = tZ
        tFdone = tZ
        oevs = []

        for I in range(NT):
            span = (I + 1) * 128
            sl = slice(I * 128, (I + 1) * 128)
            tms = []
            for s0 in range(0, span, 512):
                s1 = min(s0 + 512, span)
                mm = p.t(lambda t, s0=s0, s1=s1, sl=sl: t.matmul(
                    psB[:, s0:s1], wf[:, sl], Eb[:, s0:s1], start=True, stop=True),
                    after=[tEcast, tUp, lw], sync=(s1 == span))
                tms.append(mm)
            evT = p.s(lambda s, span=span: s.activation(
                Tbf[:, 0:span], psB[:, 0:span], AF.Copy), after=[tms[-1], tTread])
            mkT = p.g(lambda g, sl=sl: g.affine_select(
                out=Tbf[:, sl], in_=Tbf[:, sl], pattern=[[-1, 128]],
                compare_op=ALU.is_gt, fill=0.0, base=0, channel_multiplier=1),
                after=[evT])
            yms = []
            for s0 in range(0, span, 512):
                s1 = min(s0 + 512, span)
                mm = p.t(lambda t, s0=s0, s1=s1, sl=sl: t.matmul(
                    psB[:, s0:s1], Zb[:, sl], Tbf[:, s0:s1], start=True, stop=True),
                    after=[evT, mkT], sync=(s1 == span))
                yms.append(mm)
            tV = p.v(lambda v, span=span, I=I, h=h: v.tensor_scalar(
                out=Vsb[:, 0:span], in0=psB[:, 0:span],
                scalar1=betaf[:, I * 8 + h:I * 8 + h + 1], scalar2=None,
                op0=ALU.mult), after=[yms[-1], tVread])
            gm = p.t(lambda t, sl=sl: t.matmul(
                psB[:, 0:128], wf[:, sl], qf[:, sl], start=True, stop=True),
                after=[tV, lq])
            gev = p.s(lambda s: s.activation(negGm[:], psB[:, 0:128], AF.Copy,
                                             scale=-1.0), after=[gm])
            gaf = p.g(lambda g: g.affine_select(
                out=negGm[:], in_=negGm[:], pattern=[[1, 128]],
                compare_op=ALU.is_ge, fill=0.0, base=0, channel_multiplier=-1),
                after=[gev])
            ams = []
            for s0 in range(0, span, 512):
                s1 = min(s0 + 512, span)
                m1 = p.t(lambda t, s0=s0, s1=s1, sl=sl: t.matmul(
                    psA[:, s0:s1], qf[:, sl], Eb[:, s0:s1], start=True, stop=False),
                    after=[gaf, tOev, tAread, tEcast], sync=False)
                m2 = p.t(lambda t, s0=s0, s1=s1: t.matmul(
                    psA[:, s0:s1], negGm[:], Vsb[:, s0:s1], start=False, stop=False),
                    sync=False)
                m3 = p.t(lambda t, s0=s0, s1=s1: t.matmul(
                    psA[:, s0:s1], ones1f[:], negGr[0:1, s0:s1], start=False,
                    stop=True), after=[lg], sync=True)
                ams.append(m3)
            dm = p.v(lambda v, sl=sl: v.tensor_add(psA[:, sl], psA[:, sl], maskb[:]),
                     after=[ams[-1]])
            nm = p.v(lambda v, span=span: v.reduce_max(
                stt_[:, 0:1], psA[:, 0:span], axis=AX.X, negate=True), after=[dm])
            ex = p.s(lambda s, span=span: s.activation(
                Eexp[:, 0:span], psA[:, 0:span], AF.Exp, bias=stt_[:, 0:1],
                scale=1.0, accum_out=stt_[:, 1:2]), after=[nm, tFdone])
            tAread = ex
            rc = p.v(lambda v: v.reciprocal(stt_[:, 2:3], stt_[:, 1:2]), after=[ex])
            tF = p.dma("sync",
                       Fsb[:, 0:span].rearrange("t (c p) -> t c p", c=I + 1),
                       Eexp[:, 0:span].rearrange("p (c t) -> p c t", c=I + 1),
                       after=[ex], transpose=True)
            tFdone = tF
            pvs = []
            for c in range(I + 1):
                mm = p.t(lambda t, c=c, b=b, I=I, h=h: t.matmul(
                    psA[:, 0:128], Fsb[:, c * 128:(c + 1) * 128],
                    vsb[b][:, c * D + h * 128:c * D + (h + 1) * 128],
                    start=(c == 0), stop=(c == I)),
                    after=[tF, vload[b], ex], sync=(c == I))
                pvs.append(mm)
            tOev = p.v(lambda v, I=I: v.tensor_scalar(
                out=ocoll[:, I * 128:(I + 1) * 128], in0=psA[:, 0:128],
                scalar1=stt_[:, 2:3], scalar2=None, op0=ALU.mult),
                after=[pvs[-1], rc])
            oevs.append(tOev)
            if I < NT - 1:
                ums = []
                for s0 in range(0, span, 512):
                    s1 = min(s0 + 512, span)
                    mm = p.t(lambda t, s0=s0, s1=s1, sl=sl: t.matmul(
                        psB[:, s0:s1], wtm[:, sl], Vsb[:, s0:s1], start=True,
                        stop=True), after=[gev, lwt, tV], sync=(s1 == span))
                    ums.append(mm)
                tEup = p.v(lambda v, span=span: v.tensor_sub(
                    Ef[:, 0:span], Ef[:, 0:span], psB[:, 0:span]),
                    after=[ums[-1], tEup, tEcast])
                tEcast = p.s(lambda s, span=span: s.activation(
                    Eb[:, 0:span], Ef[:, 0:span], AF.Copy), after=[tEup])
                tUp = tEup
                tTread = tEup
                tVread = ums[-1]
            else:
                tVread = pvs[-1]
                tTread = tV

        oT = p.dma("sync", ofmb[:].rearrange("d (i t) -> d i t", i=NT),
                   ocoll[:].rearrange("t (i d) -> t i d", i=NT),
                   after=oevs + ([prev_ost] if prev_ost else []), transpose=True)
        ost = p.dma("gpsimd", o_fm[b][h], ofmb[:], after=[oT])
        prev_ost = ost
        prev_done = [ost, tOev, tEcast, tF, lbe, lg, lwt]
        if DEBUG_A2_BARRIER:
            for e in ("tensor", "vector", "scalar", "gpsimd", "sync"):
                p.wait(e, prev_done + [tV, ex, nm, rc, gaf, mkT, af, mtf, tZ, tX])
            p.barrier()
    return


def build_l0a_inner(pairs=None):
    p = Prog("l0ai")
    aps = {
        "q_fm": p.dram("q_fm", [B, NH, 128, T], BF16, kind="ExternalInput"),
        "k_fm": p.dram("k_fm", [B, NH, 128, T], BF16, kind="ExternalInput"),
        "w_fm": p.dram("w_fm", [B, NH, 128, T], BF16, kind="ExternalInput"),
        "v_tm": p.dram("v_tm", [B, T, D], BF16, kind="ExternalInput"),
        "w_tm": p.dram("w_tm", [B, T, D], BF16, kind="ExternalInput"),
        "beta_tm": p.dram("beta_tm", [B, T, 8], F32, kind="ExternalInput"),
        "negG_hm": p.dram("negG_hm", [B, NH, T], F32, kind="ExternalInput"),
        "o_fm": p.dram("o_fm", [B, NH, 128, T], BF16, kind="ExternalOutput"),
        "c_ident_bf": const_ap(p, "ident_bf"),
        "c_ident2_bf": const_ap(p, "ident2_bf"),
        "c_maskbias_causal": const_ap(p, "maskbias_causal"),
    }
    emit_l0a_inner(p, aps, pairs=pairs)
    return p


def build_l0a_proj():
    p = Prog("l0ap")
    aps = {
        "xn_fm": p.dram("xn_fm", [B, NC, 128, T], BF16, kind="ExternalInput"),
        "Wq": p.dram("Wq", [D, D], F32, kind="ExternalInput"),
        "Wk": p.dram("Wk", [D, D], F32, kind="ExternalInput"),
        "Wv": p.dram("Wv", [D, D], F32, kind="ExternalInput"),
        "Ww": p.dram("Ww", [D, D], F32, kind="ExternalInput"),
        "Wbeta": p.dram("Wbeta", [D, 8], F32, kind="ExternalInput"),
        "Wg": p.dram("Wg", [D, 8], F32, kind="ExternalInput"),
        "conv_w": p.dram("conv_w", [D, 4], F32, kind="ExternalInput"),
        "qn_w": p.dram("qn_w", [128], F32, kind="ExternalInput"),
        "kn_w": p.dram("kn_w", [128], F32, kind="ExternalInput"),
        "q_fm": p.dram("q_fm", [B, NH, 128, T], BF16, kind="ExternalOutput"),
        "k_fm": p.dram("k_fm", [B, NH, 128, T], BF16, kind="ExternalOutput"),
        "w_fm": p.dram("w_fm", [B, NH, 128, T], BF16, kind="ExternalOutput"),
        "v_tm": p.dram("v_tm", [B, T, D], BF16, kind="ExternalOutput"),
        "w_tm": p.dram("w_tm", [B, T, D], BF16, kind="ExternalOutput"),
        "beta_tm": p.dram("beta_tm", [B, T, 8], F32, kind="ExternalOutput"),
        "negG_hm": p.dram("negG_hm", [B, NH, T], F32, kind="ExternalOutput"),
    }
    emit_l0a_proj(p, aps)
    return p



DH = 64
NHD = 16


def emit_dec(p, aps, tag, S, x_fm, x_tm, out_tm, wq, wk, wv, wo, ln1, ln2,
             wg, wu, wd, ropec, ropes, FFN=4096):
    NTs = S // 128
    psA = p.ps(f"{tag}_psA", [128, 2048])
    psB = p.ps(f"{tag}_psB", [128, 2048])
    ones1f = p.sb(f"{tag}_ones1f", [1, 128], F32)
    t_o1 = p.v(lambda v: v.memset(ones1f[:], 1.0))
    p.init_toks.append(t_o1)
    maskb = p.sb(f"{tag}_maskb", [128, 128], F32)
    cmk = p.dma("sync", maskb[:], aps["c_maskbias_causal"])
    p.init_toks.append(cmk)

    ln1v, tl1 = load_vec_fm(p, f"{tag}_ln1v", ln1, D)
    ln2v, tl2 = load_vec_fm(p, f"{tag}_ln2v", ln2, D)
    p.init_toks += [tl1, tl2]

    wslots = ([p.sb(f"{tag}_ws{i}", [128, 1024], BF16) for i in range(NC)],
              {"toks": []})
    wring = Ring(p, f"{tag}_wld", [128, 1024], F32, n=2)

    xsb = p.sb(f"{tag}_x", [128, NC * S], BF16)
    rstd = p.sb(f"{tag}_rstd", [128, NTs * 2], F32)
    sq = Ring(p, f"{tag}_sq", [128, D], F32, n=1)
    xtf = Ring(p, f"{tag}_xt", [128, D], F32, n=2)
    st = Ring(p, f"{tag}_st", [128, 4], F32, n=2)
    bb = Ring(p, f"{tag}_bb", [128, 1024], BF16, n=2)
    tpb = Ring(p, f"{tag}_tp", [128, 1024], BF16, n=2)
    ctab = Ring(p, f"{tag}_ct", [128, DH], F32, n=2)
    stab = Ring(p, f"{tag}_st2", [128, DH], F32, n=2)
    ropebuf = Ring(p, f"{tag}_rp", [128, 1024], F32, n=4)

    q_fm = p.dram(f"{tag}_qfm", [B, NHD // 2, 128, S], BF16)
    k_fm = p.dram(f"{tag}_kfm", [B, NHD // 2, 128, S], BF16)
    v_tm = p.dram(f"{tag}_vtm", [B, S, D], BF16)
    o_fm = p.dram(f"{tag}_ofm", [B, NC, 128, S], BF16)
    xm_tm = p.dram(f"{tag}_xm", [B, S, D], F32)
    xm_fm = p.dram(f"{tag}_xmfm", [B, NC, 128, S], BF16)

    def rstd_tile(xti, sti, sqi, i, col, after):
        s1 = p.s(lambda s, xti=xti, sqi=sqi: s.activation(sqi[:], xti[:], AF.Square),
                 after=after)
        s2 = p.v(lambda v, sqi=sqi, sti=sti: v.reduce_sum(sti[:, 0:1], sqi[:],
                                                          axis=AX.X), after=[s1])
        cc = p.constcol(1e-6)
        s3 = p.s(lambda s, sti=sti, cc=cc: s.activation(
            sti[:, 1:2], sti[:, 0:1], AF.Sqrt, scale=1.0 / D, bias=cc), after=[s2])
        s4 = p.v(lambda v, sti=sti, col=col: v.reciprocal(rstd[:, col:col + 1],
                                                          sti[:, 1:2]), after=[s3])
        return s4

    qf = p.sb(f"{tag}_qf", [128, (NHD // 2) * S], BF16)
    kf = p.sb(f"{tag}_kf", [128, (NHD // 2) * S], BF16)
    Eexp = p.sb(f"{tag}_Ee", [128, 2048], BF16)
    Fsb = p.sb(f"{tag}_F", [128, 2048], BF16)
    opair = p.sb(f"{tag}_op", [128, 128], BF16)
    oT = p.sb(f"{tag}_oT", [128, 128], BF16)
    stt_ = p.sb(f"{tag}_stt", [128, 8], F32)
    GRP = min(4, S // 128)
    out_acc = p.sb(f"{tag}_oacc", [128, GRP * D], F32)
    gsl = [p.sb(f"{tag}_gs_{c}", [128, 512], BF16) for c in range(NC)]
    usl = [p.sb(f"{tag}_us_{c}", [128, 512], BF16) for c in range(NC)]
    dsl = [wslots[0][c][:, 0:1024] for c in range(4)]
    wldr = wring
    gact = Ring(p, f"{tag}_ga", [128, 512], BF16, n=2)
    ffb = Ring(p, f"{tag}_ff", [128, 512], BF16, n=2)
    ffT = Ring(p, f"{tag}_ffT", [128, 512], BF16, n=2)
    bat_guard = []
    for b in range(B):
        t_x = p.dma("sync", xsb[:].rearrange("p (c t) -> p c t", c=NC),
                    x_fm[b].rearrange("c p t -> p c t"), after=bat_guard)

        def lhs(c, ti, xsb=xsb):
            return xsb[:, c * S + ti * 128:c * S + ti * 128 + 128]

        stores = {}
        rtoks = []
        for i in range(NTs):
            xti = xtf.next()
            lx = p.dma("scalar", xti[:], x_tm[b, i * 128:(i + 1) * 128, :],
                       after=xtf.guard)
            sti = st.next()
            sqi = sq.next()
            r = rstd_tile(xti, sti, sqi, i, 2 * i, [lx] + st.guard + sq.guard)
            st.set_guard([r])
            sq.set_guard([r])
            xtf.set_guard([r])
            rtoks.append(r)

        for (nm, wdr, dst, gscale) in (("q", wq, q_fm, 0.125), ("k", wk, k_fm, 1.0),
                                       ("v", wv, v_tm, 1.0)):
            W = FMWeight(p, f"{tag}_{nm}w{b}", wdr, D, D, gamma=None, ring=wring,
                         slots=wslots)
            gtoks = []
            for c in range(NC):
                gt = p.v(lambda v, c=c, W=W, gscale=gscale: v.tensor_scalar(
                    out=W.chunks[c], in0=W.chunks[c], scalar1=ln1v[:, c:c + 1],
                    scalar2=gscale, op0=ALU.mult, op1=ALU.mult),
                    after=[W.tok[c]] + p.init_toks)
                gtoks.append(gt)
            mms_all = []
            stores[nm] = []
            for i in range(NTs):
                mms = []
                for half in range(2):
                    for c in range(NC):
                        mm = p.t(lambda t, la=lhs(c, i), c=c, half=half, W=W: t.matmul(
                            psA[:, half * 512:(half + 1) * 512], la,
                            W.chunks[c][:, half * 512:(half + 1) * 512],
                            start=(c == 0), stop=(c == NC - 1)),
                            after=[t_x] + gtoks, sync=(c == NC - 1))
                    mms.append(mm)
                mms_all.append(mms[-1])
                qtb = ropebuf.next()
                ev = p.v(lambda v, qtb=qtb, i=i: v.tensor_scalar(
                    out=qtb[:], in0=psA[:, 0:1024], scalar1=rstd[:, 2 * i:2 * i + 1],
                    scalar2=None, op0=ALU.mult),
                    after=mms + [rtoks[i]] + ropebuf.guard)
                p.wait("tensor", [ev])
                ob = bb.next()
                if ropec is not None and nm != "v":
                    ct = ctab.next()
                    stb2 = stab.next()
                    tab = ropec[b] if ropec.ndim == 3 else ropec
                    tabs = ropes[b] if ropes.ndim == 3 else ropes
                    lc = p.dma("scalar", ct[:], tab[i * 128:(i + 1) * 128, :],
                               after=ctab.guard)
                    ls = p.dma("scalar", stb2[:], tabs[i * 128:(i + 1) * 128, :],
                               after=stab.guard)
                    rb2 = ropebuf.next()
                    vi = qtb[:].rearrange("p (h two i) -> p h two i", two=2, i=32)
                    vo = rb2[:].rearrange("p (h two i) -> p h two i", two=2, i=32)
                    svi = stb2[:].rearrange("p (two i) -> p two i", two=2)
                    c1 = p.v(lambda v, vo=vo, vi=vi, svi=svi: v.tensor_mul(
                        vo[:, :, 0, :], vi[:, :, 1, :],
                        svi[:, 0, :].unsqueeze(1).to_broadcast([128, NHD, 32])),
                        after=[ev, ls] + ropebuf.guard)
                    c2 = p.v(lambda v, vo=vo, vi=vi, svi=svi: v.tensor_mul(
                        vo[:, :, 1, :], vi[:, :, 0, :],
                        svi[:, 1, :].unsqueeze(1).to_broadcast([128, NHD, 32])),
                        after=[ev, ls])
                    c3 = p.v(lambda v, qtb=qtb, ct=ct: v.tensor_mul(
                        qtb[:].rearrange("p (h e) -> p h e", e=DH),
                        qtb[:].rearrange("p (h e) -> p h e", e=DH),
                        ct[:].unsqueeze(1).to_broadcast([128, NHD, DH])),
                        after=[ev, lc, c1, c2])
                    c4 = p.v(lambda v, ob=ob, qtb=qtb, rb2=rb2: v.tensor_add(
                        ob[:], qtb[:], rb2[:]), after=[c1, c2, c3] + bb.guard)
                    ropebuf.set_guard([c4])
                    ctab.set_guard([c4])
                    stab.set_guard([c4])
                    fin = c4
                else:
                    fin = p.s(lambda s, ob=ob, qtb=qtb: s.activation(
                        ob[:], qtb[:], AF.Copy), after=[ev] + bb.guard)
                    ropebuf.set_guard([fin])
                if nm == "v":
                    so = p.dma("gpsimd", v_tm[b, i * 128:(i + 1) * 128, :], ob[:],
                               after=[fin])
                else:
                    tmp = tpb.next()
                    p.wait("sync", tpb.guard)
                    so = store_fm(p, ob, dst[b], i, [fin], tmp, nch=NC)
                    tpb.set_guard([so])
                bb.set_guard([so])
                stores[nm].append(so)
            wslots[1]["toks"] = mms_all

        vf = xsb
        lq = p.dma("sync", qf[:].rearrange("p (h t) -> p h t", h=NHD // 2),
                   q_fm[b].rearrange("h p t -> p h t"),
                   after=stores["q"] + bat_guard)
        lk = p.dma("sync", kf[:].rearrange("p (h t) -> p h t", h=NHD // 2),
                   k_fm[b].rearrange("h p t -> p h t"),
                   after=stores["k"] + bat_guard)
        lv = p.dma("scalar", vf[:].rearrange("p (i d) -> p i d", i=NTs),
                   v_tm[b].rearrange("(i p) d -> p i d", p=128),
                   after=[mms_all[-1]] + stores["v"] + bat_guard)
        tEr = tFr = tOr = lq
        prevT = [lq]
        prevTT = lq
        ostores = []
        for i in range(NTs):
            span = (i + 1) * 128
            for hp in range(NHD // 2):
                otoks = []
                for hh in range(2):
                    qsl = qf[hh * 64:(hh + 1) * 64,
                              hp * S + i * 128:hp * S + i * 128 + 128]
                    mms = []
                    for s0 in range(0, span, 512):
                        s1 = min(s0 + 512, span)
                        mm = p.t(lambda t, qsl=qsl, s0=s0, s1=s1, hp=hp, hh=hh: t.matmul(
                            psA[:, s0:s1], qsl,
                            kf[hh * 64:(hh + 1) * 64, hp * S + s0:hp * S + s1],
                            start=True, stop=True),
                            after=[lq, lk, tEr, tOr], sync=(s1 == span))
                        mms.append(mm)
                    dm = p.v(lambda v, i=i: v.tensor_add(
                        psA[:, i * 128:(i + 1) * 128],
                        psA[:, i * 128:(i + 1) * 128], maskb[:]), after=mms)
                    nmx = p.v(lambda v, span=span: v.reduce_max(
                        stt_[:, 0:1], psA[:, 0:span], axis=AX.X, negate=True),
                        after=[dm])
                    ex = p.s(lambda s, span=span: s.activation(
                        Eexp[:, 0:span], psA[:, 0:span], AF.Exp, bias=stt_[:, 0:1],
                        scale=1.0, accum_out=stt_[:, 1:2]), after=[nmx, tFr])
                    tEr = ex
                    rc = p.v(lambda v: v.reciprocal(stt_[:, 2:3], stt_[:, 1:2]),
                             after=[ex])
                    tF = p.dma("sync",
                               Fsb[:, 0:span].rearrange("t (c p) -> t c p", c=i + 1),
                               Eexp[:, 0:span].rearrange("p (c t) -> p c t", c=i + 1),
                               after=[ex], transpose=True)
                    tFr = tF
                    pvs = []
                    for c in range(i + 1):
                        mm = p.t(lambda t, c=c, hp=hp, hh=hh, i=i: t.matmul(
                            psB[:, 0:64], Fsb[:, c * 128:(c + 1) * 128],
                            vf[:, c * D + (hp * 2 + hh) * 64:
                               c * D + (hp * 2 + hh) * 64 + 64],
                            start=(c == 0), stop=(c == i)),
                            after=[tF, lv], sync=(c == i))
                        pvs.append(mm)
                    oe = p.v(lambda v, hh=hh: v.tensor_scalar(
                        out=opair[:, hh * 64:(hh + 1) * 64], in0=psB[:, 0:64],
                        scalar1=stt_[:, 2:3], scalar2=None, op0=ALU.mult),
                        after=[pvs[-1], rc] + ([tOr, prevTT] if hh == 0 else []))
                    tOr = oe
                    otoks.append(oe)
                tT = p.dma("sync", oT[:], opair[:], after=otoks + prevT,
                           transpose=True)
                prevTT = tT
                so = p.dma("gpsimd", o_fm[b][hp][:, i * 128:(i + 1) * 128], oT[:],
                           after=[tT])
                prevT = [so]
                ostores.append(so)

        rtoks2 = []
        xmstores = []
        xmtstores = []
        Wo = FMWeight(p, f"{tag}_wo{b}", wo, D, D, ring=wring, slots=wslots)
        osb = qf
        lo = p.dma("sync", osb[:, 0:NC * S].rearrange("p (c t) -> p c t", c=NC),
                   o_fm[b].rearrange("c p t -> p c t"),
                   after=ostores + [tOr, tEr, tFr])
        wom = []
        for i in range(NTs):
            mms = []
            for half in range(2):
                for c in range(NC):
                    mm = p.t(lambda t, c=c, i=i, half=half, Wo=Wo, osb=osb: t.matmul(
                        psA[:, half * 512:(half + 1) * 512],
                        osb[:, c * S + i * 128:c * S + i * 128 + 128],
                        Wo.chunks[c][:, half * 512:(half + 1) * 512],
                        start=(c == 0), stop=(c == NC - 1)),
                        after=[lo] + Wo.tok + [tOr, tEr], sync=(c == NC - 1))
                mms.append(mm)
            wom.append(mms[-1])
            xti = xtf.next()
            lx = p.dma("scalar", xti[:], x_tm[b, i * 128:(i + 1) * 128, :],
                       after=xtf.guard)
            ad = p.v(lambda v, xti=xti: v.tensor_add(xti[:], xti[:], psA[:, 0:1024]),
                     after=mms + [lx])
            p.wait("tensor", [ad])
            sx = p.dma("gpsimd", xm_tm[b, i * 128:(i + 1) * 128, :], xti[:],
                       after=[ad])
            sti = st.next()
            sqi = sq.next()
            r2 = rstd_tile(xti, sti, sqi, i, 2 * i + 1, [ad] + st.guard + sq.guard)
            rtoks2.append(r2)
            st.set_guard([r2])
            sq.set_guard([r2])
            ob = bb.next()
            cb = p.s(lambda s, ob=ob, xti=xti: s.activation(ob[:], xti[:], AF.Copy),
                     after=[ad] + bb.guard)
            xtf.set_guard([sx, r2, cb])
            tmp = tpb.next()
            p.wait("sync", tpb.guard)
            o2 = store_fm(p, ob, xm_fm[b], i, [cb], tmp, nch=NC)
            bb.set_guard([o2])
            tpb.set_guard([o2])
            xmstores.append(o2)
            xmtstores.append(sx)
        wslots[1]["toks"] = wom

        xmsb = kf
        lxm = p.dma("sync", xmsb[:, 0:NC * S].rearrange("p (c t) -> p c t", c=NC),
                    xm_fm[b].rearrange("c p t -> p c t"),
                    after=xmstores + [wom[-1], tOr])
        fin_toks = []
        for g0 in range(0, NTs, GRP):
          zacc = p.v(lambda v: v.memset(out_acc[:], 0.0), after=fin_toks)
          guard_w = []
          atoks = [zacc]
          for j in range(FFN // 512):
              wg_t, wu_t, wd_t = [], [], []
              for c in range(NC):
                  dstb = wldr.next()
                  l1 = p.dma("scalar", dstb[:, 0:512],
                             wg[c * 128:(c + 1) * 128, j * 512:(j + 1) * 512],
                             after=wldr.guard + guard_w)
                  g1 = p.v(lambda v, c=c, dstb=dstb, gsl=gsl: v.tensor_scalar(
                      out=gsl[c][:], in0=dstb[:, 0:512], scalar1=ln2v[:, c:c + 1],
                      scalar2=None, op0=ALU.mult), after=[l1])
                  wldr.set_guard([g1])
                  dstb = wldr.next()
                  l2 = p.dma("scalar", dstb[:, 0:512],
                             wu[c * 128:(c + 1) * 128, j * 512:(j + 1) * 512],
                             after=wldr.guard + guard_w)
                  g2 = p.v(lambda v, c=c, dstb=dstb, usl=usl: v.tensor_scalar(
                      out=usl[c][:], in0=dstb[:, 0:512], scalar1=ln2v[:, c:c + 1],
                      scalar2=None, op0=ALU.mult), after=[l2])
                  wldr.set_guard([g2])
                  wg_t.append(g1)
                  wu_t.append(g2)
              for c in range(4):
                  dstb = wldr.next()
                  l3 = p.dma("scalar", dstb[:],
                             wd[j * 512 + c * 128:j * 512 + (c + 1) * 128, :],
                             after=wldr.guard + guard_w)
                  g3 = p.v(lambda v, c=c, dstb=dstb, dsl=dsl: v.tensor_copy(
                      dsl[c], dstb[:]), after=[l3] + wslots[1]["toks"])
                  wldr.set_guard([g3])
                  wd_t.append(g3)
              guard_w = []
              for i in range(g0, g0 + GRP):
                  gm = []
                  for c in range(NC):
                      mm = p.t(lambda t, c=c, i=i, xmsb=xmsb, gsl=gsl: t.matmul(
                          psB[:, 0:512], xmsb[:, c * S + i * 128:c * S + i * 128 + 128],
                          gsl[c][:], start=(c == 0), stop=(c == NC - 1)),
                          after=[lxm] + wg_t + atoks, sync=(c == NC - 1))
                  gm = mm
                  for c in range(NC):
                      mm = p.t(lambda t, c=c, i=i, xmsb=xmsb, usl=usl: t.matmul(
                          psB[:, 512:1024], xmsb[:, c * S + i * 128:c * S + i * 128 + 128],
                          usl[c][:], start=(c == 0), stop=(c == NC - 1)),
                          after=wu_t, sync=(c == NC - 1))
                  um = mm
                  ga = gact.next()
                  sil = p.s(lambda s, ga=ga, i=i: s.activation(
                      ga[:], psB[:, 0:512], AF.Silu, scale=rstd[:, 2 * i + 1:2 * i + 2]),
                      after=[gm, rtoks2[i]] + gact.guard)
                  ff = ffb.next()
                  st2 = p.v(lambda v, ff=ff, ga=ga, i=i: v.scalar_tensor_tensor(
                      out=ff[:], in0=psB[:, 512:1024],
                      scalar=rstd[:, 2 * i + 1:2 * i + 2], in1=ga[:],
                      op0=ALU.mult, op1=ALU.mult), after=[um, sil] + ffb.guard)
                  gact.set_guard([st2])
                  p.wait("tensor", [st2])
                  fT = ffT.next()
                  p.wait("sync", ffT.guard)
                  tt = p.dma("sync", fT[:].rearrange("t (c p) -> t c p", c=4),
                             ff[:].rearrange("p (c t) -> p c t", c=4),
                             after=[st2], transpose=True)
                  ffb.set_guard([tt])
                  wdm = []
                  for c in range(4):
                      for half in range(2):
                          mm = p.t(lambda t, c=c, half=half, fT=fT, dsl=dsl: t.matmul(
                              psA[:, half * 512:(half + 1) * 512],
                              fT[:, c * 128:(c + 1) * 128],
                              dsl[c][:, half * 512:(half + 1) * 512],
                              start=(c == 0), stop=(c == 3)),
                              after=[tt] + wd_t, sync=(c == 3 and half == 1))
                          wdm.append(mm)
                  ac = p.v(lambda v, i=i, g0=g0: v.tensor_add(
                      out_acc[:, (i - g0) * D:(i - g0 + 1) * D],
                      out_acc[:, (i - g0) * D:(i - g0 + 1) * D],
                      psA[:, 0:1024]), after=[wdm[-1], zacc])
                  ffT.set_guard([ac])
                  atoks = [ac]
                  guard_w.append(ac)
                  p.wait("tensor", [ac])
          for i in range(g0, g0 + GRP):
            xti = xtf.next()
            lx = p.dma("scalar", xti[:], xm_tm[b, i * 128:(i + 1) * 128, :],
                       after=xtf.guard + xmtstores)
            ad = p.v(lambda v, xti=xti, i=i, g0=g0: v.tensor_add(
                xti[:], xti[:], out_acc[:, (i - g0) * D:(i - g0 + 1) * D]),
                after=[lx] + atoks)
            so = p.dma("gpsimd", out_tm[b, i * 128:(i + 1) * 128, :], xti[:],
                       after=[ad])
            xtf.set_guard([so])
            fin_toks.append(so)
            fin_toks = fin_toks + atoks
        bat_guard = fin_toks + [tOr, tEr]
    return


def build_dec(S, with_rope=True):
    p = Prog("dec")
    NTs = S // 128
    aps = {
        "x_fm": p.dram("x_fm", [B, NC, 128, S], BF16, kind="ExternalInput"),
        "x_tm": p.dram("x_tm", [B, S, D], F32, kind="ExternalInput"),
        "wq": p.dram("wq", [D, D], F32, kind="ExternalInput"),
        "wk": p.dram("wk", [D, D], F32, kind="ExternalInput"),
        "wv": p.dram("wv", [D, D], F32, kind="ExternalInput"),
        "wo": p.dram("wo", [D, D], F32, kind="ExternalInput"),
        "ln1": p.dram("ln1", [D], F32, kind="ExternalInput"),
        "ln2": p.dram("ln2", [D], F32, kind="ExternalInput"),
        "wg": p.dram("wg", [D, 4096], F32, kind="ExternalInput"),
        "wu": p.dram("wu", [D, 4096], F32, kind="ExternalInput"),
        "wd": p.dram("wd", [4096, D], F32, kind="ExternalInput"),
        "ropec": p.dram("ropec", [B, S, DH], F32, kind="ExternalInput"),
        "ropes": p.dram("ropes", [B, S, DH], F32, kind="ExternalInput"),
        "out_tm": p.dram("out_tm", [B, S, D], F32, kind="ExternalOutput"),
        "c_maskbias_causal": const_ap(p, "maskbias_causal"),
    }
    emit_dec(p, aps, "dc", S, aps["x_fm"], aps["x_tm"], aps["out_tm"],
             aps["wq"], aps["wk"], aps["wv"], aps["wo"], aps["ln1"], aps["ln2"],
             aps["wg"], aps["wu"], aps["wd"], aps["ropec"], aps["ropes"])
    return p



def emit_router1(p, aps):
    xn_tm, xn_fm, proc_tm = aps["xn_tm"], aps["xn_fm"], aps["proc_tm"]
    dd, partial = aps["dd"], aps["partial"]
    psA = p.ps("r1_psA", [128, 2048])
    gv, tgv = load_vec_fm(p, "r1_pnw", aps["prior_norm_w"], D)
    onescol = p.sb("r1_onec", [128, 1], F32)
    t_oc = p.v(lambda v: v.memset(onescol[:], 1.0))
    p.init_toks += [tgv, t_oc]
    wslots = ([p.sb(f"r1_ws{i}", [128, 2048], BF16) for i in range(NC)],
              {"toks": []})
    wring = Ring(p, "r1_wld", [128, 2048], F32, n=2)
    W = FMWeight(p, "r1_pw", aps["prior_w"], D, 2 * D, gamma=gv, ring=wring,
                 slots=wslots)
    xsb = p.sb("r1_xfm", [128, NC * T], BF16)
    xr = Ring(p, "r1_x", [128, D], F32, n=2)
    pr = Ring(p, "r1_p", [128, D], F32, n=2)
    s1b = Ring(p, "r1_s1", [128, D], F32, n=2)
    s2b = Ring(p, "r1_s2", [128, D], F32, n=2)
    st = Ring(p, "r1_st", [128, 8], F32, n=3)
    acc = p.sb("r1_acc", [128, 2], F32)
    psb = p.sb("r1_psb", [1, 2], F32)
    accp = p.ps("r1_accp", [128, 8])
    zacc = p.v(lambda v: v.memset(acc[:], 0.0))
    atok = zacc
    xload = []
    for b in range(B):
        t_x = p.dma("sync", xsb[:].rearrange("p (c t) -> p c t", c=NC),
                    xn_fm[b].rearrange("c p t -> p c t"), after=xload)
        for i in range(NT):
            xt = xr.next()
            lx = p.dma("sync", xt[:], xn_tm[b, i * 128:(i + 1) * 128, :],
                       after=xr.guard)
            pt = pr.next()
            lp = p.dma("scalar", pt[:], proc_tm[b, i * 128:(i + 1) * 128, :],
                       after=pr.guard)
            stb = st.next()
            s1 = s1b.next()
            s2 = s2b.next()
            p.wait("vector", st.guard)
            t1 = p.v(lambda v, pt=pt, xt=xt, s1=s1: v.tensor_sub(s1[:], pt[:], xt[:]),
                     after=[lx, lp] + s1b.guard)
            t2 = p.s(lambda s, s1=s1, s2=s2: s.activation(s2[:], s1[:], AF.Square),
                     after=[t1] + s2b.guard)
            t3 = p.v(lambda v, s2=s2, stb=stb: v.reduce_sum(stb[:, 0:1], s2[:],
                                                            axis=AX.X), after=[t2])
            pr.set_guard([t1])
            t4 = p.s(lambda s, xt=xt, s2=s2: s.activation(s2[:], xt[:], AF.Square),
                     after=[t3, lx])
            t5 = p.v(lambda v, s2=s2, stb=stb: v.reduce_sum(stb[:, 1:2], s2[:],
                                                            axis=AX.X), after=[t4])
            s2b.set_guard([t5])
            cc = p.constcol(1e-6)
            t6 = p.s(lambda s, stb=stb, cc=cc: s.activation(
                stb[:, 2:3], stb[:, 1:2], AF.Sqrt, scale=1.0 / D, bias=cc),
                after=[t5])
            t7 = p.v(lambda v, stb=stb: v.reciprocal(stb[:, 2:3], stb[:, 2:3]),
                     after=[t6])
            mms = []
            for half in range(4):
                for c in range(NC):
                    mm = p.t(lambda t, half=half, c=c, i=i, W=W, xsb=xsb: t.matmul(
                        psA[:, half * 512:(half + 1) * 512],
                        xsb[:, c * T + i * 128:c * T + i * 128 + 128],
                        W.chunks[c][:, half * 512:(half + 1) * 512],
                        start=(c == 0), stop=(c == NC - 1)),
                        after=[t_x] + (W.tok if b == 0 and i == 0 else []) + [atok],
                        sync=(c == NC - 1))
                mms.append(mm)
            rm = p.v(lambda v, s1=s1, s2=s2, stb=stb: v.scalar_tensor_tensor(
                out=s2[:], in0=psA[:, 0:1024], scalar=stb[:, 2:3], in1=s1[:],
                op0=ALU.mult, op1=ALU.subtract_rev if hasattr(ALU, "subtract_rev")
                else ALU.subtract), after=mms[0:2] + [t7, t3])
            sq = p.s(lambda s, s2=s2: s.activation(s2[:], s2[:], AF.Square),
                     after=[rm])
            lvt = p.v(lambda v, s1=s1, stb=stb: v.tensor_scalar(
                out=s1[:], in0=psA[:, 1024:2048], scalar1=stb[:, 2:3], scalar2=None,
                op0=ALU.mult), after=mms[2:4] + [t7, rm])
            s1b.set_guard([lvt])
            ex = p.s(lambda s, s1=s1, s2=s2, stb=stb: s.activation(
                s2[:], s2[:], AF.Copy), after=[sq]) if False else None
            eb = p.v(lambda v, s1=s1: None, after=[]) if False else None
            te = p.s(lambda s, xt=xt, s1=s1: s.activation(xt[:], s1[:], AF.Exp,
                                                          scale=-1.0),
                     after=[lvt, t4])
            u1 = p.v(lambda v, s2=s2, xt=xt: v.tensor_scalar(
                out=s2[:], in0=s2[:], scalar1=1.0, scalar2=None, op0=ALU.add),
                after=[sq, te])
            u2 = p.v(lambda v, s2=s2, xt=xt: v.tensor_mul(s2[:], s2[:], xt[:]),
                     after=[u1])
            xr.set_guard([te, u2])
            u3 = p.v(lambda v, s2=s2, s1=s1: v.tensor_add(s2[:], s2[:], s1[:]),
                     after=[u2, lvt])
            u4 = p.v(lambda v, s2=s2, stb=stb: v.reduce_sum(stb[:, 3:4], s2[:],
                                                            axis=AX.X), after=[u3])
            d1 = p.v(lambda v, stb=stb: v.tensor_scalar_mul(
                stb[:, 4:5], stb[:, 0:1], 1.0 / D), after=[u4])
            d2 = p.v(lambda v, stb=stb: v.tensor_scalar(
                out=stb[:, 5:6], in0=stb[:, 3:4], scalar1=0.5 / D, scalar2=-0.5,
                op0=ALU.mult, op1=ALU.add), after=[d1])
            sd = p.dma("gpsimd", dd[b, 0, i, :].unsqueeze(1), stb[:, 4:5],
                       after=[d2])
            sd2 = p.dma("gpsimd", dd[b, 1, i, :].unsqueeze(1), stb[:, 5:6],
                        after=[d2])
            a1 = p.v(lambda v, stb=stb: v.tensor_add(acc[:], acc[:],
                                                     stb[:, 4:6]), after=[d2, atok])
            atok = a1
            st.set_guard([sd, sd2, a1])
        xload = [mms[-1]]
    pm = p.t(lambda t: t.matmul(accp[0:1, 0:2], onescol[:], acc[:, 0:2],
                                start=True, stop=True), after=[atok, t_oc])
    pe = p.s(lambda s: s.activation(partial_sb[0:1, 0:2], accp[0:1, 0:2], AF.Copy),
             after=[pm]) if False else None
    pev = p.s(lambda s: s.activation(psb[0:1, 0:2], accp[0:1, 0:2], AF.Copy),
              after=[pm])
    p.dma("gpsimd", partial, psb[0:1, 0:2], after=[pev])
    return


def _hook_r1_psb(p):
    return p.sb("r1_psb", [1, 2], F32)



def emit_router2(p, aps):
    dd, allpart, proc_tm = aps["dd"], aps["allpart"], aps["proc_tm"]
    ST_d, S_d, scores_d = aps["ST"], aps["Smat"], aps["scores_sm"]
    ropec2, ropes2 = aps["ropec2"], aps["ropes2"]
    sel_tm, sel_fm = aps["sel_tm"], aps["sel_fm"]
    phi_d, plo_d = aps["phi"], aps["plo"]
    grow_d = p.dram("r2_grow", [B, 3, T], F32)
    psA = p.ps("r2_psA", [128, 2048])
    psB = p.ps("r2_psB", [128, 1024])
    identf = p.sb("r2_idf", [128, 128], F32)
    ld0 = p.dma("sync", identf[:], aps["c_ident_f32"])
    ones1f = p.sb("r2_ones1", [1, 128], F32)
    t1_ = p.v(lambda v: v.memset(ones1f[:], 1.0))
    ibc = p.sb("r2_ibc", [128, 512], F32)
    iota_r = p.sb("r2_iotar", [1, 512], F32)
    ld1 = p.dma("sync", iota_r[:], aps["c_iota512_row"])
    iota_c = p.sb("r2_iotac", [128, 1], F32)
    ld2 = p.dma("sync", iota_c[:], aps["c_iota128_col"])
    ctab = p.sb("r2_ctab", [128, NT * DH], BF16)
    stab = p.sb("r2_stab", [128, NT * DH], BF16)
    ld3 = p.dma("sync", ctab[:].rearrange("p (i e) -> p i e", i=NT),
                aps["c_ropeb_cos_bf"].rearrange("(i p) e -> p i e", p=128))
    ld4 = p.dma("sync", stab[:].rearrange("p (i e) -> p i e", i=NT),
                aps["c_ropeb_sin_bf"].rearrange("(i p) e -> p i e", p=128))
    p.init_toks += [ld0, t1_, ld1, ld2, ld3, ld4]
    mm = p.t(lambda t: t.matmul(psB[:, 0:512], ones1f[:], iota_r[:],
                                start=True, stop=True))
    tib = p.s(lambda s: s.activation(ibc[:], psB[:, 0:512], AF.Copy), after=[mm])
    p.init_toks.append(tib)
    if allpart.shape[0] == 8:
        ap8 = p.sb("r2_ap8", [8, 2], F32)
        lda = p.dma("sync", ap8[:], allpart)
        onc8 = p.sb("r2_onc8", [8, 1], F32)
        t8_ = p.v(lambda v: v.memset(onc8[:], 1.0))
        mm2 = p.t(lambda t: t.matmul(psB[0:1, 0:2], onc8[0:8, 0:1],
                                     ap8[0:8, 0:2], start=True, stop=True),
                  after=[lda, t8_, tib])
        nm1 = p.s(lambda s: s.activation(negm[0:1, 0:2], psB[0:1, 0:2], AF.Copy,
                                         scale=-1.0 / (16 * T)), after=[mm2])
    else:
        ap1 = p.sb("r2_ap1", [1, 2], F32)
        lda = p.dma("sync", ap1[:], allpart, after=aps.get("allpart_tok", ()))
        nm1 = p.v(lambda v: v.tensor_scalar_mul(
            negm[0:1, 0:2], ap1[0:1, 0:2], -1.0 / (16 * T)), after=[lda, tib])
    nm2 = p.v(lambda v: v.reduce_sum(negm[0:1, 2:3], negm[0:1, 0:2], axis=AX.X),
              after=[nm1])
    mm3 = p.t(lambda t: t.matmul(psB[0:16, 2:3], ones1f[0:1, 0:16],
                                 negm[0:1, 2:3], start=True, stop=True),
              after=[nm2])
    nb = p.s(lambda s: s.activation(bias16[0:16, 0:1], psB[0:16, 2:3], AF.Copy),
             after=[mm3])

    negm = p.sb("r2_negm", [1, 4], F32)
    bias16 = p.sb("r2_b16", [16, 1], F32)
    grown = p.sb("r2_grown", [1, T], F32)
    scr2b = p.sb("r2_scr2b", [128, 2048], BF16)
    scomb = p.sb("r2_scom", [128, 4], F32)
    rowb = p.sb("r2_rowb", [128, 64], F32)
    selbf = p.sb("r2_selbf", [128, D], BF16)
    seltp = p.sb("r2_seltp", [128, D], BF16)
    gcol = p.sb("r2_gcol", [128, NT], F32)
    mcol = p.sb("r2_mcol", [128, NT], F32)
    ccol = p.sb("r2_ccol", [128, NT], F32)
    gil = p.sb("r2_gil", [16, 128], F32)
    gbc = p.sb("r2_gbc", [128, 2048], F32)
    mbc = p.sb("r2_mbc", [128, 2048], F32)
    cbc = p.sb("r2_cbc", [128, 2048], F32)
    scr = p.sb("r2_scr", [128, 2048], F32)
    scr2 = p.sb("r2_scr2", [128, 2048], F32)
    STsb = [p.sb(f"r2_stt{i}", [128, 512], BF16) for i in range(NT)]
    phisb = p.sb("r2_phi", [128, NT * D], BF16)
    plosb = p.sb("r2_plo", [128, NT * D], BF16)
    prr = Ring(p, "r2_pr", [128, D], F32, n=2)
    stw = Ring(p, "r2_stw", [128, 4], F32, n=2)

    def row_bcast(dst, row_ap, after, tag2):
        toks = []
        for s0 in range(0, 2048, 512):
            m = p.t(lambda t, s0=s0: t.matmul(
                psA[:, s0:s0 + 512], ones1f[:], row_ap[0:1, s0:s0 + 512],
                start=True, stop=True), after=after)
            toks.append(p.s(lambda s, s0=s0: s.activation(
                dst[:, s0:s0 + 512], psA[:, s0:s0 + 512], AF.Copy), after=[m]))
        return toks

    bat_g = []
    for b in range(B):
        dd0 = p.sb(f"r2_dd{b}", [16, 2 * 128], F32)
        l1 = p.dma("sync", dd0[0:16, 0:128], dd[b, 0], after=bat_g)
        l2 = p.dma("sync", dd0[0:16, 128:256], dd[b, 1], after=bat_g)
        g1 = p.v(lambda v, dd0=dd0: v.tensor_add(
            gil[0:16, :], dd0[0:16, 0:128], dd0[0:16, 128:256]),
            after=[l1, l2, nb] + bat_g)
        g2 = p.s(lambda s: s.activation(gil[0:16, :], gil[0:16, :], AF.Sigmoid,
                                        bias=bias16[0:16, 0:1]), after=[g1])
        sg = p.dma("gpsimd", grow_d[b, 0].rearrange("(i p) -> i p", p=128),
                   gil[0:16, :], after=[g2])
        tm1 = p.t(lambda t: t.matmul(psB[:, 0:128], gil[0:16, 0:128],
                                     identf[0:16, :], start=True, stop=True,
                                     is_transpose=True), after=[g2]) \
            if False else p.t(lambda t: t.transpose(
                psB[:, 0:16], gil[0:16, 0:128], identf[0:16, 0:16]), after=[g2])
        ev1 = p.s(lambda s: s.activation(gcol[:, 0:NT], psB[:, 0:NT], AF.Copy),
                  after=[tm1])
        lrow = p.dma("sync", grown[0:1, :], grow_d[b, 0].unsqueeze(0), after=[sg])
        gb_t = row_bcast(gbc, grown, [lrow, ev1], "g")
        mtoks = []
        for i in range(NT):
            c1 = p.v(lambda v, i=i: v.tensor_scalar(
                out=scr[:], in0=gbc[:], scalar1=gcol[:, i:i + 1], scalar2=None,
                op0=ALU.is_gt), after=gb_t + [ev1] + mtoks[-1:])
            c2 = p.v(lambda v, i=i: v.tensor_scalar(
                out=scr2[:], in0=gbc[:], scalar1=gcol[:, i:i + 1], scalar2=None,
                op0=ALU.is_equal), after=gb_t + [c1])
            c3 = p.g(lambda g, i=i: g.affine_select(
                out=scr2[:], in_=scr2[:], pattern=[[-1, 2048]],
                compare_op=ALU.is_gt, fill=0.0, base=i * 128,
                channel_multiplier=1), after=[c2])
            c4 = p.v(lambda v: v.tensor_add(scr[:], scr[:], scr2[:]),
                     after=[c3])
            stb = stw.next()
            c5 = p.v(lambda v, stb=stb: v.reduce_sum(stb[:, 0:1], scr[:], axis=AX.X),
                     after=[c4] + stw.guard)
            c6 = p.v(lambda v, stb=stb, i=i: v.tensor_scalar(
                out=mcol[:, i:i + 1], in0=stb[:, 0:1], scalar1=512.0, scalar2=None,
                op0=ALU.is_lt), after=[c5])
            stw.set_guard([c6])
            mtoks.append(c6)
        tm2 = p.t(lambda t: t.transpose(psB[0:16, 0:128], mcol[:, 0:NT],
                                        identf[:]), after=mtoks)
        ev2 = p.s(lambda s: s.activation(gil[0:16, :], psB[0:16, 0:128], AF.Copy),
                  after=[tm2])
        sm = p.dma("gpsimd", grow_d[b, 1].rearrange("(i p) -> i p", p=128),
                   gil[0:16, :], after=[ev2])
        lrow2 = p.dma("sync", grown[0:1, :], grow_d[b, 1].unsqueeze(0), after=[sm])
        mb_t = row_bcast(mbc, grown, [lrow2], "m")
        ctoks = []
        for i in range(NT):
            c1 = p.g(lambda g, i=i: g.affine_select(
                out=scr[:], in_=mbc[:], pattern=[[-1, 2048]], compare_op=ALU.is_gt,
                fill=0.0, base=i * 128, channel_multiplier=1),
                after=mb_t + ctoks[-1:])
            stb = stw.next()
            c5 = p.v(lambda v, stb=stb: v.reduce_sum(stb[:, 1:2], scr[:], axis=AX.X),
                     after=[c1] + stw.guard)
            c6 = p.v(lambda v, stb=stb, i=i: v.tensor_copy(ccol[:, i:i + 1],
                                                           stb[:, 1:2]), after=[c5])
            stw.set_guard([c6])
            ctoks.append(c6)
        st_toks = []
        for i in range(NT):
            t1 = p.v(lambda v, i=i: v.tensor_scalar(
                out=STsb[i][:], in0=ibc[:], scalar1=ccol[:, i:i + 1],
                scalar2=mcol[:, i:i + 1], op0=ALU.is_equal, op1=ALU.mult),
                after=ctoks + [mtoks[i]] + bat_g)
            so = p.dma("gpsimd", ST_d[b, i], STsb[i][:], after=[t1])
            st_toks.append(t1)
            st_toks.append(so)
        tm3 = p.t(lambda t: t.transpose(psB[0:16, 0:128], ccol[:, 0:NT],
                                        identf[:]), after=ctoks)
        ev3 = p.s(lambda s: s.activation(gil[0:16, :], psB[0:16, 0:128], AF.Copy),
                  after=[tm3, sm])
        sc_ = p.dma("gpsimd", grow_d[b, 2].rearrange("(i p) -> i p", p=128),
                    gil[0:16, :], after=[ev3])
        lrow3 = p.dma("sync", grown[0:1, :], grow_d[b, 2].unsqueeze(0), after=[sc_])
        cb_t = row_bcast(cbc, grown, [lrow3], "c")
        s_toks = []
        for s4 in range(4):
            t1 = p.v(lambda v, s4=s4: v.tensor_scalar(
                out=scr[:], in0=cbc[:], scalar1=float(s4 * 128),
                scalar2=iota_c[:, 0:1], op0=ALU.subtract, op1=ALU.is_equal),
                after=cb_t + s_toks[-2:])
            t2 = p.v(lambda v, s4=s4: v.tensor_mul(scr2b[:], scr[:], mbc[:]),
                     after=[t1] + mb_t)
            so = p.dma("gpsimd", S_d[b, s4], scr2b[:], after=[t2])
            t3 = p.v(lambda v: v.tensor_mul(scr[:], scr2b[:], gbc[:]),
                     after=[t2] + gb_t)
            stb = stw.next()
            t4 = p.v(lambda v, stb=stb: v.reduce_sum(stb[:, 2:3], scr[:], axis=AX.X),
                     after=[t3] + stw.guard)
            t5 = p.v(lambda v, stb=stb, s4=s4: v.tensor_copy(
                scomb[:, s4:s4 + 1], stb[:, 2:3]), after=[t4])
            stw.set_guard([t5])
            s_toks += [so, t5]
        ssc = p.dma("gpsimd", scores_d[b], scomb[:, 0:4], after=s_toks)
        rg_toks = []
        for (tab, dst) in ((ctab, ropec2), (stab, ropes2)):
            for s4 in range(4):
                mms = []
                for c in range(NT):
                    mmx = p.t(lambda t, c=c, s4=s4, tab=tab: t.matmul(
                        psB[:, 0:64], STsb[c][:, s4 * 128:(s4 + 1) * 128],
                        tab[:, c * DH:(c + 1) * DH],
                        start=(c == 0), stop=(c == NT - 1)),
                        after=st_toks + [ev3], sync=(c == NT - 1))
                ev = p.s(lambda s, s4=s4, dst=dst: s.activation(
                    rowb[:, 0:64], psB[:, 0:64], AF.Copy),
                    after=[mmx] + rg_toks[-1:])
                so = p.dma("gpsimd", dst[b, s4 * 128:(s4 + 1) * 128, :],
                           rowb[:, 0:64], after=[ev])
                p.wait("tensor", [ev])
                rg_toks.append(so)
        ph_toks = []
        for i in range(NT):
            pt = prr.next()
            lp = p.dma("scalar", pt[:], proc_tm[b, i * 128:(i + 1) * 128, :],
                       after=prr.guard)
            h1 = p.s(lambda s, pt=pt, i=i: s.activation(
                phisb[:, i * D:(i + 1) * D], pt[:], AF.Copy), after=[lp] + bat_g)
            h2 = p.v(lambda v, pt=pt, i=i: v.tensor_sub(
                plosb[:, i * D:(i + 1) * D], pt[:], phisb[:, i * D:(i + 1) * D]),
                after=[h1])
            prr.set_guard([h2])
            so1 = p.dma("gpsimd", phi_d[b, i * 128:(i + 1) * 128, :],
                        phisb[:, i * D:(i + 1) * D], after=[h1])
            so2 = p.dma("gpsimd", plo_d[b, i * 128:(i + 1) * 128, :],
                        plosb[:, i * D:(i + 1) * D], after=[h2])
            ph_toks += [so1, so2, h2]
        selstores = []
        for s4 in range(4):
            mms = []
            for half in range(2):
                first = True
                for c in range(NT):
                    for (src_, last) in ((phisb, False), (plosb, c == NT - 1)):
                        mmx = p.t(lambda t, c=c, s4=s4, half=half, src_=src_,
                                  last=last: t.matmul(
                            psA[:, half * 512:(half + 1) * 512],
                            STsb[c][:, s4 * 128:(s4 + 1) * 128],
                            src_[:, c * D + half * 512:c * D + half * 512 + 512],
                            start=(c == 0 and src_ is phisb), stop=last),
                            after=ph_toks + st_toks, sync=last)
                mms.append(mmx)
            pt = prr.next()
            ev = p.v(lambda v, pt=pt: v.tensor_copy(pt[:], psA[:, 0:1024]),
                     after=mms + prr.guard)
            p.wait("tensor", [ev])
            so1 = p.dma("gpsimd", sel_tm[b, s4 * 128:(s4 + 1) * 128, :], pt[:],
                        after=[ev])
            cb2 = p.s(lambda s, pt=pt: s.activation(selbf[:], pt[:], AF.Copy),
                      after=[ev] + selstores[-1:])
            o2 = store_fm(p, selbf, sel_fm[b], s4, [cb2], seltp, nch=NC)
            prr.set_guard([so1, o2])
            selstores += [so1, o2]
        bat_g = selstores + rg_toks + [ssc] + st_toks
    return


def build_router1():
    p = Prog("r1")
    aps = {
        "xn_tm": p.dram("xn_tm", [B, T, D], F32, kind="ExternalInput"),
        "xn_fm": p.dram("xn_fm", [B, NC, 128, T], BF16, kind="ExternalInput"),
        "proc_tm": p.dram("proc_tm", [B, T, D], F32, kind="ExternalInput"),
        "prior_w": p.dram("prior_w", [D, 2 * D], F32, kind="ExternalInput"),
        "prior_norm_w": p.dram("prior_norm_w", [D], F32, kind="ExternalInput"),
        "dd": p.dram("dd", [B, 2, NT, 128], F32, kind="ExternalOutput"),
        "partial": p.dram("partial", [1, 2], F32, kind="ExternalOutput"),
    }
    emit_router1(p, aps)
    return p


def build_router2():
    p = Prog("r2")
    aps = {
        "dd": p.dram("dd", [B, 2, NT, 128], F32, kind="ExternalInput"),
        "allpart": p.dram("allpart", [8, 2], F32, kind="ExternalInput"),
        "proc_tm": p.dram("proc_tm", [B, T, D], F32, kind="ExternalInput"),
        "ST": p.dram("ST", [B, NT, 128, 512], BF16, kind="ExternalOutput"),
        "Smat": p.dram("Smat", [B, 4, 128, T], BF16, kind="ExternalOutput"),
        "scores_sm": p.dram("scores_sm", [B, 128, 4], F32, kind="ExternalOutput"),
        "ropec2": p.dram("ropec2", [B, 512, DH], F32, kind="ExternalOutput"),
        "ropes2": p.dram("ropes2", [B, 512, DH], F32, kind="ExternalOutput"),
        "sel_tm": p.dram("sel_tm", [B, 512, D], F32, kind="ExternalOutput"),
        "sel_fm": p.dram("sel_fm", [B, NC, 128, 512], BF16, kind="ExternalOutput"),
        "phi": p.dram("phi", [B, T, D], BF16, kind="ExternalOutput"),
        "plo": p.dram("plo", [B, T, D], BF16, kind="ExternalOutput"),
        "c_ident_f32": const_ap(p, "ident_f32"),
        "c_iota512_row": const_ap(p, "iota512_row"),
        "c_iota128_col": const_ap(p, "iota128_col"),
        "c_ropeb_cos_bf": const_ap(p, "ropeb_cos_bf"),
        "c_ropeb_sin_bf": const_ap(p, "ropeb_sin_bf"),
    }
    emit_router2(p, aps)
    return p



def emit_delta(p, aps):
    y_tm, sel_tm, scores_sm = aps["y_tm"], aps["sel_tm"], aps["scores_sm"]
    dhi, dlo = aps["dhi"], aps["dlo"]
    yr = Ring(p, "dl_y", [128, D], F32, n=2)
    sr = Ring(p, "dl_s", [128, D], F32, n=2)
    hb = Ring(p, "dl_h", [128, D], BF16, n=2)
    sc = p.sb("dl_sc", [128, 4], F32)
    for b in range(B):
        ls = p.dma("sync", sc[:], scores_sm[b])
        for s4 in range(4):
            yt = yr.next()
            l1 = p.dma("sync", yt[:], y_tm[b, s4 * 128:(s4 + 1) * 128, :],
                       after=yr.guard)
            st_ = sr.next()
            l2 = p.dma("scalar", st_[:], sel_tm[b, s4 * 128:(s4 + 1) * 128, :],
                       after=sr.guard)
            t1 = p.v(lambda v, yt=yt, st_=st_: v.tensor_sub(yt[:], yt[:], st_[:]),
                     after=[l1, l2])
            t2 = p.v(lambda v, yt=yt, s4=s4: v.tensor_scalar(
                out=yt[:], in0=yt[:], scalar1=sc[:, s4:s4 + 1], scalar2=None,
                op0=ALU.mult), after=[t1, ls])
            sr.set_guard([t1])
            h1b = hb.next()
            h1 = p.s(lambda s, yt=yt, h1b=h1b: s.activation(h1b[:], yt[:], AF.Copy),
                     after=[t2] + hb.guard)
            so1 = p.dma("gpsimd", dhi[b, s4 * 128:(s4 + 1) * 128, :], h1b[:],
                        after=[h1])
            t3 = p.v(lambda v, yt=yt, h1b=h1b: v.tensor_sub(yt[:], yt[:], h1b[:]),
                     after=[h1])
            yr.set_guard([t3, so1])
            hb.set_guard([so1, t3])
            h2b = hb.next()
            h2 = p.s(lambda s, yt=yt, h2b=h2b: s.activation(h2b[:], yt[:], AF.Copy),
                     after=[t3] + hb.guard)
            so2 = p.dma("gpsimd", dlo[b, s4 * 128:(s4 + 1) * 128, :], h2b[:],
                        after=[h2])
            hb.set_guard([so2])
    return



def emit_combine(p, aps):
    x0, o_fm, Wo_d = aps["x0"], aps["o_fm"], aps["Wo"]
    proc_tm, S_d = aps["proc_tm"], aps["Smat"]
    dhi, dlo = aps["dhi"], aps["dlo"]
    x1_tm, xn1_tm, xn1_fm = aps["x1_tm"], aps["xn1_tm"], aps["xn1_fm"]
    psA = p.ps("cb_psA", [128, 1024])
    wring = Ring(p, "cb_wld", [128, 1024], F32, n=2)
    Wo = FMWeight(p, "cb_wo", Wo_d, D, D, ring=wring)
    osb = p.sb("cb_osb", [128, NC * T], BF16)
    Ssb = [p.sb(f"cb_S{s4}", [128, T], BF16) for s4 in range(4)]
    dsb = [p.sb(f"cb_d{s4}", [128, 2 * D], BF16) for s4 in range(4)]
    xr = Ring(p, "cb_x", [128, D], F32, n=2)
    pr = Ring(p, "cb_p", [128, D], F32, n=2)
    st = Ring(p, "cb_st", [128, 4], F32, n=3)
    sqr = Ring(p, "cb_sq", [128, D], F32, n=2)
    cbb = Ring(p, "cb_bb", [128, D], BF16, n=2)
    tpb = Ring(p, "cb_tp", [128, D], BF16, n=2)
    bat_g = []
    for b in range(B):
        lo_ = p.dma("sync", osb[:].rearrange("p (c t) -> p c t", c=NC),
                    o_fm[b].rearrange("c p t -> p c t"), after=bat_g)
        lS = [p.dma("sync", Ssb[s4][:], S_d[b, s4], after=bat_g) for s4 in range(4)]
        lD = []
        for s4 in range(4):
            lD.append(p.dma("scalar", dsb[s4][:, 0:D],
                            dhi[b, s4 * 128:(s4 + 1) * 128, :], after=bat_g))
            lD.append(p.dma("scalar", dsb[s4][:, D:2 * D],
                            dlo[b, s4 * 128:(s4 + 1) * 128, :], after=bat_g))
        for i in range(NT):
            mms = []
            for half in range(2):
                for c in range(NC):
                    mm = p.t(lambda t, c=c, i=i, half=half, Wo=Wo: t.matmul(
                        psA[:, half * 512:(half + 1) * 512],
                        osb[:, c * T + i * 128:c * T + i * 128 + 128],
                        Wo.chunks[c][:, half * 512:(half + 1) * 512],
                        start=(c == 0), stop=False),
                        after=[lo_] + (Wo.tok if b == 0 and i == 0 else []),
                        sync=False)
                for s4 in range(4):
                    for (off, last) in ((0, False), (D, s4 == 3)):
                        mm = p.t(lambda t, s4=s4, i=i, half=half, off=off,
                                 last=last: t.matmul(
                            psA[:, half * 512:(half + 1) * 512],
                            Ssb[s4][:, i * 128:(i + 1) * 128],
                            dsb[s4][:, off + half * 512:off + half * 512 + 512],
                            start=False, stop=last),
                            after=lS + lD, sync=last)
                mms.append(mm)
            xt = xr.next()
            lx = p.dma("sync", xt[:], x0[b, i * 128:(i + 1) * 128, :],
                       after=xr.guard)
            pt = pr.next()
            lp = p.dma("scalar", pt[:], proc_tm[b, i * 128:(i + 1) * 128, :],
                       after=pr.guard)
            a1 = p.v(lambda v, xt=xt, pt=pt: v.tensor_add(xt[:], xt[:], pt[:]),
                     after=[lx, lp])
            a2 = p.v(lambda v, xt=xt: v.tensor_add(xt[:], xt[:], psA[:, 0:1024]),
                     after=[a1] + mms)
            pr.set_guard([a1])
            p.wait("tensor", [a2])
            sx = p.dma("gpsimd", x1_tm[b, i * 128:(i + 1) * 128, :], xt[:],
                       after=[a2])
            stb = st.next()
            sqb = sqr.next()
            p.wait("vector", st.guard + sqr.guard)
            tok, _ = ln_tile(p, xt, stb, sqb, after=[a2, sx])
            st.set_guard([tok])
            sqr.set_guard([tok])
            sx2 = p.dma("gpsimd", xn1_tm[b, i * 128:(i + 1) * 128, :], xt[:],
                        after=[tok])
            ob = cbb.next()
            cb1 = p.s(lambda s, ob=ob, xt=xt: s.activation(ob[:], xt[:], AF.Copy),
                      after=[tok] + cbb.guard)
            xr.set_guard([sx, sx2, cb1])
            tmp = tpb.next()
            p.wait("sync", tpb.guard)
            o2 = store_fm(p, ob, xn1_fm[b], i, [cb1], tmp, nch=NC)
            cbb.set_guard([o2])
            tpb.set_guard([o2])
            bat_g = [sx, sx2, o2, a2]
    return



def emit_l1attn(p, aps):
    xn1_fm, wqkv_d, wout_d, bout_d = aps["xn1_fm"], aps["w_qkv"], aps["w_out"], aps["b_out"]
    a1_tm = aps["a1_tm"]
    psA = p.ps("f_psA", [128, 2048])
    psB = p.ps("f_psB", [128, 1024])
    maskb = p.sb("f_maskb", [128, 128], F32)
    cmk = p.dma("sync", maskb[:], aps["c_maskbias_causal"])
    p.init_toks.append(cmk)
    q_fm = p.dram("f_qfm", [B, 8, 128, T], BF16)
    k_fm = p.dram("f_kfm", [B, 8, 128, T], BF16)
    v_tm = p.dram("f_vtm", [B, T, D], BF16)
    wslots = ([p.sb(f"f_ws{i}", [128, 1024], BF16) for i in range(NC)],
              {"toks": []})
    wring = Ring(p, "f_wld", [128, 1024], F32, n=2)
    xsb = p.sb("f_x", [128, NC * T], BF16)
    bb = Ring(p, "f_bb", [128, 1024], BF16, n=2)
    tpb = Ring(p, "f_tp", [128, 1024], BF16, n=2)
    xtf = Ring(p, "f_xt", [128, D], F32, n=2)
    ones1f = p.sb("f_ones1", [1, 128], F32)
    t_o1 = p.v(lambda v: v.memset(ones1f[:], 1.0))
    brow = p.sb("f_brow", [1, D], F32)
    lb = p.dma("sync", brow[:], bout_d.unsqueeze(0))
    bbc = p.sb("f_bbc", [128, D], F32)
    bts = []
    for s0 in range(0, D, 512):
        m = p.t(lambda t, s0=s0: t.matmul(psB[:, s0:s0 + 512], ones1f[:],
                                          brow[0:1, s0:s0 + 512], start=True,
                                          stop=True), after=[lb, t_o1])
        bts.append(p.s(lambda s, s0=s0: s.activation(
            bbc[:, s0:s0 + 512], psB[:, s0:s0 + 512], AF.Copy), after=[m]))
    p.init_toks += bts
    bat_g = []
    for b in range(B):
        t_x = p.dma("sync", xsb[:].rearrange("p (c t) -> p c t", c=NC),
                    xn1_fm[b].rearrange("c p t -> p c t"), after=bat_g)
        stores = {"q": [], "k": [], "v": []}
        for (nm, coff, dst) in (("q", 0, q_fm), ("k", D, k_fm), ("v", 2 * D, v_tm)):
            W = FMWeight(p, f"f_{nm}w{b}",
                         wqkv_d[:, coff:coff + D], D, D, ring=wring, slots=wslots)
            mlast = []
            for i in range(NT):
                mms = []
                for half in range(2):
                    for c in range(NC):
                        mm = p.t(lambda t, c=c, i=i, half=half, W=W, xsb=xsb: t.matmul(
                            psA[:, half * 512:(half + 1) * 512],
                            xsb[:, c * T + i * 128:c * T + i * 128 + 128],
                            W.chunks[c][:, half * 512:(half + 1) * 512],
                            start=(c == 0), stop=(c == NC - 1)),
                            after=[t_x] + (W.tok if i == 0 else []),
                            sync=(c == NC - 1))
                    mms.append(mm)
                mlast.append(mms[-1])
                ob = bb.next()
                ev = p.s(lambda s, ob=ob: s.activation(ob[:], psA[:, 0:1024],
                                                       AF.Copy),
                         after=mms + bb.guard)
                p.wait("tensor", [ev])
                if nm == "v":
                    so = p.dma("gpsimd", v_tm[b, i * 128:(i + 1) * 128, :], ob[:],
                               after=[ev])
                else:
                    tmp = tpb.next()
                    p.wait("sync", tpb.guard)
                    so = store_fm(p, ob, dst[b], i, [ev], tmp, nch=NC)
                    tpb.set_guard([so])
                bb.set_guard([so])
                stores[nm].append(so)
            wslots[1]["toks"] = mlast
        qf = p.sb(f"f_qf", [128, 8 * T], BF16) if b == 0 else qf
        kf = p.sb(f"f_kf", [128, 8 * T], BF16) if b == 0 else kf
        Eexp = p.sb(f"f_Ee", [128, 2048], BF16) if b == 0 else Eexp
        Fsb = p.sb(f"f_F", [128, 2048], BF16) if b == 0 else Fsb
        opair = p.sb(f"f_op", [128, 128], BF16) if b == 0 else opair
        oTb = p.sb(f"f_oT", [128, 128], BF16) if b == 0 else oTb
        stt_ = p.sb(f"f_stt", [128, 8], F32) if b == 0 else stt_
        o_fm = p.dram("f_ofm", [B, NC, 128, T], BF16) if b == 0 else o_fm
        lq = p.dma("sync", qf[:].rearrange("p (h t) -> p h t", h=8),
                   q_fm[b].rearrange("h p t -> p h t"),
                   after=stores["q"] + bat_g)
        lk = p.dma("sync", kf[:].rearrange("p (h t) -> p h t", h=8),
                   k_fm[b].rearrange("h p t -> p h t"),
                   after=stores["k"] + bat_g)
        vf = xsb
        lv = p.dma("scalar", vf[:].rearrange("p (i d) -> p i d", i=NT),
                   v_tm[b].rearrange("(i p) d -> p i d", p=128),
                   after=[mlast[-1]] + stores["v"] + bat_g)
        tEr = tFr = tOr = lq
        prevT = [lq]
        prevTT = lq
        ostores = []
        SCL = 0.125
        for i in range(NT):
            span = (i + 1) * 128
            for hp in range(8):
                otoks = []
                for hh in range(2):
                    qsl = qf[hh * 64:(hh + 1) * 64,
                              hp * T + i * 128:hp * T + i * 128 + 128]
                    mms = []
                    for s0 in range(0, span, 512):
                        s1 = min(s0 + 512, span)
                        mm = p.t(lambda t, qsl=qsl, s0=s0, s1=s1, hp=hp, hh=hh: t.matmul(
                            psA[:, s0:s1], qsl,
                            kf[hh * 64:(hh + 1) * 64, hp * T + s0:hp * T + s1],
                            start=True, stop=True),
                            after=[lq, lk, tEr, tOr], sync=(s1 == span))
                        mms.append(mm)
                    dm = p.v(lambda v, i=i: v.tensor_add(
                        psA[:, i * 128:(i + 1) * 128],
                        psA[:, i * 128:(i + 1) * 128], maskb[:]), after=mms)
                    nmx = p.v(lambda v, span=span: v.reduce_max(
                        stt_[:, 0:1], psA[:, 0:span], axis=AX.X, negate=True),
                        after=[dm])
                    sc2 = p.v(lambda v: v.tensor_scalar_mul(
                        stt_[:, 0:1], stt_[:, 0:1], 0.125), after=[nmx])
                    ex = p.s(lambda s, span=span: s.activation(
                        Eexp[:, 0:span], psA[:, 0:span], AF.Exp, bias=stt_[:, 0:1],
                        scale=0.125, accum_out=stt_[:, 1:2]), after=[sc2, tFr])
                    tEr = ex
                    rc = p.v(lambda v: v.reciprocal(stt_[:, 2:3], stt_[:, 1:2]),
                             after=[ex])
                    tF = p.dma("sync",
                               Fsb[:, 0:span].rearrange("t (c p) -> t c p", c=i + 1),
                               Eexp[:, 0:span].rearrange("p (c t) -> p c t", c=i + 1),
                               after=[ex], transpose=True)
                    tFr = tF
                    pvs = []
                    for c in range(i + 1):
                        mm = p.t(lambda t, c=c, hp=hp, hh=hh, i=i: t.matmul(
                            psB[:, 0:64], Fsb[:, c * 128:(c + 1) * 128],
                            vf[:, c * D + (hp * 2 + hh) * 64:
                               c * D + (hp * 2 + hh) * 64 + 64],
                            start=(c == 0), stop=(c == i)),
                            after=[tF, lv], sync=(c == i))
                        pvs.append(mm)
                    oe = p.v(lambda v, hh=hh: v.tensor_scalar(
                        out=opair[:, hh * 64:(hh + 1) * 64], in0=psB[:, 0:64],
                        scalar1=stt_[:, 2:3], scalar2=None, op0=ALU.mult),
                        after=[pvs[-1], rc] + ([tOr, prevTT] if hh == 0 else []))
                    tOr = oe
                    otoks.append(oe)
                tT = p.dma("sync", oTb[:], opair[:], after=otoks + prevT,
                           transpose=True)
                prevTT = tT
                so = p.dma("gpsimd", o_fm[b][hp][:, i * 128:(i + 1) * 128], oTb[:],
                           after=[tT])
                prevT = [so]
                ostores.append(so)
        Wo = FMWeight(p, f"f_wo{b}", wout_d, D, D, ring=wring, slots=wslots)
        osb = qf
        lo_ = p.dma("sync", osb[:, 0:NC * T].rearrange("p (c t) -> p c t", c=NC),
                    o_fm[b].rearrange("c p t -> p c t"),
                    after=ostores + [tOr, tEr, tFr])
        wom = []
        for i in range(NT):
            mms = []
            for half in range(2):
                for c in range(NC):
                    mm = p.t(lambda t, c=c, i=i, half=half, Wo=Wo, osb=osb: t.matmul(
                        psA[:, half * 512:(half + 1) * 512],
                        osb[:, c * T + i * 128:c * T + i * 128 + 128],
                        Wo.chunks[c][:, half * 512:(half + 1) * 512],
                        start=(c == 0), stop=(c == NC - 1)),
                        after=[lo_] + Wo.tok, sync=(c == NC - 1))
                mms.append(mm)
            wom.append(mms[-1])
            xt = xtf.next()
            ad = p.v(lambda v, xt=xt: v.tensor_add(xt[:], bbc[:], psA[:, 0:1024]),
                     after=mms + xtf.guard)
            p.wait("tensor", [ad])
            so = p.dma("gpsimd", a1_tm[b, i * 128:(i + 1) * 128, :], xt[:],
                       after=[ad])
            xtf.set_guard([so])
        wslots[1]["toks"] = wom
        bat_g = [so, tOr, tEr]
    return



L1H = 8
L1I = 512
L1HID = 2730


def emit_l1layer(p, aps, tag, li, x_tm, x_fm, out_tm, out_fm, final=None):
    wqkv, wo_d = aps["wqkv"], aps["wo"]
    qs_d, ks_d = aps["q_scale"], aps["k_scale"]
    lng, lnb = aps["ln_g"], aps["ln_b"]
    w1_d, b1_d, w2_d, b2_d = aps["w1"], aps["b1"], aps["w2"], aps["b2"]
    psA = p.ps(f"{tag}_psA", [128, 2048])
    psB = p.ps(f"{tag}_psB", [128, 2048])
    maskb = p.sb(f"{tag}_mkb", [128, 128], F32)
    maska = p.sb(f"{tag}_mka", [128, 128], F32)
    c1_ = p.dma("sync", maskb[:], aps["c_maskbias_causal"])
    c2_ = p.dma("sync", maska[:], aps["c_maskbias_anticausal"])
    xq = p.sb(f"{tag}_xqc", [128, NT * DH], BF16)
    xq2 = p.sb(f"{tag}_xqs", [128, NT * DH], BF16)
    xk = p.sb(f"{tag}_xkc", [128, NT * DH], BF16)
    xk2 = p.sb(f"{tag}_xks", [128, NT * DH], BF16)
    c3_ = p.dma("sync", xq[:].rearrange("p (i e) -> p i e", i=NT),
                aps["c_xpos_qc_bf"].rearrange("(i p) e -> p i e", p=128))
    c4_ = p.dma("sync", xq2[:].rearrange("p (i e) -> p i e", i=NT),
                aps["c_xpos_qs_bf"].rearrange("(i p) e -> p i e", p=128))
    c5_ = p.dma("sync", xk[:].rearrange("p (i e) -> p i e", i=NT),
                aps["c_xpos_kc_bf"].rearrange("(i p) e -> p i e", p=128))
    c6_ = p.dma("sync", xk2[:].rearrange("p (i e) -> p i e", i=NT),
                aps["c_xpos_ks_bf"].rearrange("(i p) e -> p i e", p=128))
    ones1f = p.sb(f"{tag}_ones1", [1, 128], F32)
    c7_ = p.v(lambda v: v.memset(ones1f[:], 1.0))
    p.init_toks += [c1_, c2_, c3_, c4_, c5_, c6_, c7_]

    def brow(name, dram_ap, n, reps):
        row = p.sb(f"{tag}_{name}r", [1, reps * n], F32)
        toks = [p.dma("sync", row[0:1, r * n:(r + 1) * n], dram_ap.unsqueeze(0))
                for r in range(reps)]
        return row, toks

    qsr, t1 = brow("qs", qs_d[li], DH, L1H)
    ksr, t2 = brow("ks", ks_d[li], DH, L1H)
    lgr, t3 = brow("lg", lng[li], D, 1)
    lbr, t4 = brow("lb", lnb[li], D, 1)
    qsb = p.sb(f"{tag}_qsb", [128, L1I], F32)
    ksb = p.sb(f"{tag}_ksb", [128, L1I], F32)
    lgb = p.sb(f"{tag}_lgb", [128, D], F32)
    lbb = p.sb(f"{tag}_lbb", [128, D], F32)
    bct = []
    for (dst, row, toks) in ((qsb, qsr, t1), (ksb, ksr, t2), (lgb, lgr, t3),
                             (lbb, lbr, t4)):
        for s0 in range(0, dst[:].shape[1], 512):
            m = p.t(lambda t, s0=s0, row=row: t.matmul(
                psB[:, s0 % 2048:s0 % 2048 + 512], ones1f[:],
                row[0:1, s0:s0 + 512], start=True, stop=True),
                after=toks + [c7_] + bct[-4:])
            bct.append(p.s(lambda s, s0=s0, dst=dst: s.activation(
                dst[:, s0:s0 + 512], psB[:, s0 % 2048:s0 % 2048 + 512], AF.Copy),
                after=[m]))
    p.init_toks += bct

    b1rowa = p.sb(f"{tag}_b1ra", [1, 512], F32)
    b1rowg = p.sb(f"{tag}_b1rg", [1, 512], F32)
    b2row = p.sb(f"{tag}_b2r", [1, D], F32)
    t6 = p.dma("sync", b2row[:], b2_d[li].unsqueeze(0))
    p.init_toks += [t6]

    wslots = ([p.sb(f"{tag}_ws{i}", [128, 1536], BF16) for i in range(NC)],
              {"toks": []})
    wring = Ring(p, f"{tag}_wld", [128, 1536], F32, n=2)
    xsb = p.sb(f"{tag}_x", [128, NC * T], BF16)
    q_fm = p.dram(f"{tag}_qfm", [B, 4, 128, T], BF16)
    k_fm = p.dram(f"{tag}_kfm", [B, 4, 128, T], BF16)
    v_tm = p.dram(f"{tag}_vtm", [B, T, L1I], BF16)
    o_fm = p.dram(f"{tag}_ofm", [B, 4, 128, T], BF16)
    xm_tm = p.dram(f"{tag}_xm", [B, T, D], F32)
    xm_fm = p.dram(f"{tag}_xmfm", [B, NC, 128, T], BF16)
    h_fm = p.dram(f"{tag}_hfm", [B, NC, 128, T], BF16)
    ff_d = p.dram(f"{tag}_ffd", [B, NT, 128, 2816], BF16)
    bb = Ring(p, f"{tag}_bb", [128, 1536], BF16, n=2)
    tpb = Ring(p, f"{tag}_tp", [128, 1024], BF16, n=2)
    xtf = Ring(p, f"{tag}_xt", [128, D], F32, n=2)
    sqr = Ring(p, f"{tag}_sq", [128, D], F32, n=1)
    st = Ring(p, f"{tag}_st", [128, 8], F32, n=2)
    qv = Ring(p, f"{tag}_qv", [128, L1I], F32, n=2)
    rv = Ring(p, f"{tag}_rv", [128, L1I], F32, n=2)
    sqtmp = p.sb(f"{tag}_sqt", [128, 512], F32)
    w1asl = [wslots[0][c][:, 0:512] for c in range(NC)]
    w1gsl = [wslots[0][c][:, 512:1024] for c in range(NC)]
    w2sl = [wslots[0][c][:, 0:1024] for c in range(4, NC)] +         [p.sb(f"{tag}_w2x{c}", [128, 1024], BF16) for c in range(2)][0:0]
    ffTg = [p.sb(f"{tag}_ffT{i}", [128, 2816], BF16) for i in range(1)]
    zpadb = p.sb(f"{tag}_zp", [128, 86], BF16)
    gctb = Ring(p, f"{tag}_ga", [128, 512], BF16, n=2)
    ffb2 = Ring(p, f"{tag}_ffb2", [128, 512], BF16, n=2)
    out_acc = p.sb(f"{tag}_oacc", [128, 1 * D], F32)
    xex = Ring(p, f"{tag}_xex", [128, D], F32, n=2)
    fin_toks = []
    b2b = p.sb(f"{tag}_b2b", [128, D], F32)
    for s0 in range(0, D, 512):
        m = p.t(lambda t, s0=s0: t.matmul(
            psB[:, s0:s0 + 512], ones1f[:], b2row[0:1, s0:s0 + 512],
            start=True, stop=True), after=[t6, c7_] + p.init_toks[-4:] + bct[-4:])
        p.init_toks.append(p.s(lambda s, s0=s0: s.activation(
            b2b[:, s0:s0 + 512], psB[:, s0:s0 + 512], AF.Copy), after=[m]))

    def load_shift(dst, src_fm, after):
        toks = []
        for c in range(NC):
            if c < 4:
                toks.append(p.dma("sync", dst[:, c * T:(c + 1) * T],
                                  src_fm[c], after=after))
            else:
                toks.append(p.dma("sync", dst[:, c * T + 1:(c + 1) * T],
                                  src_fm[c][:, 0:T - 1], after=after))
        z = p.v(lambda v: v.memset(
            dst[:].rearrange("p (c t) -> p c t", c=NC)[:, 4:8, 0:1], 0.0),
            after=after)
        return toks + [z]

    bat_g = []
    for b in range(B):
        lsh = load_shift(xsb, x_fm[b], bat_g)
        stores = {"q": [], "k": [], "v": []}
        W = FMWeight(p, f"{tag}_qkv{b}", wqkv[li], D, 3 * L1I, ring=wring,
                     slots=wslots)
        mlast = []
        for i in range(NT):
            mms = []
            for sp in range(3):
                for c in range(NC):
                    mm = p.t(lambda t, c=c, i=i, sp=sp, W=W, xsb=xsb: t.matmul(
                        psA[:, sp * 512:(sp + 1) * 512],
                        xsb[:, c * T + i * 128:c * T + i * 128 + 128],
                        W.chunks[c][:, sp * 512:(sp + 1) * 512],
                        start=(c == 0), stop=(c == NC - 1)),
                        after=lsh + (W.tok if i == 0 else []), sync=(c == NC - 1))
                mms.append(mm)
            mlast.append(mms[-1])
            for (nm, off, tabc, tabs, scb) in (("q", 0, xq, xq2, qsb),
                                               ("k", 512, xk, xk2, ksb)):
                qt = qv.next()
                ev = p.v(lambda v, qt=qt, off=off: v.tensor_copy(
                    qt[:], psA[:, off:off + 512]), after=mms + qv.guard)
                stb = st.next()
                p.wait("scalar", st.guard)
                sq2 = p.s(lambda s, qt=qt: s.activation(
                    sqtmp[:, 0:512], qt[:], AF.Square), after=[ev])
                r1b = p.v(lambda v, stb=stb: v.reduce_sum(
                    stb[:, 0:8], sqtmp[:, 0:512].rearrange("p (h d) -> p h d",
                                                           h=L1H),
                    axis=AX.X), after=[sq2])
                r2 = p.s(lambda s, stb=stb: s.activation(stb[:, 0:8], stb[:, 0:8],
                                                         AF.Sqrt), after=[r1b])
                r3 = p.v(lambda v, stb=stb: v.tensor_scalar_max(
                    stb[:, 0:8], stb[:, 0:8], 1e-12), after=[r2])
                r4 = p.v(lambda v, stb=stb: v.reciprocal(stb[:, 0:8], stb[:, 0:8]),
                         after=[r3])
                n1 = p.v(lambda v, qt=qt, stb=stb: v.tensor_mul(
                    qt[:].rearrange("p (h d) -> p h d", h=L1H),
                    qt[:].rearrange("p (h d) -> p h d", h=L1H),
                    stb[:, 0:8].unsqueeze(2).to_broadcast([128, L1H, DH])),
                    after=[r4])
                n2 = p.v(lambda v, qt=qt, scb=scb: v.tensor_mul(qt[:], qt[:],
                                                                scb[:]),
                         after=[n1])
                st.set_guard([n2])
                rb2 = rv.next()
                vi = qt[:].rearrange("p (h two i2) -> p h two i2", two=2, i2=32)
                vo = rb2[:].rearrange("p (h two i2) -> p h two i2", two=2, i2=32)
                svi = tabs[:, i * DH:(i + 1) * DH].rearrange(
                    "p (two i2) -> p two i2", two=2)
                x1_ = p.v(lambda v, vo=vo, vi=vi, svi=svi: v.tensor_mul(
                    vo[:, :, 0, :], vi[:, :, 1, :],
                    svi[:, 0, :].unsqueeze(1).to_broadcast([128, L1H, 32])),
                    after=[n2] + rv.guard)
                x2_ = p.v(lambda v, vo=vo, vi=vi, svi=svi: v.tensor_mul(
                    vo[:, :, 1, :], vi[:, :, 0, :],
                    svi[:, 1, :].unsqueeze(1).to_broadcast([128, L1H, 32])),
                    after=[n2])
                x3_ = p.v(lambda v, qt=qt, tabc=tabc, i=i: v.tensor_mul(
                    qt[:].rearrange("p (h e) -> p h e", e=DH),
                    qt[:].rearrange("p (h e) -> p h e", e=DH),
                    tabc[:, i * DH:(i + 1) * DH].unsqueeze(1).to_broadcast(
                        [128, L1H, DH])), after=[x1_, x2_])
                ob = bb.next()
                x4_ = p.v(lambda v, ob=ob, qt=qt, rb2=rb2: v.tensor_add(
                    ob[:, 0:512], qt[:], rb2[:]), after=[x3_] + bb.guard)
                rv.set_guard([x4_])
                qv.set_guard([x4_])
                tmp = tpb.next()
                p.wait("sync", tpb.guard)
                dst = q_fm if nm == "q" else k_fm
                so = store_fm(p, ob[:, 0:512], dst[b], i, [x4_], tmp, nch=4)
                tpb.set_guard([so])
                bb.set_guard([so])
                stores[nm].append(so)
            ob = bb.next()
            ev = p.s(lambda s, ob=ob: s.activation(ob[:, 0:512],
                                                   psA[:, 1024:1536], AF.Copy),
                     after=mms + bb.guard)
            p.wait("tensor", [ev])
            so = p.dma("gpsimd", v_tm[b, i * 128:(i + 1) * 128, :], ob[:, 0:512],
                       after=[ev])
            bb.set_guard([so])
            stores["v"].append(so)
        wslots[1]["toks"] = mlast
        qf = p.sb(f"{tag}_qf", [128, 4 * T], BF16) if b == 0 else qf
        kf = p.sb(f"{tag}_kf", [128, 4 * T], BF16) if b == 0 else kf
        Eexp = p.sb(f"{tag}_Ee", [128, 640], BF16) if b == 0 else Eexp
        Fsb = p.sb(f"{tag}_F", [128, 640], BF16) if b == 0 else Fsb
        opair = p.sb(f"{tag}_op", [128, 128], BF16) if b == 0 else opair
        oTb = p.sb(f"{tag}_oT", [128, 128], BF16) if b == 0 else oTb
        stt_ = p.sb(f"{tag}_stt", [128, 8], F32) if b == 0 else stt_
        lq = p.dma("sync", qf[:].rearrange("p (h t) -> p h t", h=4),
                   q_fm[b].rearrange("h p t -> p h t"), after=stores["q"] + bat_g)
        lk = p.dma("sync", kf[:].rearrange("p (h t) -> p h t", h=4),
                   k_fm[b].rearrange("h p t -> p h t"), after=stores["k"] + bat_g)
        vf = xsb
        lv = p.dma("scalar", vf[:, 0:NT * L1I].rearrange("p (i d) -> p i d", i=NT),
                   v_tm[b].rearrange("(i p) d -> p i d", p=128),
                   after=[mlast[-1]] + stores["v"] + bat_g)
        tEr = tFr = tOr = lq
        prevT = [lq]
        prevTT = lq
        ostores = []
        for i in range(NT):
            w = i // 4
            kt0 = max(0, i - 4)
            nkt = i - kt0 + 1
            span = nkt * 128
            for hp in range(4):
                otoks = []
                for hh in range(2):
                    qsl = qf[hh * 64:(hh + 1) * 64,
                              hp * T + i * 128:hp * T + i * 128 + 128]
                    mms = []
                    for s0 in range(0, span, 512):
                        s1 = min(s0 + 512, span)
                        mm = p.t(lambda t, qsl=qsl, s0=s0, s1=s1, hp=hp, hh=hh,
                                 kt0=kt0: t.matmul(
                            psA[:, s0:s1], qsl,
                            kf[hh * 64:(hh + 1) * 64,
                               hp * T + kt0 * 128 + s0:hp * T + kt0 * 128 + s1],
                            start=True, stop=True),
                            after=[lq, lk, tEr, tOr], sync=(s1 == span))
                        mms.append(mm)
                    dm = p.v(lambda v, span=span: v.tensor_add(
                        psA[:, span - 128:span],
                        psA[:, span - 128:span], maskb[:]), after=mms)
                    if i >= 4:
                        dm = p.v(lambda v: v.tensor_add(
                            psA[:, 0:128], psA[:, 0:128], maska[:]), after=[dm])
                    nmx = p.v(lambda v, span=span: v.reduce_max(
                        stt_[:, 0:1], psA[:, 0:span], axis=AX.X, negate=True),
                        after=[dm])
                    sc2 = p.v(lambda v: v.tensor_scalar_mul(
                        stt_[:, 0:1], stt_[:, 0:1], 8.0), after=[nmx])
                    ex = p.s(lambda s, span=span: s.activation(
                        Eexp[:, 0:span], psA[:, 0:span], AF.Exp, bias=stt_[:, 0:1],
                        scale=8.0, accum_out=stt_[:, 1:2]), after=[sc2, tFr])
                    tEr = ex
                    rc = p.v(lambda v: v.reciprocal(stt_[:, 2:3], stt_[:, 1:2]),
                             after=[ex])
                    tF = p.dma("sync",
                               Fsb[:, 0:span].rearrange("t (c p) -> t c p", c=nkt),
                               Eexp[:, 0:span].rearrange("p (c t) -> p c t", c=nkt),
                               after=[ex], transpose=True)
                    tFr = tF
                    pvs = []
                    for c in range(nkt):
                        mm = p.t(lambda t, c=c, hp=hp, hh=hh, kt0=kt0, nkt=nkt: t.matmul(
                            psB[:, 0:64], Fsb[:, c * 128:(c + 1) * 128],
                            vf[:, (kt0 + c) * L1I + (hp * 2 + hh) * 64:
                               (kt0 + c) * L1I + (hp * 2 + hh) * 64 + 64],
                            start=(c == 0), stop=(c == nkt - 1)),
                            after=[tF, lv], sync=(c == nkt - 1))
                        pvs.append(mm)
                    oe = p.v(lambda v, hh=hh: v.tensor_scalar(
                        out=opair[:, hh * 64:(hh + 1) * 64], in0=psB[:, 0:64],
                        scalar1=stt_[:, 2:3], scalar2=None, op0=ALU.mult),
                        after=[pvs[-1], rc] + ([tOr, prevTT] if hh == 0 else []))
                    tOr = oe
                    otoks.append(oe)
                tT = p.dma("sync", oTb[:], opair[:], after=otoks + prevT,
                           transpose=True)
                prevTT = tT
                so = p.dma("gpsimd", o_fm[b][hp][:, i * 128:(i + 1) * 128], oTb[:],
                           after=[tT])
                prevT = [so]
                ostores.append(so)
        Wo = FMWeight(p, f"{tag}_wo{b}", wo_d[li], L1I, D, ring=wring,
                      slots=wslots)
        osb = qf
        lo_ = p.dma("sync", osb[:, 0:4 * T].rearrange("p (c t) -> p c t", c=4),
                    o_fm[b].rearrange("c p t -> p c t"),
                    after=ostores + [tOr, tEr, tFr])
        wom = []
        xmstores = []
        for i in range(NT):
            mms = []
            for half in range(2):
                for c in range(4):
                    mm = p.t(lambda t, c=c, i=i, half=half, Wo=Wo, osb=osb: t.matmul(
                        psA[:, half * 512:(half + 1) * 512],
                        osb[:, c * T + i * 128:c * T + i * 128 + 128],
                        Wo.chunks[c][:, half * 512:(half + 1) * 512],
                        start=(c == 0), stop=(c == 3)),
                        after=[lo_] + Wo.tok, sync=(c == 3))
                mms.append(mm)
            wom.append(mms[-1])
            xt = xtf.next()
            lx = p.dma("scalar", xt[:], x_tm[b, i * 128:(i + 1) * 128, :],
                       after=xtf.guard)
            ad = p.v(lambda v, xt=xt: v.tensor_add(xt[:], xt[:], psA[:, 0:1024]),
                     after=mms + [lx])
            p.wait("tensor", [ad])
            sx = p.dma("gpsimd", xm_tm[b, i * 128:(i + 1) * 128, :], xt[:],
                       after=[ad])
            ob = bb.next()
            cb = p.s(lambda s, ob=ob, xt=xt: s.activation(ob[:, 0:1024], xt[:],
                                                          AF.Copy),
                     after=[ad] + bb.guard)
            xtf.set_guard([sx, cb])
            tmp = tpb.next()
            p.wait("sync", tpb.guard)
            o2 = store_fm(p, ob[:, 0:1024], xm_fm[b], i, [cb], tmp, nch=NC)
            bb.set_guard([o2])
            tpb.set_guard([o2])
            xmstores += [sx, o2]
        wslots[1]["toks"] = wom
        hstores = []
        for i in range(NT):
            xt = xtf.next()
            p.wait("sync", xtf.guard)
            l_a = p.dma("sync", xt[:, 0:512],
                        xm_tm[b, i * 128:(i + 1) * 128, 0:512],
                        after=xtf.guard + xmstores)
            if i == 0:
                zr = p.v(lambda v, xt=xt: v.memset(xt[0:1, 512:1024], 0.0),
                         after=xtf.guard)
                l_b = p.dma("sync", xt[1:128, 512:1024],
                            xm_tm[b, 0:127, 512:1024],
                            after=xtf.guard + xmstores + [zr])
            else:
                zr = None
                l_b = p.dma("sync", xt[:, 512:1024],
                            xm_tm[b, i * 128 - 1:(i + 1) * 128 - 1, 512:1024],
                            after=xtf.guard + xmstores)
            up = [l_a, l_b] + ([zr] if zr else [])
            stb = st.next()
            sqb = sqr.next()
            p.wait("vector", st.guard + sqr.guard)
            tok, _ = ln_tile(p, xt, stb, sqb, after=up, eps=1e-5)
            g1 = p.v(lambda v, xt=xt: v.tensor_mul(xt[:], xt[:], lgb[:]),
                     after=[tok])
            g2 = p.v(lambda v, xt=xt: v.tensor_add(xt[:], xt[:], lbb[:]),
                     after=[g1])
            st.set_guard([g2])
            sqr.set_guard([g2])
            ob = bb.next()
            cb = p.s(lambda s, ob=ob, xt=xt: s.activation(ob[:, 0:1024], xt[:],
                                                          AF.Copy),
                     after=[g2] + bb.guard)
            xtf.set_guard([cb])
            tmp2 = tpb.next()
            p.wait("sync", tpb.guard)
            o2 = store_fm(p, ob[:, 0:1024], h_fm[b], i, [cb], tmp2, nch=NC)
            bb.set_guard([o2])
            tpb.set_guard([o2])
            hstores.append(o2)
        lh = p.dma("sync", xsb[:].rearrange("p (c t) -> p c t", c=NC),
                   h_fm[b].rearrange("c p t -> p c t"), after=hstores)
        ffstores = []
        prevj = []
        for jb in range(6):
            a0_ = jb * 512
            a1_ = min(a0_ + 512, L1HID)
            wid = a1_ - a0_
            wa_t, wg_t = [], []
            lb1a = p.dma("sync", b1rowa[0:1, 0:wid],
                         b1_d[li][a0_:a1_].unsqueeze(0), after=prevj)
            lb1g = p.dma("sync", b1rowg[0:1, 0:wid],
                         b1_d[li][L1HID + a0_:L1HID + a1_].unsqueeze(0),
                         after=prevj)
            for (sl, off, toks) in ((w1asl, a0_, wa_t), (w1gsl, L1HID + a0_, wg_t)):
                for c in range(NC):
                    dstb = wring.next()
                    l1 = p.dma("scalar", dstb[:, 0:wid],
                               w1_d[li][c * 128:(c + 1) * 128, off:off + wid],
                               after=wring.guard + prevj)
                    gg = p.v(lambda v, c=c, dstb=dstb, sl=sl, wid=wid: v.tensor_copy(
                        sl[c][:, 0:wid], dstb[:, 0:wid]), after=[l1])
                    wring.set_guard([gg])
                    toks.append(gg)
            jtoks = []
            for i in range(NT):
                mma = []
                for c in range(NC):
                    mm = p.t(lambda t, c=c, i=i, wid=wid: t.matmul(
                        psB[:, 0:wid],
                        xsb[:, c * T + i * 128:c * T + i * 128 + 128],
                        w1asl[c][:, 0:wid], start=(c == 0), stop=False),
                        after=[lh] + wa_t + ffstores[-2:], sync=False)
                mm1 = p.t(lambda t, wid=wid: t.matmul(
                    psB[:, 0:wid], ones1f[:], b1rowa[0:1, 0:wid],
                    start=False, stop=True), after=[lb1a], sync=True)
                for c in range(NC):
                    mm = p.t(lambda t, c=c, i=i, wid=wid: t.matmul(
                        psB[:, 512:512 + wid],
                        xsb[:, c * T + i * 128:c * T + i * 128 + 128],
                        w1gsl[c][:, 0:wid], start=(c == 0), stop=False),
                        after=wg_t, sync=False)
                mm2 = p.t(lambda t, wid=wid: t.matmul(
                    psB[:, 512:512 + wid], ones1f[:], b1rowg[0:1, 0:wid],
                    start=False, stop=True), after=[lb1g], sync=True)
                gb_ = gctb.next()
                ge = p.s(lambda s, gb_=gb_, wid=wid: s.activation(
                    gb_[:, 0:wid], psB[:, 512:512 + wid], AF.Gelu),
                    after=[mm2] + gctb.guard)
                fb_ = ffb2.next()
                fm_ = p.v(lambda v, gb_=gb_, fb_=fb_, wid=wid: v.tensor_mul(
                    fb_[:, 0:wid], psB[:, 0:wid], gb_[:, 0:wid]),
                    after=[mm1, ge] + ffb2.guard)
                gctb.set_guard([fm_])
                p.wait("tensor", [fm_])
                so = p.dma("gpsimd", ff_d[b, i, :, a0_:a1_], fb_[:, 0:wid],
                           after=[fm_])
                ffb2.set_guard([so])
                jtoks.append(so)
                ffstores.append(so)
            prevj = jtoks
        zp = p.v(lambda v: v.memset(zpadb[:, 0:86], 0.0))
        zps = [p.dma("gpsimd", ff_d[b, i, :, 2730:2816], zpadb[:, 0:86],
                     after=[zp]) for i in range(NT)]
        for g0 in range(0, NT, 1):
            zacc = p.v(lambda v: v.memset(out_acc[:], 0.0), after=fin_toks)
            gts = []
            for ii in range(1):
                fT = ffTg[ii]
                lf = p.dma("sync", fT[:].rearrange("t (c p) -> t c p", c=22),
                           ff_d[b, g0 + ii].rearrange("p (c t) -> p c t", c=22),
                           after=ffstores + zps + fin_toks, transpose=True)
                gts.append(lf)
            atoks = [zacc]
            w2prev = []
            for c4 in range(0, 22, 4):
                ncc = min(4, 22 - c4)
                w2t = []
                for cc in range(c4, c4 + ncc):
                    rows = 42 if cc == 21 else 128
                    dstb = wring.next()
                    l2 = p.dma("scalar", dstb[0:rows, 0:1024],
                               w2_d[li][cc * 128:cc * 128 + rows, :],
                               after=wring.guard + w2prev)
                    gg = p.v(lambda v, cc=cc, dstb=dstb, rows=rows: v.tensor_copy(
                        w2sl[cc % 4][0:rows, 0:1024], dstb[0:rows, 0:1024]),
                        after=[l2])
                    wring.set_guard([gg])
                    w2t.append(gg)
                mlist = []
                for ii in range(1):
                    for cc in range(c4, c4 + ncc):
                        rows = 42 if cc == 21 else 128
                        for half in range(2):
                            lastcc = (cc == c4 + ncc - 1)
                            lastmm = (lastcc and half == 1)
                            mm = p.t(lambda t, cc=cc, half=half, rows=rows,
                                     ii=ii, lastcc=lastcc, c4=c4: t.matmul(
                                psA[:, (ii % 2) * 1024 + half * 512:
                                    (ii % 2) * 1024 + half * 512 + 512],
                                ffTg[ii][0:rows, cc * 128:(cc + 1) * 128],
                                w2sl[cc % 4][0:rows, half * 512:half * 512 + 512],
                                start=(cc == c4), stop=lastcc),
                                after=gts + w2t + atoks, sync=lastmm)
                        if lastmm:
                            mlist.append(mm)
                    if ii % 2 == 1 or ii == 3:
                        pass
                    ac = p.v(lambda v, ii=ii, g0=g0: v.tensor_add(
                        out_acc[:, ii * D:(ii + 1) * D],
                        out_acc[:, ii * D:(ii + 1) * D],
                        psA[:, (ii % 2) * 1024:(ii % 2) * 1024 + 1024]),
                        after=[mlist[-1], zacc] + atoks)
                    atoks = [ac]
                    p.wait("tensor", [ac])
                w2prev = [atoks[0]]
            for ii in range(1):
                i = g0 + ii
                xt = xtf.next()
                lx = p.dma("scalar", xt[:], xm_tm[b, i * 128:(i + 1) * 128, :],
                           after=xtf.guard + xmstores)
                ad = p.v(lambda v, xt=xt, ii=ii: v.tensor_add(
                    xt[:], xt[:], out_acc[:, ii * D:(ii + 1) * D]),
                    after=[lx] + atoks)
                ad = p.v(lambda v, xt=xt: v.tensor_add(xt[:], xt[:], b2b[:]),
                         after=[ad])
                if final is not None:
                    x1_tm, a1_tm, mod_out = final
                    e1b = xex.next()
                    l1_ = p.dma("sync", e1b[:], x1_tm[b, i * 128:(i + 1) * 128, :],
                                after=xex.guard)
                    ad = p.v(lambda v, xt=xt, e1b=e1b: v.tensor_add(
                        xt[:], xt[:], e1b[:]), after=[ad, l1_])
                    l2_ = p.dma("sync", e1b[:], a1_tm[b, i * 128:(i + 1) * 128, :],
                                after=[ad])
                    ad2 = p.v(lambda v, xt=xt, e1b=e1b: v.tensor_add(
                        xt[:], xt[:], e1b[:]), after=[ad, l2_])
                    xex.set_guard([ad2])
                    so = p.dma("gpsimd", mod_out[b, i * 128:(i + 1) * 128, :],
                               xt[:], after=[ad2])
                    xtf.set_guard([so])
                    fin_toks.append(so)
                else:
                    so = p.dma("gpsimd", out_tm[b, i * 128:(i + 1) * 128, :],
                               xt[:], after=[ad])
                    ob = bb.next()
                    cb = p.s(lambda s, ob=ob, xt=xt: s.activation(
                        ob[:, 0:1024], xt[:], AF.Copy), after=[ad] + bb.guard)
                    xtf.set_guard([so, cb])
                    tmp = tpb.next()
                    p.wait("sync", tpb.guard)
                    o2 = store_fm(p, ob[:, 0:1024], out_fm[b], i, [cb], tmp,
                                  nch=NC)
                    bb.set_guard([o2])
                    tpb.set_guard([o2])
                    fin_toks += [so, o2]
        bat_g = fin_toks + [tOr, tEr]
    return


def build_l1attn():
    p = Prog("l1a")
    aps = {
        "xn1_fm": p.dram("xn1_fm", [B, NC, 128, T], BF16, kind="ExternalInput"),
        "w_qkv": p.dram("w_qkv", [D, 3 * D], F32, kind="ExternalInput"),
        "w_out": p.dram("w_out", [D, D], F32, kind="ExternalInput"),
        "b_out": p.dram("b_out", [D], F32, kind="ExternalInput"),
        "a1_tm": p.dram("a1_tm", [B, T, D], F32, kind="ExternalOutput"),
        "c_maskbias_causal": const_ap(p, "maskbias_causal"),
    }
    emit_l1attn(p, aps)
    return p


def build_delta():
    p = Prog("dl")
    aps = {
        "y_tm": p.dram("y_tm", [B, 512, D], F32, kind="ExternalInput"),
        "sel_tm": p.dram("sel_tm", [B, 512, D], F32, kind="ExternalInput"),
        "scores_sm": p.dram("scores_sm", [B, 128, 4], F32, kind="ExternalInput"),
        "dhi": p.dram("dhi", [B, 512, D], BF16, kind="ExternalOutput"),
        "dlo": p.dram("dlo", [B, 512, D], BF16, kind="ExternalOutput"),
    }
    emit_delta(p, aps)
    return p


def build_combine():
    p = Prog("cb")
    aps = {
        "x0": p.dram("x0", [B, T, D], F32, kind="ExternalInput"),
        "o_fm": p.dram("o_fm", [B, NH, 128, T], BF16, kind="ExternalInput"),
        "Wo": p.dram("Wo", [D, D], F32, kind="ExternalInput"),
        "proc_tm": p.dram("proc_tm", [B, T, D], F32, kind="ExternalInput"),
        "Smat": p.dram("Smat", [B, 4, 128, T], BF16, kind="ExternalInput"),
        "dhi": p.dram("dhi", [B, 512, D], BF16, kind="ExternalInput"),
        "dlo": p.dram("dlo", [B, 512, D], BF16, kind="ExternalInput"),
        "x1_tm": p.dram("x1_tm", [B, T, D], F32, kind="ExternalOutput"),
        "xn1_tm": p.dram("xn1_tm", [B, T, D], F32, kind="ExternalOutput"),
        "xn1_fm": p.dram("xn1_fm", [B, NC, 128, T], BF16, kind="ExternalOutput"),
    }
    emit_combine(p, aps)
    return p


def build_l1layer(li, final_stage=False):
    p = Prog(f"l1f{li}")
    aps = {
        "x_tm": p.dram("x_tm", [B, T, D], F32, kind="ExternalInput"),
        "x_fm": p.dram("x_fm", [B, NC, 128, T], BF16, kind="ExternalInput"),
        "wqkv": p.dram("wqkv", [2, D, 3 * L1I], F32, kind="ExternalInput"),
        "q_scale": p.dram("q_scale", [2, DH], F32, kind="ExternalInput"),
        "k_scale": p.dram("k_scale", [2, DH], F32, kind="ExternalInput"),
        "wo": p.dram("wo", [2, L1I, D], F32, kind="ExternalInput"),
        "ln_g": p.dram("ln_g", [2, D], F32, kind="ExternalInput"),
        "ln_b": p.dram("ln_b", [2, D], F32, kind="ExternalInput"),
        "w1": p.dram("w1", [2, D, 2 * L1HID], F32, kind="ExternalInput"),
        "b1": p.dram("b1", [2, 2 * L1HID], F32, kind="ExternalInput"),
        "w2": p.dram("w2", [2, L1HID, D], F32, kind="ExternalInput"),
        "b2": p.dram("b2", [2, D], F32, kind="ExternalInput"),
        "c_maskbias_causal": const_ap(p, "maskbias_causal"),
        "c_maskbias_anticausal": const_ap(p, "maskbias_anticausal"),
        "c_xpos_qc_bf": const_ap(p, "xpos_qc_bf"),
        "c_xpos_qs_bf": const_ap(p, "xpos_qs_bf"),
        "c_xpos_kc_bf": const_ap(p, "xpos_kc_bf"),
        "c_xpos_ks_bf": const_ap(p, "xpos_ks_bf"),
    }
    if final_stage:
        aps["x1_tm"] = p.dram("x1_tm", [B, T, D], F32, kind="ExternalInput")
        aps["a1_tm"] = p.dram("a1_tm", [B, T, D], F32, kind="ExternalInput")
        aps["mod_out"] = p.dram("mod_out", [B, T, D], F32, kind="ExternalOutput")
        final = (aps["x1_tm"], aps["a1_tm"], aps["mod_out"])
        out_tm = out_fm = None
    else:
        aps["out_tm"] = p.dram("out_tm", [B, T, D], F32, kind="ExternalOutput")
        aps["out_fm"] = p.dram("out_fm", [B, NC, 128, T], BF16,
                               kind="ExternalOutput")
        final = None
        out_tm, out_fm = aps["out_tm"], aps["out_fm"]
    emit_l1layer(p, aps, f"l1f{li}", li, aps["x_tm"], aps["x_fm"],
                 out_tm, out_fm, final=final)
    return p


def build_s0():
    p = Prog("s0")
    x = p.dram("x", [B, T, D], F32, kind="ExternalInput")
    xn_tm = p.dram("xn_tm", [B, T, D], F32, kind="ExternalOutput")
    xn_fm = p.dram("xn_fm", [B, NC, 128, T], BF16, kind="ExternalOutput")
    emit_s0(p, x, xn_tm, xn_fm)
    return p



_ROPE1C = None
_ROPE1S = None


def _dec1_tables():
    global _ROPE1C, _ROPE1S
    if _ROPE1C is None:
        half = 32
        inv = 1.0 / (10000.0 ** (np.arange(half, dtype=np.float32) / half))
        ang = np.arange(T, dtype=np.float32)[:, None] * inv[None, :]
        c = np.concatenate([np.cos(ang), np.cos(ang)], 1).astype(np.float32)
        s = np.concatenate([-np.sin(ang), np.sin(ang)], 1).astype(np.float32)
        _ROPE1C = np.ascontiguousarray(np.broadcast_to(c[None], (B, T, DH)))
        _ROPE1S = np.ascontiguousarray(np.broadcast_to(s[None], (B, T, DH)))
    return _ROPE1C, _ROPE1S


_PCACHE = {}


def _prog(name, fn):
    if name not in _PCACHE:
        p = fn()
        finish(p)
        _PCACHE[name] = p.finalize()
    return _PCACHE[name]


def _run(name, fn, in_maps, outs):
    nc = _prog(name, fn)
    res = run_bass_kernel_spmd(nc, in_maps, core_ids=list(range(NCORE)))
    return [{k: r[k] for k in outs} for r in res.results]


def kernel(**inputs):
    if FUSED:
        return kernel_fused(**inputs)
    inp = {k: np.ascontiguousarray(v) for k, v in inputs.items()}
    x_full = inp["L0_attn_x"].astype(np.float32, copy=False)
    shards = [np.ascontiguousarray(x_full[2 * c:2 * c + 2]) for c in range(NCORE)]

    def rep(arrnames):
        return {k: inp[k] for k in arrnames}

    r = _run("s0", build_s0, [{"x": s} for s in shards], ["xn_tm", "xn_fm"])
    xn_tm = [c["xn_tm"] for c in r]
    xn_fm = [c["xn_fm"] for c in r]

    cst = const_feed([])
    im = [{"xn_fm": xn_fm[c], "Wq": inp["L0_attn_Wq"], "Wk": inp["L0_attn_Wk"],
           "Wv": inp["L0_attn_Wv"], "Ww": inp["L0_attn_Ww"],
           "Wbeta": inp["L0_attn_Wbeta"], "Wg": inp["L0_attn_Wg"],
           "conv_w": inp["L0_attn_conv_w"], "qn_w": inp["L0_attn_qn_w"],
           "kn_w": inp["L0_attn_kn_w"]} for c in range(NCORE)]
    r = _run("l0ap", build_l0a_proj, im,
             ["q_fm", "k_fm", "w_fm", "v_tm", "w_tm", "beta_tm", "negG_hm"])

    cst = const_feed(["ident_bf", "ident2_bf", "maskbias_causal"])
    im = [dict(r[c], **cst) for c in range(NCORE)]
    r2 = _run("l0ai", build_l0a_inner, im, ["o_fm"])
    o_fm_l0 = [c["o_fm"] for c in r2]

    rc1, rs1 = _dec1_tables()
    cst = const_feed(["maskbias_causal"])
    im = [{"x_fm": xn_fm[c], "x_tm": xn_tm[c], "wq": inp["L0_ffn_wq1"],
           "wk": inp["L0_ffn_wk1"], "wv": inp["L0_ffn_wv1"],
           "wo": inp["L0_ffn_wo1"], "ln1": inp["L0_ffn_ln1_1"],
           "ln2": inp["L0_ffn_ln2_1"], "wg": inp["L0_ffn_wg1"],
           "wu": inp["L0_ffn_wu1"], "wd": inp["L0_ffn_wd1"],
           "ropec": rc1, "ropes": rs1, **cst} for c in range(NCORE)]
    r3 = _run("dec1", lambda: build_dec(T), im, ["out_tm"])
    proc = [c["out_tm"] for c in r3]

    im = [{"xn_tm": xn_tm[c], "xn_fm": xn_fm[c], "proc_tm": proc[c],
           "prior_w": inp["L0_ffn_prior_w"],
           "prior_norm_w": inp["L0_ffn_prior_norm_w"]} for c in range(NCORE)]
    r4 = _run("r1", build_router1, im, ["dd", "partial"])
    allpart = np.ascontiguousarray(
        np.concatenate([c["partial"] for c in r4], 0))

    cst = const_feed(["ident_f32", "iota512_row", "iota128_col",
                      "ropeb_cos_bf", "ropeb_sin_bf"])
    im = [{"dd": r4[c]["dd"], "allpart": allpart, "proc_tm": proc[c], **cst}
          for c in range(NCORE)]
    r5 = _run("r2", build_router2, im,
              ["ST", "Smat", "scores_sm", "ropec2", "ropes2", "sel_tm",
               "sel_fm", "phi", "plo"])

    cst = const_feed(["maskbias_causal"])
    im = [{"x_fm": r5[c]["sel_fm"], "x_tm": r5[c]["sel_tm"],
           "wq": inp["L0_ffn_wq2"], "wk": inp["L0_ffn_wk2"],
           "wv": inp["L0_ffn_wv2"], "wo": inp["L0_ffn_wo2"],
           "ln1": inp["L0_ffn_ln1_2"], "ln2": inp["L0_ffn_ln2_2"],
           "wg": inp["L0_ffn_wg2"], "wu": inp["L0_ffn_wu2"],
           "wd": inp["L0_ffn_wd2"], "ropec": r5[c]["ropec2"],
           "ropes": r5[c]["ropes2"], **cst} for c in range(NCORE)]
    r6 = _run("dec2", lambda: build_dec(512), im, ["out_tm"])

    im = [{"y_tm": r6[c]["out_tm"], "sel_tm": r5[c]["sel_tm"],
           "scores_sm": r5[c]["scores_sm"]} for c in range(NCORE)]
    r7 = _run("dl", build_delta, im, ["dhi", "dlo"])

    im = [{"x0": shards[c], "o_fm": o_fm_l0[c], "Wo": inp["L0_attn_Wo"],
           "proc_tm": proc[c], "Smat": r5[c]["Smat"], "dhi": r7[c]["dhi"],
           "dlo": r7[c]["dlo"]} for c in range(NCORE)]
    r8 = _run("cb", build_combine, im, ["x1_tm", "xn1_tm", "xn1_fm"])

    cst = const_feed(["maskbias_causal"])
    im = [{"xn1_fm": r8[c]["xn1_fm"], "w_qkv": inp["L1_attn_w_qkv"],
           "w_out": inp["L1_attn_w_out"], "b_out": inp["L1_attn_b_out"], **cst}
          for c in range(NCORE)]
    r9 = _run("l1a", build_l1attn, im, ["a1_tm"])

    cst = const_feed(["maskbias_causal", "maskbias_anticausal", "xpos_qc_bf",
                      "xpos_qs_bf", "xpos_kc_bf", "xpos_ks_bf"])
    wl1 = {"wqkv": inp["L1_ffn_wqkv"], "q_scale": inp["L1_ffn_q_scale"],
           "k_scale": inp["L1_ffn_k_scale"], "wo": inp["L1_ffn_wo"],
           "ln_g": inp["L1_ffn_ln_g"], "ln_b": inp["L1_ffn_ln_b"],
           "w1": inp["L1_ffn_w1"], "b1": inp["L1_ffn_b1"],
           "w2": inp["L1_ffn_w2"], "b2": inp["L1_ffn_b2"]}
    im = [{"x_tm": r8[c]["xn1_tm"], "x_fm": r8[c]["xn1_fm"], **wl1, **cst}
          for c in range(NCORE)]
    r10 = _run("l1f0", lambda: build_l1layer(0), im, ["out_tm", "out_fm"])

    im = [{"x_tm": r10[c]["out_tm"], "x_fm": r10[c]["out_fm"], **wl1, **cst,
           "x1_tm": r8[c]["x1_tm"], "a1_tm": r9[c]["a1_tm"]}
          for c in range(NCORE)]
    r11 = _run("l1f1", lambda: build_l1layer(1, final_stage=True), im,
               ["mod_out"])

    out = np.concatenate([r11[c]["mod_out"] for c in range(NCORE)], 0)
    return out.astype(np.float32)



def build_fused():
    p = Prog("mk")
    E = "ExternalInput"
    g = {}
    g["x"] = p.dram("L0_attn_x", [B, T, D], F32, kind=E)
    wnames = [
        ("L0_attn_Wq", [D, D]), ("L0_attn_Wk", [D, D]), ("L0_attn_Wv", [D, D]),
        ("L0_attn_Ww", [D, D]), ("L0_attn_Wbeta", [D, 8]), ("L0_attn_Wg", [D, 8]),
        ("L0_attn_Wo", [D, D]), ("L0_attn_conv_w", [D, 4]),
        ("L0_attn_qn_w", [128]), ("L0_attn_kn_w", [128]),
        ("L0_ffn_ln1_1", [D]), ("L0_ffn_wq1", [D, D]), ("L0_ffn_wk1", [D, D]),
        ("L0_ffn_wv1", [D, D]), ("L0_ffn_wo1", [D, D]), ("L0_ffn_ln2_1", [D]),
        ("L0_ffn_wg1", [D, 4096]), ("L0_ffn_wu1", [D, 4096]),
        ("L0_ffn_wd1", [4096, D]), ("L0_ffn_ln1_2", [D]), ("L0_ffn_wq2", [D, D]),
        ("L0_ffn_wk2", [D, D]), ("L0_ffn_wv2", [D, D]), ("L0_ffn_wo2", [D, D]),
        ("L0_ffn_ln2_2", [D]), ("L0_ffn_wg2", [D, 4096]),
        ("L0_ffn_wu2", [D, 4096]), ("L0_ffn_wd2", [4096, D]),
        ("L0_ffn_prior_norm_w", [D]), ("L0_ffn_prior_w", [D, 2 * D]),
        ("L1_attn_w_qkv", [D, 3 * D]), ("L1_attn_w_out", [D, D]),
        ("L1_attn_b_out", [D]),
        ("L1_ffn_wqkv", [2, D, 3 * L1I]), ("L1_ffn_q_scale", [2, DH]),
        ("L1_ffn_k_scale", [2, DH]), ("L1_ffn_wo", [2, L1I, D]),
        ("L1_ffn_ln_g", [2, D]), ("L1_ffn_ln_b", [2, D]),
        ("L1_ffn_w1", [2, D, 2 * L1HID]), ("L1_ffn_b1", [2, 2 * L1HID]),
        ("L1_ffn_w2", [2, L1HID, D]), ("L1_ffn_b2", [2, D]),
    ]
    for (nm, sh) in wnames:
        g[nm] = p.dram(nm, sh, F32, kind=E)
    CN = ["ident_bf", "ident2_bf", "maskbias_causal", "maskbias_anticausal",
          "ident_f32", "iota512_row", "iota128_col", "ropeb_cos_bf",
          "ropeb_sin_bf", "xpos_qc_bf", "xpos_qs_bf", "xpos_kc_bf",
          "xpos_ks_bf", "rope1c", "rope1s"]
    cap = {("c_" + n): const_ap(p, n) for n in CN}
    mod_out = p.dram("mod_out", [B, T, D], F32, kind="ExternalOutput")

    I = "Internal"
    xn_tm = p.dram("xn_tm", [B, T, D], F32, kind=I)
    xn_fm = p.dram("xn_fm", [B, NC, 128, T], BF16, kind=I)
    q_fm = p.dram("q_fm", [B, NH, 128, T], BF16, kind=I)
    k_fm = p.dram("k_fm", [B, NH, 128, T], BF16, kind=I)
    w_fm = p.dram("w_fm", [B, NH, 128, T], BF16, kind=I)
    v_tm = p.dram("v_tm", [B, T, D], BF16, kind=I)
    w_tm = p.dram("w_tm", [B, T, D], BF16, kind=I)
    beta_tm = p.dram("beta_tm", [B, T, 8], F32, kind=I)
    negG_hm = p.dram("negG_hm", [B, NH, T], F32, kind=I)
    o_fm0 = p.dram("o_fm0", [B, NH, 128, T], BF16, kind=I)
    proc_tm = p.dram("proc_tm", [B, T, D], F32, kind=I)
    dd = p.dram("dd", [B, 2, NT, 128], F32, kind=I)
    partial = p.dram("partial", [1, 2], F32, kind=I)
    allsum = p.dram("allsum", [1, 2], F32, kind=I)
    ST_d = p.dram("ST", [B, NT, 128, 512], BF16, kind=I)
    S_d = p.dram("Smat", [B, 4, 128, T], BF16, kind=I)
    scores_sm = p.dram("scores_sm", [B, 128, 4], F32, kind=I)
    ropec2 = p.dram("ropec2", [B, 512, DH], F32, kind=I)
    ropes2 = p.dram("ropes2", [B, 512, DH], F32, kind=I)
    sel_tm = p.dram("sel_tm", [B, 512, D], F32, kind=I)
    sel_fm = p.dram("sel_fm", [B, NC, 128, 512], BF16, kind=I)
    phi = p.dram("phi", [B, T, D], BF16, kind=I)
    plo = p.dram("plo", [B, T, D], BF16, kind=I)
    y_tm = p.dram("y_tm", [B, 512, D], F32, kind=I)
    dhi = p.dram("dhi", [B, 512, D], BF16, kind=I)
    dlo = p.dram("dlo", [B, 512, D], BF16, kind=I)
    x1_tm = p.dram("x1_tm", [B, T, D], F32, kind=I)
    xn1_tm = p.dram("xn1_tm", [B, T, D], F32, kind=I)
    xn1_fm = p.dram("xn1_fm", [B, NC, 128, T], BF16, kind=I)
    a1_tm = p.dram("a1_tm", [B, T, D], F32, kind=I)
    l1x_tm = p.dram("l1x_tm", [B, T, D], F32, kind=I)
    l1x_fm = p.dram("l1x_fm", [B, NC, 128, T], BF16, kind=I)

    p.constcol(1e-5)
    p.constcol(1e-6)
    p.constcol(1.0)

    def stage(fn):
        p.push_scope()
        fn()
        p.barrier()
        p.pop_scope()

    stage(lambda: emit_s0(p, g["x"], xn_tm, xn_fm))
    stage(lambda: emit_l0a_proj(p, {
        "xn_fm": xn_fm, "q_fm": q_fm, "k_fm": k_fm, "w_fm": w_fm,
        "v_tm": v_tm, "w_tm": w_tm, "beta_tm": beta_tm, "negG_hm": negG_hm,
        "Wq": g["L0_attn_Wq"], "Wk": g["L0_attn_Wk"], "Wv": g["L0_attn_Wv"],
        "Ww": g["L0_attn_Ww"], "Wbeta": g["L0_attn_Wbeta"],
        "Wg": g["L0_attn_Wg"], "conv_w": g["L0_attn_conv_w"],
        "qn_w": g["L0_attn_qn_w"], "kn_w": g["L0_attn_kn_w"]}))
    stage(lambda: emit_l0a_inner(p, {
        "q_fm": q_fm, "k_fm": k_fm, "w_fm": w_fm, "v_tm": v_tm, "w_tm": w_tm,
        "beta_tm": beta_tm, "negG_hm": negG_hm, "o_fm": o_fm0,
        "c_ident_bf": cap["c_ident_bf"], "c_ident2_bf": cap["c_ident2_bf"],
        "c_maskbias_causal": cap["c_maskbias_causal"]}))
    stage(lambda: emit_dec(p, {"c_maskbias_causal": cap["c_maskbias_causal"]},
                           "dc1", T, xn_fm, xn_tm, proc_tm,
                           g["L0_ffn_wq1"], g["L0_ffn_wk1"], g["L0_ffn_wv1"],
                           g["L0_ffn_wo1"], g["L0_ffn_ln1_1"], g["L0_ffn_ln2_1"],
                           g["L0_ffn_wg1"], g["L0_ffn_wu1"], g["L0_ffn_wd1"],
                           cap["c_rope1c"], cap["c_rope1s"]))
    stage(lambda: emit_router1(p, {
        "xn_tm": xn_tm, "xn_fm": xn_fm, "proc_tm": proc_tm,
        "prior_w": g["L0_ffn_prior_w"],
        "prior_norm_w": g["L0_ffn_prior_norm_w"], "dd": dd, "partial": partial}))

    def do_cc():
        ccs = p.sem("ccs")
        dt = [(s, s.n) for q in p.dpool for s in p.dpool[q] if s.n > 0]

        def fn(e):
            return e.collective_compute(
                "AllReduce", ALU.add, replica_groups=[list(range(NCORE))],
                ins=[partial.opt()], outs=[allsum.opt()])
        p._emit("gpsimd", fn, dt, [(ccs, 1)])
        p._cc_tok = (ccs, 1)
    stage(do_cc)
    stage(lambda: emit_router2(p, {
        "dd": dd, "allpart": allsum, "allpart_tok": [p._cc_tok],
        "proc_tm": proc_tm, "ST": ST_d, "Smat": S_d, "scores_sm": scores_sm,
        "ropec2": ropec2, "ropes2": ropes2, "sel_tm": sel_tm, "sel_fm": sel_fm,
        "phi": phi, "plo": plo, "c_ident_f32": cap["c_ident_f32"],
        "c_iota512_row": cap["c_iota512_row"],
        "c_iota128_col": cap["c_iota128_col"],
        "c_ropeb_cos_bf": cap["c_ropeb_cos_bf"],
        "c_ropeb_sin_bf": cap["c_ropeb_sin_bf"]}))
    stage(lambda: emit_dec(p, {"c_maskbias_causal": cap["c_maskbias_causal"]},
                           "dc2", 512, sel_fm, sel_tm, y_tm,
                           g["L0_ffn_wq2"], g["L0_ffn_wk2"], g["L0_ffn_wv2"],
                           g["L0_ffn_wo2"], g["L0_ffn_ln1_2"], g["L0_ffn_ln2_2"],
                           g["L0_ffn_wg2"], g["L0_ffn_wu2"], g["L0_ffn_wd2"],
                           ropec2, ropes2))
    stage(lambda: emit_delta(p, {"y_tm": y_tm, "sel_tm": sel_tm,
                                 "scores_sm": scores_sm, "dhi": dhi,
                                 "dlo": dlo}))
    stage(lambda: emit_combine(p, {
        "x0": g["x"], "o_fm": o_fm0, "Wo": g["L0_attn_Wo"], "proc_tm": proc_tm,
        "Smat": S_d, "dhi": dhi, "dlo": dlo, "x1_tm": x1_tm, "xn1_tm": xn1_tm,
        "xn1_fm": xn1_fm}))
    stage(lambda: emit_l1attn(p, {
        "xn1_fm": xn1_fm, "w_qkv": g["L1_attn_w_qkv"],
        "w_out": g["L1_attn_w_out"], "b_out": g["L1_attn_b_out"],
        "a1_tm": a1_tm, "c_maskbias_causal": cap["c_maskbias_causal"]}))
    l1aps = {"wqkv": g["L1_ffn_wqkv"], "q_scale": g["L1_ffn_q_scale"],
             "k_scale": g["L1_ffn_k_scale"], "wo": g["L1_ffn_wo"],
             "ln_g": g["L1_ffn_ln_g"], "ln_b": g["L1_ffn_ln_b"],
             "w1": g["L1_ffn_w1"], "b1": g["L1_ffn_b1"], "w2": g["L1_ffn_w2"],
             "b2": g["L1_ffn_b2"],
             "c_maskbias_causal": cap["c_maskbias_causal"],
             "c_maskbias_anticausal": cap["c_maskbias_anticausal"],
             "c_xpos_qc_bf": cap["c_xpos_qc_bf"],
             "c_xpos_qs_bf": cap["c_xpos_qs_bf"],
             "c_xpos_kc_bf": cap["c_xpos_kc_bf"],
             "c_xpos_ks_bf": cap["c_xpos_ks_bf"]}
    stage(lambda: emit_l1layer(p, l1aps, "l1f0", 0, xn1_tm, xn1_fm,
                               l1x_tm, l1x_fm))
    stage(lambda: emit_l1layer(p, l1aps, "l1f1", 1, l1x_tm, l1x_fm,
                               None, None, final=(x1_tm, a1_tm, mod_out)))
    return p


FUSED = True


def kernel_fused(**inputs):
    inp = {k: np.ascontiguousarray(np.asarray(v)) for k, v in inputs.items()}
    x_full = inp["L0_attn_x"].astype(np.float32, copy=False)
    CN = ["ident_bf", "ident2_bf", "maskbias_causal", "maskbias_anticausal",
          "ident_f32", "iota512_row", "iota128_col", "ropeb_cos_bf",
          "ropeb_sin_bf", "xpos_qc_bf", "xpos_qs_bf", "xpos_kc_bf",
          "xpos_ks_bf", "rope1c", "rope1s"]
    cst = const_feed(CN)
    base = {k: inp[k] for k in inp if k != "L0_attn_x"}
    base.update(cst)
    nc = _prog("fused", build_fused)
    ims = [dict(base, L0_attn_x=np.ascontiguousarray(x_full[2 * c:2 * c + 2]))
           for c in range(NCORE)]
    res = run_bass_kernel_spmd(nc, ims, core_ids=list(range(NCORE)))
    out = np.concatenate([res.results[c]["mod_out"] for c in range(NCORE)], 0)
    return out.astype(np.float32)


# version 07
# speedup vs baseline: 1.0263x; 1.0263x over previous
import contextlib
import math
import numpy as np
import ml_dtypes

import concourse.bass as bass
import concourse.mybir as mybir
from concourse.bass_utils import run_bass_kernel_spmd

F32 = mybir.dt.float32
BF16 = mybir.dt.bfloat16
AF = mybir.ActivationFunctionType
ALU = mybir.AluOpType
AX = mybir.AxisListType
BF = ml_dtypes.bfloat16

NCORE = 8
B = 2
T = 2048
D = 1024
NT = T // 128
NC = D // 128

ENGS = ("tensor", "vector", "scalar", "gpsimd", "sync")


class Sem:
    __slots__ = ("h", "n", "name")

    def __init__(self, h, name):
        self.h = h
        self.n = 0
        self.name = name


class Prog:

    def __init__(self, name="mk"):
        self.nc = bass.Bass("TRN2", target_bir_lowering=False)
        self.es = contextlib.ExitStack()
        self.q = {e: [] for e in ENGS}
        self.pipe = {e: self.sem("pipe_" + e)
                     for e in ("vector", "scalar", "gpsimd", "tensor")}
        self.DPOOL = 10
        self.dpool = {e: [self.sem(f"dma_{e}{i}") for i in range(self.DPOOL)]
                      for e in ("sync", "scalar", "gpsimd")}
        self.didx = {e: 0 for e in ("sync", "scalar", "gpsimd")}
        self.dlast = {e: [None] * self.DPOOL for e in ("sync", "scalar", "gpsimd")}
        self.waited = {e: {} for e in ENGS}
        self._names = set()
        self.init_toks = []

    def sem(self, name):
        h = self.es.enter_context(self.nc.semaphore(name))
        return Sem(h, name)

    def push_scope(self):
        self._scopes = getattr(self, "_scopes", [])
        self._scopes.append(contextlib.ExitStack())

    def pop_scope(self):
        self._scopes.pop().close()

    def _alloc_stack(self):
        scopes = getattr(self, "_scopes", None)
        return scopes[-1] if scopes else self.es

    def _nm(self, name):
        base, i = name, 0
        while name in self._names:
            i += 1
            name = f"{base}_{i}"
        self._names.add(name)
        return name

    def sb(self, name, shape, dtype):
        st_ = self._alloc_stack()
        if st_ is self.es:
            return self.nc.alloc_sbuf_tensor(self._nm(name), shape, dtype)
        return st_.enter_context(
            self.nc.sbuf_tensor(self._nm(name), shape, dtype))

    def ps(self, name, shape, dtype=F32):
        st_ = self._alloc_stack()
        if st_ is self.es:
            return self.nc.alloc_psum_tensor(self._nm(name), shape, dtype)
        return st_.enter_context(
            self.nc.psum_tensor(self._nm(name), shape, dtype))

    def dram(self, name, shape, dtype, kind="Internal"):
        if kind == "Internal":
            name = self._nm("dr_" + name)
        return self.nc.dram_tensor(name, shape, dtype, kind=kind).ap()

    def constcol(self, value):
        if not hasattr(self, "_ccol"):
            self._ccol = {}
        if value not in self._ccol:
            t = self.nc.alloc_sbuf_tensor(self._nm(f"ccol_{len(self._ccol)}"),
                                          [128, 1], F32)
            tok = self.v(lambda v, t=t, value=value: v.memset(t[:], value))
            self.init_toks.append(tok)
            self._ccol[value] = t[:, 0:1]
        return self._ccol[value]

    def _emit(self, eng, fn, after, incs):
        waits = []
        wl = self.waited[eng]
        for tok in list(self.init_toks) + list(after):
            if tok is None:
                continue
            s, v = tok
            if v > 0 and wl.get(s.name, -1) < v:
                waits.append((s, v))
                wl[s.name] = v

        def thunk(e, waits=waits, incs=incs, fn=fn):
            for (s, v) in waits:
                e.wait_ge(s.h, v)
            inst = fn(e)
            for (s, k) in incs:
                inst.then_inc(s.h, k)

        self.q[eng].append(thunk)

    def _compute(self, eng, fn, after=(), sync=True):
        incs = []
        if sync:
            p = self.pipe[eng]
            p.n += 1
            incs.append((p, 1))
        self._emit(eng, fn, after, incs)
        return (self.pipe[eng], self.pipe[eng].n) if sync else None

    def v(self, fn, after=(), sync=True):
        return self._compute("vector", fn, after, sync)

    def s(self, fn, after=(), sync=True):
        return self._compute("scalar", fn, after, sync)

    def g(self, fn, after=(), sync=True):
        return self._compute("gpsimd", fn, after, sync)

    def t(self, fn, after=(), sync=True):
        return self._compute("tensor", fn, after, sync)

    def dma(self, queue, out, in_, after=(), transpose=False):
        i = self.didx[queue] % self.DPOOL
        self.didx[queue] += 1
        ds = self.dpool[queue][i]
        prev = self.dlast[queue][i]
        ds.n += 16

        def fn(e, out=out, in_=in_, transpose=transpose):
            return e.dma_start(out=out, in_=in_, transpose=transpose)

        self._emit(queue, fn, list(after) + ([prev] if prev else []), [(ds, 16)])
        tok = (ds, ds.n)
        self.dlast[queue][i] = tok
        return tok

    def wait(self, eng, after):
        self._emit(eng, lambda e: e.nop(), after, [])

    def barrier(self):
        if not hasattr(self, "_barsem"):
            self._barsem = self.sem("bar")
        bs = self._barsem
        dtoks = []
        for q in self.dpool:
            dtoks += [(s, s.n) for s in self.dpool[q] if s.n > 0]
        for e in ("tensor", "vector", "scalar", "gpsimd", "sync"):
            self._emit(e, lambda en: en.nop(), dtoks, [(bs, 1)])
            bs.n += 1
        for e in ("tensor", "vector", "scalar", "gpsimd", "sync"):
            self.wait(e, [(bs, bs.n)])

    def finalize(self):
        with self.nc.Block() as block:
            for eng in ENGS:
                if self.q[eng]:
                    def body(e, eng=eng):
                        for t in self.q[eng]:
                            t(e)
                    getattr(block, eng)(body)
        self.es.close()
        return self.nc


class Ring:

    def __init__(self, p, name, shape, dtype, n=2):
        self.bufs = [p.sb(f"{name}{i}", shape, dtype) for i in range(n)]
        self.guards = [[] for _ in range(n)]
        self.i = -1

    def next(self):
        self.i = (self.i + 1) % len(self.bufs)
        return self.bufs[self.i]

    @property
    def guard(self):
        return self.guards[self.i]

    def set_guard(self, toks):
        self.guards[self.i] = list(toks)



def host_consts():
    c = {}
    c["ones1_f32"] = np.ones((1, 128), dtype=np.float32)
    c["ones1_bf"] = np.ones((1, 128), dtype=np.float32).astype(BF)
    iu = np.triu(np.ones((128, 128), dtype=np.float32), 1)
    c["maskbias_causal"] = (-1e30 * iu).astype(np.float32)
    c["ident2_bf"] = (2.0 * np.eye(128, dtype=np.float32)).astype(BF)
    c["ident_bf"] = np.eye(128, dtype=np.float32).astype(BF)
    half = 32
    inv = 1.0 / (10000.0 ** (np.arange(half, dtype=np.float32) / half))
    ang = np.arange(T, dtype=np.float32)[:, None] * inv[None, :]
    cos1, sin1 = np.cos(ang), np.sin(ang)
    c["rope0_cos"] = np.tile(np.concatenate([cos1, cos1], 1), (1, 16)).astype(np.float32)
    c["rope0_sin"] = np.tile(np.concatenate([-sin1, sin1], 1), (1, 16)).astype(np.float32)
    c["ident_f32"] = np.eye(128, dtype=np.float32)
    c["iota512_row"] = np.arange(512, dtype=np.float32)[None, :]
    c["iota128_col"] = np.arange(128, dtype=np.float32)[:, None]
    c["ropeb_cos"] = np.concatenate([cos1, cos1], 1).astype(np.float32)
    c["ropeb_sin"] = np.concatenate([-sin1, sin1], 1).astype(np.float32)
    c["ropeb_cos_bf"] = c["ropeb_cos"].astype(BF)
    c["ropeb_sin_bf"] = c["ropeb_sin"].astype(BF)
    c["maskbias_anticausal"] = (-1e30 * np.tril(
        np.ones((128, 128), dtype=np.float32), -1)).astype(np.float32)
    half_idx = np.arange(0, 64, 2, dtype=np.float32)
    inv_freq = 1.0 / 10000.0 ** (half_idx / 64)
    tpos = np.arange(T, dtype=np.float32)
    freqs = tpos[:, None] * inv_freq[None, :]
    base = (half_idx + 0.4 * 64) / (1.4 * 64)
    power = (tpos - T // 2) / 256.0
    xsc = base[None, :] ** power[:, None]
    cosf, sinf = np.cos(freqs), np.sin(freqs)
    c["xpos_qc"] = np.concatenate([cosf * xsc, cosf * xsc], 1).astype(np.float32)
    c["xpos_qs"] = np.concatenate([-sinf * xsc, sinf * xsc], 1).astype(np.float32)
    c["xpos_kc"] = np.concatenate([cosf / xsc, cosf / xsc], 1).astype(np.float32)
    c["xpos_ks"] = np.concatenate([-sinf / xsc, sinf / xsc], 1).astype(np.float32)
    for k2 in ("xpos_qc", "xpos_qs", "xpos_kc", "xpos_ks"):
        c[k2 + "_bf"] = c[k2].astype(BF)
    c["rope1c"] = np.ascontiguousarray(
        np.broadcast_to(c["ropeb_cos"][None], (B, T, 64))).astype(np.float32)
    c["rope1s"] = np.ascontiguousarray(
        np.broadcast_to(c["ropeb_sin"][None], (B, T, 64))).astype(np.float32)
    return c


_CONSTS = host_consts()


def const_ap(p, name):
    arr = _CONSTS[name]
    dt = BF16 if arr.dtype == BF else F32
    return p.dram("c_" + name, list(arr.shape), dt, kind="ExternalInput")


def const_feed(names):
    return {"c_" + n: np.ascontiguousarray(_CONSTS[n]) for n in names}




def ln_tile(p, xs, stb, sqb, after, kind="layernorm", eps=1e-5, nfeat=D):
    if kind == "layernorm":
        t1 = p.v(lambda v: v.reduce_sum(stb[:, 0:1], xs[:], axis=AX.X, negate=True),
                 after=after)
        t2 = p.v(lambda v: v.tensor_scalar_mul(stb[:, 1:2], stb[:, 0:1], 1.0 / nfeat),
                 after=[t1])
        t3 = p.v(lambda v: v.tensor_scalar(
            out=xs[:], in0=xs[:], scalar1=stb[:, 1:2], scalar2=None, op0=ALU.add),
            after=[t2])
        base = [t3]
    else:
        base = list(after)
    cc = p.constcol(eps)
    t5 = p.s(lambda s: s.activation(sqb[:], xs[:], AF.Square), after=base)
    t6 = p.v(lambda v: v.reduce_sum(stb[:, 2:3], sqb[:], axis=AX.X), after=[t5])
    t7 = p.s(lambda s, cc=cc: s.activation(stb[:, 3:4], stb[:, 2:3], AF.Sqrt,
                                           scale=1.0 / nfeat, bias=cc), after=[t6])
    t8 = p.v(lambda v: v.reciprocal(stb[:, 3:4], stb[:, 3:4]), after=[t7])
    t9 = p.v(lambda v: v.tensor_scalar(
        out=xs[:], in0=xs[:], scalar1=stb[:, 3:4], scalar2=None, op0=ALU.mult),
        after=[t8])
    return t9, stb[:, 3:4]


def store_fm(p, xnb, dst_fm_b, ti, after, tmp, nch=NC):
    tt = p.dma("sync",
               tmp[:, :nch * 128].rearrange("p (c t) -> p c t", c=nch),
               xnb[:].rearrange("p (c t) -> p c t", c=nch),
               after=after, transpose=True)
    o2 = p.dma("sync",
               dst_fm_b[:, :, ti * 128:(ti + 1) * 128].rearrange("c d t -> d c t"),
               tmp[:, :nch * 128].rearrange("d (c t) -> d c t", c=nch),
               after=[tt])
    return o2


class FMWeight:

    def __init__(self, p, tag, w_dram, din, e, gamma=None, ring=None, slots=None):
        self.chunks = []
        self.tok = []
        nch = din // 128
        f32b = ring or Ring(p, tag + "_wf32", [128, e], F32, n=2)
        guard = slots[1]["toks"] if slots else []
        for c in range(nch):
            wb = (slots[0][c][:, 0:e] if slots
                  else p.sb(f"{tag}_w{c}", [128, e], BF16)[:, 0:e])
            dst = f32b.next()
            ld = p.dma("scalar", dst[:, 0:e], w_dram[c * 128:(c + 1) * 128, :],
                       after=f32b.guard)
            src = dst[:, 0:e]
            if gamma is not None:
                tk = p.v(lambda v, wb=wb, src=src, c=c: v.tensor_scalar(
                    out=wb, in0=src, scalar1=gamma[:, c:c + 1], scalar2=None,
                    op0=ALU.mult), after=[ld] + guard)
            else:
                tk = p.v(lambda v, wb=wb, src=src: v.tensor_copy(wb, src),
                         after=[ld] + guard)
            f32b.set_guard([tk])
            self.chunks.append(wb)
            self.tok.append(tk)


def load_vec_fm(p, tag, vec_dram, n):
    vb = p.sb(tag, [128, n // 128], F32)
    tok = None
    for c in range(n // 128):
        tok = p.dma("sync", vb[:, c:c + 1],
                    vec_dram[c * 128:(c + 1) * 128].unsqueeze(1),
                    after=[tok] if tok else ())
    return vb, tok



def emit_s0(p, x, xn_tm, xn_fm):
    st = Ring(p, "s0_st", [128, 4], F32, n=3)
    sq = Ring(p, "s0_sq", [128, D], F32, n=2)
    xr = Ring(p, "s0_x", [128, D], F32, n=3)
    cb = Ring(p, "s0_cb", [128, D], BF16, n=2)
    tpb = Ring(p, "s0_tp", [128, D], BF16, n=2)
    for b in range(B):
        for i in range(NT):
            xs = xr.next()
            ld = p.dma("sync", xs[:], x[b, i * 128:(i + 1) * 128, :], after=xr.guard)
            stb = st.next()
            sqb = sq.next()
            p.wait("vector", st.guard + sq.guard)
            tok, _ = ln_tile(p, xs, stb, sqb, after=[ld])
            st.set_guard([tok])
            sq.set_guard([tok])
            o1 = p.dma("gpsimd", xn_tm[b, i * 128:(i + 1) * 128, :], xs[:], after=[tok])
            xnb = cb.next()
            tcast = p.s(lambda s, xnb=xnb, xs=xs: s.activation(
                xnb[:], xs[:], AF.Copy), after=[tok] + cb.guard)
            tmp = tpb.next()
            p.wait("sync", tpb.guard)
            o2 = store_fm(p, xnb, xn_fm[b], i, [tcast], tmp)
            cb.set_guard([o2])
            tpb.set_guard([o2])
            xr.set_guard([o1, tcast])



def finish(p):
    toks = []
    for q in p.dpool:
        toks += [(s, s.n) for s in p.dpool[q] if s.n > 0]
    for e in ENGS:
        p.wait(e, toks)


def run_stage(build_fn, in_maps, core_ids=None, **kw):
    p, names = build_fn()
    finish(p)
    nc = p.finalize()
    res = run_bass_kernel_spmd(nc, in_maps, core_ids=core_ids or list(range(len(in_maps))), **kw)
    return res


HD = 128
NH = 8
SC0 = HD ** -0.5


def _load_xnfm(p, tag, xn_fm_b):
    xsb = p.sb(tag, [128, NC * T], BF16)
    tok = p.dma("sync", xsb[:].rearrange("p (c t) -> p c t", c=NC),
                xn_fm_b.rearrange("c p t -> p c t"))
    return xsb, tok


def _vec_row(p, tag, vec_dram, reps, n):
    row = p.sb(tag, [1, reps * n], F32)
    toks = [p.dma("sync", row[0:1, r * n:(r + 1) * n], vec_dram.unsqueeze(0))
            for r in range(reps)]
    return row, toks


def _bcast_tile(p, tag, row_ap, width, after, ones_f32, psum, scale_row=None):
    out = p.sb(tag, [128, width], F32)
    toks = []
    for s0 in range(0, width, 512):
        s1 = min(s0 + 512, width)
        tm = p.t(lambda t, s0=s0, s1=s1: t.matmul(
            psum[:, s0:s1], ones_f32[:, :], row_ap[:, s0:s1],
            start=True, stop=True), after=after)
        toks.append(p.s(lambda s, s0=s0, s1=s1: s.activation(
            out[:, s0:s1], psum[:, s0:s1], AF.Copy), after=[tm]))
    return out, toks


def emit_l0a_proj(p, aps):
    xn_fm, q_fm, k_fm, w_fm = aps["xn_fm"], aps["q_fm"], aps["k_fm"], aps["w_fm"]
    v_tm, w_tm, beta_tm, negG_hm = aps["v_tm"], aps["w_tm"], aps["beta_tm"], aps["negG_hm"]
    ones1_f32 = p.sb("ones1f", [1, 128], F32)
    onescol_f32 = p.sb("onescolf", [128, 1], F32)
    t_ones = p.v(lambda v: v.memset(ones1_f32[:], 1.0))
    t_ones2 = p.v(lambda v: v.memset(onescol_f32[:], 1.0))
    psA = p.ps("l0p_psA", [128, 2048])
    psB = p.ps("l0p_psB", [128, 2048])

    wring = Ring(p, "l0p_wld", [128, 1024], F32, n=2)
    sring = Ring(p, "l0p_sld", [128, 8], F32, n=2)
    wslots = ([p.sb(f"l0p_wslot{i}", [128, 1024], BF16) for i in range(NC)],
              {"toks": []})
    Wbeta = FMWeight(p, "Wbeta", aps["Wbeta"], D, 8, ring=sring)
    Wg = FMWeight(p, "Wg", aps["Wg"], D, 8, ring=sring)

    qn_row, t1 = _vec_row(p, "qn_row", aps["qn_w"], NH, HD)
    kn_row, t2 = _vec_row(p, "kn_row", aps["kn_w"], NH, HD)
    sc_row = p.sb("sc_row", [1, 128], F32)
    t3 = p.v(lambda v: v.memset(sc_row[:], SC0))
    qn_bc, tq = _bcast_tile(p, "qn_bc", qn_row, D, t1 + [t3], sc_row, psA)
    kn_bc, tk = _bcast_tile(p, "kn_bc", kn_row, D, tq + t2, ones1_f32, psA)
    cw = p.sb("cw", [128, NC * 4], F32)
    t_cw = p.dma("sync", cw[:].rearrange("p (c k) -> p c k", c=NC),
                 aps["conv_w"].rearrange("(c p) k -> p c k", p=128))

    S = [p.sb(f"l0p_S{i}", [128, T], F32) for i in range(3)]
    wpad = p.sb("l0p_wpad", [128, T + 4], F32)
    srow = p.sb("l0p_srow", [1, T], F32)
    wcb = p.sb("l0p_wcb", [128, T], BF16)
    wT = p.sb("l0p_wT", [128, T], BF16)
    st = Ring(p, "l0p_st", [128, 10], F32, n=3)
    qt = Ring(p, "l0p_qt", [128, 1024], F32, n=2)
    sqb = Ring(p, "l0p_sq", [128, 1024], F32, n=2)
    bb = Ring(p, "l0p_bf", [128, 1024], BF16, n=3)
    tpb = Ring(p, "l0p_tp", [128, 1024], BF16, n=2)
    xsb = p.sb("l0p_xn", [128, NC * T], BF16)

    wg_s = []
    prev_xload = []
    for b in range(B):
        t_x = p.dma("sync", xsb[:].rearrange("p (c t) -> p c t", c=NC),
                    xn_fm[b].rearrange("c p t -> p c t"), after=prev_xload)

        def lhs(c, ti, xsb=xsb):
            return xsb[:, c * T + ti * 128:c * T + ti * 128 + 128]

        batch_toks = []

        Wv = FMWeight(p, "Wv", aps["Wv"], D, D, ring=wring, slots=wslots)
        for ti in range(NT):
            for half in range(2):
                for c in range(NC):
                    mm = p.t(lambda t, la=lhs(c, ti), c=c, half=half, Wv=Wv: t.matmul(
                        psA[:, half * 512:(half + 1) * 512], la,
                        Wv.chunks[c][:, half * 512:(half + 1) * 512],
                        start=(c == 0), stop=(c == NC - 1)),
                        after=[t_x] + (Wv.tok if ti == 0 else []),
                        sync=(c == NC - 1))
            mmv = mm
            for c in range(NC):
                mmb = p.t(lambda t, la=lhs(c, ti), c=c: t.matmul(
                    psB[:, 0:8], la, Wbeta.chunks[c][:, 0:8],
                    start=(c == 0), stop=(c == NC - 1)),
                    after=(Wbeta.tok if ti == 0 else []), sync=(c == NC - 1))
            vb = bb.next()
            ev = p.s(lambda s, vb=vb: s.activation(vb[:], psA[:, 0:1024], AF.Copy),
                     after=[mmv] + bb.guard)
            so = p.dma("gpsimd", v_tm[b, ti * 128:(ti + 1) * 128, :], vb[:],
                       after=[ev])
            bb.set_guard([so])
            stb = st.next()
            e1 = p.s(lambda s, stb=stb: s.activation(stb[:, 0:8], psB[:, 0:8],
                                                     AF.Sigmoid),
                     after=[mmb] + st.guard)
            e2 = p.v(lambda v, stb=stb: v.tensor_scalar_mul(stb[:, 0:8],
                                                            stb[:, 0:8], 2.0),
                     after=[e1])
            so2 = p.dma("gpsimd", beta_tm[b, ti * 128:(ti + 1) * 128, :],
                        stb[:, 0:8], after=[e2])
            st.set_guard([so2])
            p.wait("tensor", [ev, e1])
            batch_toks += [so, so2]
            wslots[1]["toks"] = [mmv]

        for (name, Wdr, bcast, bctok, dst_fm) in (
                ("q", aps["Wq"], qn_bc, tq, q_fm), ("k", aps["Wk"], kn_bc, tk, k_fm)):
            W = FMWeight(p, name + "W", Wdr, D, D, ring=wring, slots=wslots)
            for ti in range(NT):
                mms = []
                for half in range(2):
                    for c in range(NC):
                        mm = p.t(lambda t, la=lhs(c, ti), c=c, half=half, W=W: t.matmul(
                            psA[:, half * 512:(half + 1) * 512], la,
                            W.chunks[c][:, half * 512:(half + 1) * 512],
                            start=(c == 0), stop=(c == NC - 1)),
                            after=[t_x] + (W.tok if ti == 0 else []),
                            sync=(c == NC - 1))
                    mms.append(mm)
                qtb = qt.next()
                ev = p.s(lambda s, qtb=qtb: s.activation(qtb[:], psA[:, 0:1024], AF.Copy),
                         after=mms + qt.guard)
                p.wait("tensor", [ev])
                sb_ = sqb.next()
                s1 = p.s(lambda s, qtb=qtb, sb_=sb_: s.activation(
                    sb_[:], qtb[:], AF.Square), after=[ev] + sqb.guard)
                stb = st.next()
                r1 = p.v(lambda v, sb_=sb_, stb=stb: v.reduce_sum(
                    stb[:, 0:8], sb_[:].rearrange("p (h d) -> p h d", h=NH),
                    axis=AX.X), after=[s1] + st.guard)
                sqb.set_guard([r1])
                cc6 = p.constcol(1e-6)
                r2 = p.s(lambda s, stb=stb, cc6=cc6: s.activation(
                    stb[:, 0:8], stb[:, 0:8], AF.Sqrt, scale=1.0 / HD, bias=cc6),
                    after=[r1])
                r3 = p.v(lambda v, stb=stb: v.reciprocal(stb[:, 0:8], stb[:, 0:8]),
                         after=[r2])
                a1 = p.v(lambda v, qtb=qtb, stb=stb: v.tensor_mul(
                    qtb[:].rearrange("p (h d) -> p h d", h=NH),
                    qtb[:].rearrange("p (h d) -> p h d", h=NH),
                    stb[:, 0:8].unsqueeze(2).to_broadcast([128, NH, HD])),
                    after=[r3])
                st.set_guard([a1])
                qb2 = bb.next()
                a2 = p.v(lambda v, qtb=qtb, qb2=qb2, bcast=bcast: v.tensor_mul(
                    qb2[:], qtb[:], bcast[:]), after=[a1] + bctok + bb.guard)
                qt.set_guard([a2])
                tmp = tpb.next()
                p.wait("sync", tpb.guard)
                o2 = store_fm(p, qb2, dst_fm[b], ti, [a2], tmp, nch=NC)
                bb.set_guard([o2])
                tpb.set_guard([o2])
                batch_toks.append(o2)
                wslots[1]["toks"] = mms

        Ww = FMWeight(p, "Ww", aps["Ww"], D, D, ring=wring, slots=wslots)
        z3 = p.v(lambda v: v.memset(wpad[:, 0:3], 0.0), after=wg_s)
        prev_c = list(wg_s)
        for c in range(NC):
            mms = []
            for s0 in range(0, T, 512):
                for dc in range(NC):
                    mm = p.t(lambda t, c=c, s0=s0, dc=dc, xsb=xsb, Ww=Ww: t.matmul(
                        psB[:, s0:s0 + 512],
                        Ww.chunks[dc][:, c * 128:(c + 1) * 128],
                        xsb[:, dc * T + s0:dc * T + s0 + 512],
                        start=(dc == 0), stop=(dc == NC - 1)),
                        after=[t_x] + (Ww.tok if c == 0 and s0 == 0 else []) + prev_c,
                        sync=(dc == NC - 1))
                mms.append(mm)
            ev = p.s(lambda s: s.activation(wpad[:, 3:3 + T], psB[:, 0:T], AF.Copy),
                     after=mms + [z3])
            a0 = p.v(lambda v, c=c: v.tensor_scalar(
                out=S[0][:], in0=wpad[:, 0:T], scalar1=cw[:, c * 4:c * 4 + 1],
                scalar2=None, op0=ALU.mult), after=[ev, t_cw])
            cur, nxt = S[0], S[1]
            tok = a0
            for i in range(1, 4):
                tok = p.v(lambda v, i=i, c=c, cur=cur, nxt=nxt: v.scalar_tensor_tensor(
                    out=nxt[:], in0=wpad[:, i:i + T],
                    scalar=cw[:, c * 4 + i:c * 4 + i + 1], in1=cur[:],
                    op0=ALU.mult, op1=ALU.add), after=[tok])
                cur, nxt = nxt, cur
            sil = p.s(lambda s, cur=cur: s.activation(cur[:], cur[:], AF.Silu),
                      after=[tok])
            sq = p.s(lambda s, cur=cur: s.activation(S[2][:], cur[:], AF.Square),
                     after=[sil])
            rmm = []
            for s0 in range(0, T, 512):
                rm = p.t(lambda t, s0=s0: t.matmul(
                    psA[0:1, s0:s0 + 512], onescol_f32[:], S[2][:, s0:s0 + 512],
                    start=True, stop=True), after=[sq, t_ones2, ev])
                rmm.append(rm)
            cc6b = p.constcol(1e-6)
            sr = p.s(lambda s, cc6b=cc6b: s.activation(
                srow[0:1, 0:T], psA[0:1, 0:T], AF.Sqrt,
                bias=cc6b[0:1, :]), after=rmm)
            rr = p.v(lambda v: v.reciprocal(srow[0:1, 0:T], srow[0:1, 0:T]),
                     after=[sr])
            bmm = []
            for s0 in range(0, T, 512):
                bm = p.t(lambda t, s0=s0: t.matmul(
                    psA[:, s0:s0 + 512], ones1_f32[:], srow[0:1, s0:s0 + 512],
                    start=True, stop=True), after=[rr, t_ones])
                bmm.append(bm)
            wn = p.v(lambda v, cur=cur: v.tensor_mul(wcb[:], cur[:], psA[:, 0:T]),
                     after=bmm + [sil])
            of = p.dma("sync", w_fm[b][c], wcb[:], after=[wn])
            tt = p.dma("sync", wT[:].rearrange("p (i t) -> p i t", i=NT),
                       wcb[:].rearrange("p (i t) -> p i t", i=NT),
                       after=[wn], transpose=True)
            ot = p.dma("sync",
                       w_tm[b][:, c * 128:(c + 1) * 128].rearrange(
                           "(i p) d -> p i d", p=128),
                       wT[:].rearrange("p (i d) -> p i d", i=NT), after=[tt])
            prev_c = [of, ot, wn]
            batch_toks += [of, ot]
            wslots[1]["toks"] = mms

        zb, gb, g1 = S[0], S[1], S[2]
        gones = p.sb("l0p_gones", [8, T], F32)
        t_go = p.v(lambda v, gones=gones: v.memset(gones[:], 1.0))
        mms = []
        for s0 in range(0, T, 512):
            for dc in range(NC):
                mm = p.t(lambda t, s0=s0, dc=dc, xsb=xsb: t.matmul(
                    psB[0:8, s0:s0 + 512], Wg.chunks[dc][:, 0:8],
                    xsb[:, dc * T + s0:dc * T + s0 + 512],
                    start=(dc == 0), stop=(dc == NC - 1)),
                    after=[t_x] + (Wg.tok if s0 == 0 else []) + prev_c,
                    sync=(dc == NC - 1))
            mms.append(mm)
        ge = p.s(lambda s, zb=zb: s.activation(zb[0:8, 0:T], psB[0:8, 0:T], AF.Copy),
                 after=mms + prev_c)
        gtmp = p.sb("l0p_gtmp", [8, T], F32) if b == 0 else gtmp
        ga = p.s(lambda s, gtmp=gtmp, zb=zb: s.activation(
            gtmp[0:8, 0:T], zb[0:8, 0:T], AF.Abs), after=[ge] + wg_s)
        gx = p.s(lambda s, gb=gb, gtmp=gtmp: s.activation(
            gb[0:8, 0:T], gtmp[0:8, 0:T], AF.Exp, scale=-1.0), after=[ga])
        cc1 = p.constcol(1.0)
        gl = p.s(lambda s, gtmp=gtmp, gb=gb, cc1=cc1: s.activation(
            gtmp[0:8, 0:T], gb[0:8, 0:T], AF.Ln,
            bias=cc1[0:8, :]), after=[gx])
        gm = p.v(lambda v, g1=g1, zb=zb: v.tensor_scalar_min(g1[0:8, 0:T],
                                                             zb[0:8, 0:T], 0.0),
                 after=[ge])
        gn = p.v(lambda v, gb=gb, gtmp=gtmp, g1=g1: v.tensor_sub(
            gb[0:8, 0:T], gtmp[0:8, 0:T], g1[0:8, 0:T]), after=[gl, gm])
        gs = p.v(lambda v, g1=g1, gones=gones, gb=gb: v.tensor_tensor_scan(
            out=g1[0:8, 0:T], data0=gones[0:8, 0:T],
            data1=gb[0:8, 0:T], initial=0.0,
            op0=ALU.mult, op1=ALU.add), after=[gn, t_go])
        sg = p.dma("gpsimd", negG_hm[b], g1[0:8, 0:T], after=[gs])
        wg_s = [sg, gs, gn]
        prev_xload = batch_toks + [sg]




DEBUG_A2_BARRIER = False


def emit_l0a_inner(p, aps, pairs=None):
    q_fm, k_fm, w_fm = aps["q_fm"], aps["k_fm"], aps["w_fm"]
    v_tm, w_tm, beta_tm, negG_hm = aps["v_tm"], aps["w_tm"], aps["beta_tm"], aps["negG_hm"]
    o_fm = aps["o_fm"]
    identrep = p.sb("identrep", [128, T], BF16)
    ident2rep = p.sb("ident2rep", [128, T], BF16)
    maskb = p.sb("maskb", [128, 128], F32)
    ones1f = p.sb("a2_ones1f", [1, 128], F32)
    c1 = p.dma("sync", identrep[:].rearrange("p (i t) -> p i t", i=NT),
               aps["c_ident_bf"].unsqueeze(0).to_broadcast([NT, 128, 128]).rearrange(
                   "i p t -> p i t"))
    c2 = p.dma("sync", ident2rep[:].rearrange("p (i t) -> p i t", i=NT),
               aps["c_ident2_bf"].unsqueeze(0).to_broadcast([NT, 128, 128]).rearrange(
                   "i p t -> p i t"))
    c3 = p.dma("sync", maskb[:], aps["c_maskbias_causal"])
    c4 = p.v(lambda v: v.memset(ones1f[:], 1.0))
    consts = [c1, c2, c3, c4]

    psA = p.ps("a2_psA", [128, 2048])
    psB = p.ps("a2_psB", [128, 2048])

    qf = p.sb("a2_qf", [128, T], BF16)
    kf = p.sb("a2_kf", [128, T], BF16)
    wf = p.sb("a2_wf", [128, T], BF16)
    wtm = p.sb("a2_wtm", [128, T], BF16)
    vsb = [p.sb(f"a2_vsb{b}", [128, NT * D], BF16) for b in range(B)]
    betaf = p.sb("a2_betaf", [128, NT * 8], F32)
    negGr = p.sb("a2_negG", [1, T], F32)
    MT = p.sb("a2_MT", [128, T], BF16)
    Xb = p.sb("a2_X", [128, T], BF16)
    Zb = p.sb("a2_Z", [128, T], BF16)
    Rb = p.sb("a2_R", [128, T], BF16)
    Ef = p.sb("a2_Ef", [128, T], F32)
    Eb = p.sb("a2_Eb", [128, T], BF16)
    Tbf = p.sb("a2_Tbf", [128, T], BF16)
    Vsb = p.sb("a2_Vsb", [128, T], BF16)
    Eexp = p.sb("a2_Eexp", [128, T], BF16)
    Fsb = p.sb("a2_F", [128, T], BF16)
    negGm = p.sb("a2_negGm", [128, 128], BF16)
    ocoll = p.sb("a2_ocoll", [128, T], BF16)
    ofmb = p.sb("a2_ofmb", [128, T], BF16)
    stt_ = p.sb("a2_stats", [128, 8], F32)
    mtt = p.sb("a2_mtt", [128, 128], BF16)

    vload = [None, None]
    if pairs is None:
        pairs = [(b, h) for b in range(B) for h in range(NH)]
    prev_done = list(consts)
    prev_ost = None

    for (b, h) in pairs:
        if vload[b] is None:
            vload[b] = p.dma("scalar", vsb[b][:].rearrange("p (i d) -> p i d", i=NT),
                             v_tm[b].rearrange("(i p) d -> p i d", p=128))
        lq = p.dma("sync", qf[:], q_fm[b][h], after=prev_done)
        lk = p.dma("sync", kf[:], k_fm[b][h], after=prev_done)
        lw = p.dma("sync", wf[:], w_fm[b][h], after=prev_done)
        lwt = p.dma("sync", wtm[:].rearrange("p (i d) -> p i d", i=NT),
                    w_tm[b][:, h * 128:(h + 1) * 128].rearrange(
                        "(i p) d -> p i d", p=128), after=prev_done)
        lbe = p.dma("sync", betaf[:].rearrange("p (i e) -> p i e", i=NT),
                    beta_tm[b].rearrange("(i p) e -> p i e", p=128), after=prev_done)
        lg = p.dma("sync", negGr[:], negG_hm[b, h].unsqueeze(0), after=prev_done)

        mt_toks = []
        pw_mms = []
        for I in range(NT):
            sl = slice(I * 128, (I + 1) * 128)
            mm = p.t(lambda t, sl=sl: t.matmul(
                psA[:, sl], wf[:, sl], wf[:, sl], start=True, stop=True),
                after=[lw] + prev_done)
            pw_mms.append(mm)
            st1 = p.v(lambda v, sl=sl, I=I, h=h: v.tensor_scalar(
                out=MT[:, sl], in0=psA[:, sl], scalar1=betaf[:, I * 8 + h:I * 8 + h + 1],
                scalar2=None, op0=ALU.mult), after=[mm, lbe])
            mt_toks.append(st1)
        af = p.g(lambda g: g.affine_select(
            out=MT[:].rearrange("p (i t) -> p i t", i=NT),
            in_=MT[:].rearrange("p (i t) -> p i t", i=NT),
            pattern=[[0, NT], [1, 128]], compare_op=ALU.is_gt, fill=0.0, base=0,
            channel_multiplier=-1), after=mt_toks)
        mtf = p.v(lambda v: v.tensor_add(MT[:], MT[:], identrep[:]), after=[af])

        xi = p.g(lambda g: g.tensor_copy(Xb[:], identrep[:]), after=[af] + prev_done)
        zi = p.g(lambda g: g.tensor_copy(Zb[:], identrep[:]), after=[af] + prev_done)
        tX, tZ = xi, zi
        tR = None
        t_evX = t_evZ = mtf
        for it in range(7):
            t1s = []
            for I in range(NT):
                sl = slice(I * 128, (I + 1) * 128)
                mm = p.t(lambda t, sl=sl: t.matmul(
                    psB[:, sl], MT[:, sl], Xb[:, sl], start=True, stop=True),
                    after=[mtf, tX, t_evZ], sync=(I == NT - 1))
                t1s.append(mm)
            tR = p.v(lambda v: v.tensor_sub(Rb[:], ident2rep[:], psB[:]),
                     after=[t1s[-1], tR] if tR else [t1s[-1]])
            xms = []
            for I in range(NT):
                sl = slice(I * 128, (I + 1) * 128)
                mm = p.t(lambda t, sl=sl: t.matmul(
                    psA[:, sl], Zb[:, sl], Rb[:, sl], start=True, stop=True),
                    after=[tR, t_evX, tZ], sync=(I == NT - 1))
                xms.append(mm)
            zms = []
            for I in range(NT):
                sl = slice(I * 128, (I + 1) * 128)
                mm = p.t(lambda t, sl=sl: t.matmul(
                    psB[:, sl], Rb[:, sl], Zb[:, sl], start=True, stop=True),
                    after=[tR, xms[-1], tZ], sync=(I == NT - 1))
                zms.append(mm)
            tX = p.s(lambda s: s.activation(Xb[:], psA[:], AF.Copy),
                     after=[xms[-1]])
            tZ = p.s(lambda s: s.activation(Zb[:], psB[:], AF.Copy),
                     after=[zms[-1]])
            t_evX, t_evZ = tX, tZ

        e1 = p.s(lambda s: s.activation(Ef[:], kf[:], AF.Copy), after=[lk] + prev_done)
        e2 = p.g(lambda g: g.tensor_copy(Eb[:], kf[:]), after=[lk] + prev_done)
        tEup, tEcast = e1, e2
        tVread = tZ
        tTread = tZ
        tOev = tZ
        tAread = tZ
        tUp = tZ
        tFdone = tZ
        oevs = []

        for I in range(NT):
            span = (I + 1) * 128
            sl = slice(I * 128, (I + 1) * 128)
            tms = []
            for s0 in range(0, span, 512):
                s1 = min(s0 + 512, span)
                mm = p.t(lambda t, s0=s0, s1=s1, sl=sl: t.matmul(
                    psB[:, s0:s1], wf[:, sl], Eb[:, s0:s1], start=True, stop=True),
                    after=[tEcast, tUp, lw], sync=(s1 == span))
                tms.append(mm)
            evT = p.s(lambda s, span=span: s.activation(
                Tbf[:, 0:span], psB[:, 0:span], AF.Copy), after=[tms[-1], tTread])
            mkT = p.g(lambda g, sl=sl: g.affine_select(
                out=Tbf[:, sl], in_=Tbf[:, sl], pattern=[[-1, 128]],
                compare_op=ALU.is_gt, fill=0.0, base=0, channel_multiplier=1),
                after=[evT])
            yms = []
            for s0 in range(0, span, 512):
                s1 = min(s0 + 512, span)
                mm = p.t(lambda t, s0=s0, s1=s1, sl=sl: t.matmul(
                    psB[:, s0:s1], Zb[:, sl], Tbf[:, s0:s1], start=True, stop=True),
                    after=[evT, mkT], sync=(s1 == span))
                yms.append(mm)
            tV = p.v(lambda v, span=span, I=I, h=h: v.tensor_scalar(
                out=Vsb[:, 0:span], in0=psB[:, 0:span],
                scalar1=betaf[:, I * 8 + h:I * 8 + h + 1], scalar2=None,
                op0=ALU.mult), after=[yms[-1], tVread])
            gm = p.t(lambda t, sl=sl: t.matmul(
                psB[:, 0:128], wf[:, sl], qf[:, sl], start=True, stop=True),
                after=[tV, lq])
            gev = p.s(lambda s: s.activation(negGm[:], psB[:, 0:128], AF.Copy,
                                             scale=-1.0), after=[gm])
            gaf = p.g(lambda g: g.affine_select(
                out=negGm[:], in_=negGm[:], pattern=[[1, 128]],
                compare_op=ALU.is_ge, fill=0.0, base=0, channel_multiplier=-1),
                after=[gev])
            ams = []
            for s0 in range(0, span, 512):
                s1 = min(s0 + 512, span)
                m1 = p.t(lambda t, s0=s0, s1=s1, sl=sl: t.matmul(
                    psA[:, s0:s1], qf[:, sl], Eb[:, s0:s1], start=True, stop=False),
                    after=[gaf, tOev, tAread, tEcast], sync=False)
                m2 = p.t(lambda t, s0=s0, s1=s1: t.matmul(
                    psA[:, s0:s1], negGm[:], Vsb[:, s0:s1], start=False, stop=False),
                    sync=False)
                m3 = p.t(lambda t, s0=s0, s1=s1: t.matmul(
                    psA[:, s0:s1], ones1f[:], negGr[0:1, s0:s1], start=False,
                    stop=True), after=[lg], sync=True)
                ams.append(m3)
            dm = p.v(lambda v, sl=sl: v.tensor_add(psA[:, sl], psA[:, sl], maskb[:]),
                     after=[ams[-1]])
            nm = p.v(lambda v, span=span: v.reduce_max(
                stt_[:, 0:1], psA[:, 0:span], axis=AX.X, negate=True), after=[dm])
            ex = p.s(lambda s, span=span: s.activation(
                Eexp[:, 0:span], psA[:, 0:span], AF.Exp, bias=stt_[:, 0:1],
                scale=1.0, accum_out=stt_[:, 1:2]), after=[nm, tFdone])
            tAread = ex
            rc = p.v(lambda v: v.reciprocal(stt_[:, 2:3], stt_[:, 1:2]), after=[ex])
            tF = p.dma("sync",
                       Fsb[:, 0:span].rearrange("t (c p) -> t c p", c=I + 1),
                       Eexp[:, 0:span].rearrange("p (c t) -> p c t", c=I + 1),
                       after=[ex], transpose=True)
            tFdone = tF
            pvs = []
            for c in range(I + 1):
                mm = p.t(lambda t, c=c, b=b, I=I, h=h: t.matmul(
                    psA[:, 0:128], Fsb[:, c * 128:(c + 1) * 128],
                    vsb[b][:, c * D + h * 128:c * D + (h + 1) * 128],
                    start=(c == 0), stop=(c == I)),
                    after=[tF, vload[b], ex], sync=(c == I))
                pvs.append(mm)
            tOev = p.v(lambda v, I=I: v.tensor_scalar(
                out=ocoll[:, I * 128:(I + 1) * 128], in0=psA[:, 0:128],
                scalar1=stt_[:, 2:3], scalar2=None, op0=ALU.mult),
                after=[pvs[-1], rc])
            oevs.append(tOev)
            if I < NT - 1:
                ums = []
                for s0 in range(0, span, 512):
                    s1 = min(s0 + 512, span)
                    mm = p.t(lambda t, s0=s0, s1=s1, sl=sl: t.matmul(
                        psB[:, s0:s1], wtm[:, sl], Vsb[:, s0:s1], start=True,
                        stop=True), after=[gev, lwt, tV], sync=(s1 == span))
                    ums.append(mm)
                tEup = p.v(lambda v, span=span: v.tensor_sub(
                    Ef[:, 0:span], Ef[:, 0:span], psB[:, 0:span]),
                    after=[ums[-1], tEup, tEcast])
                tEcast = p.s(lambda s, span=span: s.activation(
                    Eb[:, 0:span], Ef[:, 0:span], AF.Copy), after=[tEup])
                tUp = tEup
                tTread = tEup
                tVread = ums[-1]
            else:
                tVread = pvs[-1]
                tTread = tV

        oT = p.dma("sync", ofmb[:].rearrange("d (i t) -> d i t", i=NT),
                   ocoll[:].rearrange("t (i d) -> t i d", i=NT),
                   after=oevs + ([prev_ost] if prev_ost else []), transpose=True)
        ost = p.dma("gpsimd", o_fm[b][h], ofmb[:], after=[oT])
        prev_ost = ost
        prev_done = [ost, tOev, tEcast, tF, lbe, lg, lwt]
        if DEBUG_A2_BARRIER:
            for e in ("tensor", "vector", "scalar", "gpsimd", "sync"):
                p.wait(e, prev_done + [tV, ex, nm, rc, gaf, mkT, af, mtf, tZ, tX])
            p.barrier()
    return


def build_l0a_inner(pairs=None):
    p = Prog("l0ai")
    aps = {
        "q_fm": p.dram("q_fm", [B, NH, 128, T], BF16, kind="ExternalInput"),
        "k_fm": p.dram("k_fm", [B, NH, 128, T], BF16, kind="ExternalInput"),
        "w_fm": p.dram("w_fm", [B, NH, 128, T], BF16, kind="ExternalInput"),
        "v_tm": p.dram("v_tm", [B, T, D], BF16, kind="ExternalInput"),
        "w_tm": p.dram("w_tm", [B, T, D], BF16, kind="ExternalInput"),
        "beta_tm": p.dram("beta_tm", [B, T, 8], F32, kind="ExternalInput"),
        "negG_hm": p.dram("negG_hm", [B, NH, T], F32, kind="ExternalInput"),
        "o_fm": p.dram("o_fm", [B, NH, 128, T], BF16, kind="ExternalOutput"),
        "c_ident_bf": const_ap(p, "ident_bf"),
        "c_ident2_bf": const_ap(p, "ident2_bf"),
        "c_maskbias_causal": const_ap(p, "maskbias_causal"),
    }
    emit_l0a_inner(p, aps, pairs=pairs)
    return p


def build_l0a_proj():
    p = Prog("l0ap")
    aps = {
        "xn_fm": p.dram("xn_fm", [B, NC, 128, T], BF16, kind="ExternalInput"),
        "Wq": p.dram("Wq", [D, D], F32, kind="ExternalInput"),
        "Wk": p.dram("Wk", [D, D], F32, kind="ExternalInput"),
        "Wv": p.dram("Wv", [D, D], F32, kind="ExternalInput"),
        "Ww": p.dram("Ww", [D, D], F32, kind="ExternalInput"),
        "Wbeta": p.dram("Wbeta", [D, 8], F32, kind="ExternalInput"),
        "Wg": p.dram("Wg", [D, 8], F32, kind="ExternalInput"),
        "conv_w": p.dram("conv_w", [D, 4], F32, kind="ExternalInput"),
        "qn_w": p.dram("qn_w", [128], F32, kind="ExternalInput"),
        "kn_w": p.dram("kn_w", [128], F32, kind="ExternalInput"),
        "q_fm": p.dram("q_fm", [B, NH, 128, T], BF16, kind="ExternalOutput"),
        "k_fm": p.dram("k_fm", [B, NH, 128, T], BF16, kind="ExternalOutput"),
        "w_fm": p.dram("w_fm", [B, NH, 128, T], BF16, kind="ExternalOutput"),
        "v_tm": p.dram("v_tm", [B, T, D], BF16, kind="ExternalOutput"),
        "w_tm": p.dram("w_tm", [B, T, D], BF16, kind="ExternalOutput"),
        "beta_tm": p.dram("beta_tm", [B, T, 8], F32, kind="ExternalOutput"),
        "negG_hm": p.dram("negG_hm", [B, NH, T], F32, kind="ExternalOutput"),
    }
    emit_l0a_proj(p, aps)
    return p



DH = 64
NHD = 16


def emit_dec(p, aps, tag, S, x_fm, x_tm, out_tm, wq, wk, wv, wo, ln1, ln2,
             wg, wu, wd, ropec, ropes, FFN=4096):
    NTs = S // 128
    psA = p.ps(f"{tag}_psA", [128, 2048])
    psB = p.ps(f"{tag}_psB", [128, 2048])
    ones1f = p.sb(f"{tag}_ones1f", [1, 128], F32)
    t_o1 = p.v(lambda v: v.memset(ones1f[:], 1.0))
    p.init_toks.append(t_o1)
    maskb = p.sb(f"{tag}_maskb", [128, 128], F32)
    cmk = p.dma("sync", maskb[:], aps["c_maskbias_causal"])
    p.init_toks.append(cmk)

    ln1v, tl1 = load_vec_fm(p, f"{tag}_ln1v", ln1, D)
    ln2v, tl2 = load_vec_fm(p, f"{tag}_ln2v", ln2, D)
    p.init_toks += [tl1, tl2]

    wslots = ([p.sb(f"{tag}_ws{i}", [128, 1024], BF16) for i in range(NC)],
              {"toks": []})
    wring = Ring(p, f"{tag}_wld", [128, 1024], F32, n=2)

    xsb = p.sb(f"{tag}_x", [128, NC * S], BF16)
    rstd = p.sb(f"{tag}_rstd", [128, NTs * 2], F32)
    sq = Ring(p, f"{tag}_sq", [128, D], F32, n=1)
    xtf = Ring(p, f"{tag}_xt", [128, D], F32, n=2)
    st = Ring(p, f"{tag}_st", [128, 4], F32, n=2)
    bb = Ring(p, f"{tag}_bb", [128, 1024], BF16, n=2)
    tpb = Ring(p, f"{tag}_tp", [128, 1024], BF16, n=2)
    ctab = Ring(p, f"{tag}_ct", [128, DH], F32, n=2)
    stab = Ring(p, f"{tag}_st2", [128, DH], F32, n=2)
    ropebuf = Ring(p, f"{tag}_rp", [128, 1024], F32, n=3)

    q_fm = p.dram(f"{tag}_qfm", [B, NHD // 2, 128, S], BF16)
    k_fm = p.dram(f"{tag}_kfm", [B, NHD // 2, 128, S], BF16)
    v_tm = p.dram(f"{tag}_vtm", [B, S, D], BF16)
    o_fm = p.dram(f"{tag}_ofm", [B, NC, 128, S], BF16)
    xm_tm = p.dram(f"{tag}_xm", [B, S, D], F32)
    xm_fm = p.dram(f"{tag}_xmfm", [B, NC, 128, S], BF16)

    def rstd_tile(xti, sti, sqi, i, col, after):
        s1 = p.s(lambda s, xti=xti, sqi=sqi: s.activation(sqi[:], xti[:], AF.Square),
                 after=after)
        s2 = p.v(lambda v, sqi=sqi, sti=sti: v.reduce_sum(sti[:, 0:1], sqi[:],
                                                          axis=AX.X), after=[s1])
        cc = p.constcol(1e-6)
        s3 = p.s(lambda s, sti=sti, cc=cc: s.activation(
            sti[:, 1:2], sti[:, 0:1], AF.Sqrt, scale=1.0 / D, bias=cc), after=[s2])
        s4 = p.v(lambda v, sti=sti, col=col: v.reciprocal(rstd[:, col:col + 1],
                                                          sti[:, 1:2]), after=[s3])
        return s4

    qf = p.sb(f"{tag}_qf", [128, (NHD // 2) * S], BF16)
    kf = p.sb(f"{tag}_kf", [128, (NHD // 2) * S], BF16)
    Eexp = [p.sb(f"{tag}_Ee", [128, 2048], BF16),
            p.sb(f"{tag}_Ee2", [128, 2048], BF16)]
    Fsb = [p.sb(f"{tag}_F", [128, 2048], BF16),
           p.sb(f"{tag}_F2", [128, 2048], BF16)]
    opair = p.sb(f"{tag}_op", [128, 128], BF16)
    oT = p.sb(f"{tag}_oT", [128, 128], BF16)
    stt_ = p.sb(f"{tag}_stt", [128, 8], F32)
    GRP = min(4, S // 128)
    out_acc = p.sb(f"{tag}_oacc", [128, GRP * D], F32)
    gsl = [p.sb(f"{tag}_gs_{c}", [128, 512], BF16) for c in range(NC)]
    usl = [p.sb(f"{tag}_us_{c}", [128, 512], BF16) for c in range(NC)]
    dsl = [wslots[0][c][:, 0:1024] for c in range(4)]
    wldr = wring
    gact = Ring(p, f"{tag}_ga", [128, 512], BF16, n=1)
    ffb = Ring(p, f"{tag}_ff", [128, 512], BF16, n=2)
    ffT = Ring(p, f"{tag}_ffT", [128, 512], BF16, n=1)
    bat_guard = []
    for b in range(B):
        t_x = p.dma("sync", xsb[:].rearrange("p (c t) -> p c t", c=NC),
                    x_fm[b].rearrange("c p t -> p c t"), after=bat_guard)

        def lhs(c, ti, xsb=xsb):
            return xsb[:, c * S + ti * 128:c * S + ti * 128 + 128]

        stores = {}
        rtoks = []
        for i in range(NTs):
            xti = xtf.next()
            lx = p.dma("scalar", xti[:], x_tm[b, i * 128:(i + 1) * 128, :],
                       after=xtf.guard)
            sti = st.next()
            sqi = sq.next()
            r = rstd_tile(xti, sti, sqi, i, 2 * i, [lx] + st.guard + sq.guard)
            st.set_guard([r])
            sq.set_guard([r])
            xtf.set_guard([r])
            rtoks.append(r)

        for (nm, wdr, dst, gscale) in (("q", wq, q_fm, 0.125), ("k", wk, k_fm, 1.0),
                                       ("v", wv, v_tm, 1.0)):
            W = FMWeight(p, f"{tag}_{nm}w{b}", wdr, D, D, gamma=None, ring=wring,
                         slots=wslots)
            gtoks = []
            for c in range(NC):
                gt = p.v(lambda v, c=c, W=W, gscale=gscale: v.tensor_scalar(
                    out=W.chunks[c], in0=W.chunks[c], scalar1=ln1v[:, c:c + 1],
                    scalar2=gscale, op0=ALU.mult, op1=ALU.mult),
                    after=[W.tok[c]] + p.init_toks)
                gtoks.append(gt)
            mms_all = []
            stores[nm] = []
            evp = [None, None]
            for i in range(NTs):
                po = (i % 2) * 1024
                mms = []
                for half in range(2):
                    for c in range(NC):
                        mm = p.t(lambda t, la=lhs(c, i), c=c, half=half, W=W,
                                 po=po: t.matmul(
                            psA[:, po + half * 512:po + (half + 1) * 512], la,
                            W.chunks[c][:, half * 512:(half + 1) * 512],
                            start=(c == 0), stop=(c == NC - 1)),
                            after=[t_x] + gtoks +
                            ([evp[i % 2]] if evp[i % 2] else []),
                            sync=(c == NC - 1))
                    mms.append(mm)
                mms_all.append(mms[-1])
                qtb = ropebuf.next()
                ev = p.v(lambda v, qtb=qtb, i=i, po=po: v.tensor_scalar(
                    out=qtb[:], in0=psA[:, po:po + 1024],
                    scalar1=rstd[:, 2 * i:2 * i + 1],
                    scalar2=None, op0=ALU.mult),
                    after=mms + [rtoks[i]] + ropebuf.guard)
                evp[i % 2] = ev
                ob = bb.next()
                if ropec is not None and nm != "v":
                    ct = ctab.next()
                    stb2 = stab.next()
                    tab = ropec[b] if ropec.ndim == 3 else ropec
                    tabs = ropes[b] if ropes.ndim == 3 else ropes
                    lc = p.dma("scalar", ct[:], tab[i * 128:(i + 1) * 128, :],
                               after=ctab.guard)
                    ls = p.dma("scalar", stb2[:], tabs[i * 128:(i + 1) * 128, :],
                               after=stab.guard)
                    rb2 = ropebuf.next()
                    vi = qtb[:].rearrange("p (h two i) -> p h two i", two=2, i=32)
                    vo = rb2[:].rearrange("p (h two i) -> p h two i", two=2, i=32)
                    svi = stb2[:].rearrange("p (two i) -> p two i", two=2)
                    c1 = p.v(lambda v, vo=vo, vi=vi, svi=svi: v.tensor_mul(
                        vo[:, :, 0, :], vi[:, :, 1, :],
                        svi[:, 0, :].unsqueeze(1).to_broadcast([128, NHD, 32])),
                        after=[ev, ls] + ropebuf.guard)
                    c2 = p.v(lambda v, vo=vo, vi=vi, svi=svi: v.tensor_mul(
                        vo[:, :, 1, :], vi[:, :, 0, :],
                        svi[:, 1, :].unsqueeze(1).to_broadcast([128, NHD, 32])),
                        after=[ev, ls])
                    c3 = p.v(lambda v, qtb=qtb, ct=ct: v.tensor_mul(
                        qtb[:].rearrange("p (h e) -> p h e", e=DH),
                        qtb[:].rearrange("p (h e) -> p h e", e=DH),
                        ct[:].unsqueeze(1).to_broadcast([128, NHD, DH])),
                        after=[ev, lc, c1, c2])
                    c4 = p.v(lambda v, ob=ob, qtb=qtb, rb2=rb2: v.tensor_add(
                        ob[:], qtb[:], rb2[:]), after=[c1, c2, c3] + bb.guard)
                    ropebuf.set_guard([c4])
                    ctab.set_guard([c4])
                    stab.set_guard([c4])
                    fin = c4
                else:
                    fin = p.s(lambda s, ob=ob, qtb=qtb: s.activation(
                        ob[:], qtb[:], AF.Copy), after=[ev] + bb.guard)
                    ropebuf.set_guard([fin])
                if nm == "v":
                    so = p.dma("gpsimd", v_tm[b, i * 128:(i + 1) * 128, :], ob[:],
                               after=[fin])
                else:
                    tmp = tpb.next()
                    p.wait("sync", tpb.guard)
                    so = store_fm(p, ob, dst[b], i, [fin], tmp, nch=NC)
                    tpb.set_guard([so])
                bb.set_guard([so])
                stores[nm].append(so)
            wslots[1]["toks"] = mms_all

        vf = xsb
        lq = p.dma("sync", qf[:].rearrange("p (h t) -> p h t", h=NHD // 2),
                   q_fm[b].rearrange("h p t -> p h t"),
                   after=stores["q"] + bat_guard)
        lk = p.dma("sync", kf[:].rearrange("p (h t) -> p h t", h=NHD // 2),
                   k_fm[b].rearrange("h p t -> p h t"),
                   after=stores["k"] + bat_guard)
        lv = p.dma("scalar", vf[:].rearrange("p (i d) -> p i d", i=NTs),
                   v_tm[b].rearrange("(i p) d -> p i d", p=128),
                   after=[mms_all[-1]] + stores["v"] + bat_guard)
        pe0 = [t for t in evp if t] or [lq]
        tEr = [pe0[-1], pe0[-1]]
        tOr = [pe0[0], pe0[0]]
        tFr = [lq, lq]
        prevT = [lq]
        prevTT = lq
        ostores = []
        for i in range(NTs):
            span = (i + 1) * 128
            for hp in range(NHD // 2):
                otoks = []
                for hh in range(2):
                    pp = (hp * 2 + hh) % 2
                    R = psA if pp == 0 else psB
                    q0 = pp * 4
                    qsl = qf[hh * 64:(hh + 1) * 64,
                              hp * S + i * 128:hp * S + i * 128 + 128]
                    mms = []
                    for s0 in range(0, span, 512):
                        s1 = min(s0 + 512, span)
                        mm = p.t(lambda t, qsl=qsl, s0=s0, s1=s1, hp=hp, hh=hh,
                                 R=R: t.matmul(
                            R[:, s0:s1], qsl,
                            kf[hh * 64:(hh + 1) * 64, hp * S + s0:hp * S + s1],
                            start=True, stop=True),
                            after=[lq, lk, tEr[pp], tOr[pp]], sync=(s1 == span))
                        mms.append(mm)
                    dm = p.v(lambda v, i=i, R=R: v.tensor_add(
                        R[:, i * 128:(i + 1) * 128],
                        R[:, i * 128:(i + 1) * 128], maskb[:]), after=mms)
                    nmx = p.v(lambda v, span=span, R=R, q0=q0: v.reduce_max(
                        stt_[:, q0:q0 + 1], R[:, 0:span], axis=AX.X, negate=True),
                        after=[dm])
                    ex = p.s(lambda s, span=span, R=R, q0=q0, pp=pp: s.activation(
                        Eexp[pp][:, 0:span], R[:, 0:span], AF.Exp,
                        bias=stt_[:, q0:q0 + 1],
                        scale=1.0, accum_out=stt_[:, q0 + 1:q0 + 2]),
                        after=[nmx, tFr[pp]])
                    tEr[pp] = ex
                    rc = p.v(lambda v, q0=q0: v.reciprocal(
                        stt_[:, q0 + 2:q0 + 3], stt_[:, q0 + 1:q0 + 2]),
                        after=[ex])
                    tF = p.dma("sync",
                               Fsb[pp][:, 0:span].rearrange("t (c p) -> t c p",
                                                            c=i + 1),
                               Eexp[pp][:, 0:span].rearrange("p (c t) -> p c t",
                                                             c=i + 1),
                               after=[ex], transpose=True)
                    tFr[pp] = tF
                    pvs = []
                    for c in range(i + 1):
                        mm = p.t(lambda t, c=c, hp=hp, hh=hh, i=i, R=R,
                                 pp=pp: t.matmul(
                            R[:, 0:64], Fsb[pp][:, c * 128:(c + 1) * 128],
                            vf[:, c * D + (hp * 2 + hh) * 64:
                               c * D + (hp * 2 + hh) * 64 + 64],
                            start=(c == 0), stop=(c == i)),
                            after=[tF, lv], sync=(c == i))
                        pvs.append(mm)
                    oe = p.v(lambda v, hh=hh, R=R, q0=q0: v.tensor_scalar(
                        out=opair[:, hh * 64:(hh + 1) * 64], in0=R[:, 0:64],
                        scalar1=stt_[:, q0 + 2:q0 + 3], scalar2=None,
                        op0=ALU.mult),
                        after=[pvs[-1], rc] + ([prevTT] if hh == 0 else []))
                    tOr[pp] = oe
                    otoks.append(oe)
                tT = p.dma("sync", oT[:], opair[:], after=otoks + prevT,
                           transpose=True)
                prevTT = tT
                so = p.dma("gpsimd", o_fm[b][hp][:, i * 128:(i + 1) * 128], oT[:],
                           after=[tT])
                prevT = [so]
                ostores.append(so)
        rtoks2 = []
        xmstores = []
        xmtstores = []
        Wo = FMWeight(p, f"{tag}_wo{b}", wo, D, D, ring=wring, slots=wslots)
        osb = qf
        lo = p.dma("sync", osb[:, 0:NC * S].rearrange("p (c t) -> p c t", c=NC),
                   o_fm[b].rearrange("c p t -> p c t"),
                   after=ostores + tOr + tEr + tFr)
        wom = []
        evp = [None, None]
        for i in range(NTs):
            po = (i % 2) * 1024
            mms = []
            for half in range(2):
                for c in range(NC):
                    mm = p.t(lambda t, c=c, i=i, half=half, Wo=Wo, osb=osb,
                             po=po: t.matmul(
                        psA[:, po + half * 512:po + (half + 1) * 512],
                        osb[:, c * S + i * 128:c * S + i * 128 + 128],
                        Wo.chunks[c][:, half * 512:(half + 1) * 512],
                        start=(c == 0), stop=(c == NC - 1)),
                        after=[lo] + Wo.tok + tOr + tEr +
                        ([evp[i % 2]] if evp[i % 2] else []), sync=(c == NC - 1))
                mms.append(mm)
            wom.append(mms[-1])
            xti = xtf.next()
            lx = p.dma("scalar", xti[:], x_tm[b, i * 128:(i + 1) * 128, :],
                       after=xtf.guard)
            ad = p.v(lambda v, xti=xti, po=po: v.tensor_add(
                xti[:], xti[:], psA[:, po:po + 1024]), after=mms + [lx])
            evp[i % 2] = ad
            sx = p.dma("gpsimd", xm_tm[b, i * 128:(i + 1) * 128, :], xti[:],
                       after=[ad])
            sti = st.next()
            sqi = sq.next()
            r2 = rstd_tile(xti, sti, sqi, i, 2 * i + 1, [ad] + st.guard + sq.guard)
            rtoks2.append(r2)
            st.set_guard([r2])
            sq.set_guard([r2])
            ob = bb.next()
            cb = p.s(lambda s, ob=ob, xti=xti: s.activation(ob[:], xti[:], AF.Copy),
                     after=[ad] + bb.guard)
            xtf.set_guard([sx, r2, cb])
            tmp = tpb.next()
            p.wait("sync", tpb.guard)
            o2 = store_fm(p, ob, xm_fm[b], i, [cb], tmp, nch=NC)
            bb.set_guard([o2])
            tpb.set_guard([o2])
            xmstores.append(o2)
            xmtstores.append(sx)
        wslots[1]["toks"] = wom

        xmsb = kf
        lxm = p.dma("sync", xmsb[:, 0:NC * S].rearrange("p (c t) -> p c t", c=NC),
                    xm_fm[b].rearrange("c p t -> p c t"),
                    after=xmstores + [wom[-1]] + tOr)
        fin_toks = []
        for g0 in range(0, NTs, GRP):
          zacc = p.v(lambda v: v.memset(out_acc[:], 0.0), after=fin_toks)
          guard_w = []
          atoks = [zacc]
          for j in range(FFN // 512):
              wg_t, wu_t, wd_t = [], [], []
              for c in range(NC):
                  dstb = wldr.next()
                  l1 = p.dma("scalar", dstb[:, 0:512],
                             wg[c * 128:(c + 1) * 128, j * 512:(j + 1) * 512],
                             after=wldr.guard + guard_w)
                  g1 = p.v(lambda v, c=c, dstb=dstb, gsl=gsl: v.tensor_scalar(
                      out=gsl[c][:], in0=dstb[:, 0:512], scalar1=ln2v[:, c:c + 1],
                      scalar2=None, op0=ALU.mult), after=[l1])
                  wldr.set_guard([g1])
                  dstb = wldr.next()
                  l2 = p.dma("scalar", dstb[:, 0:512],
                             wu[c * 128:(c + 1) * 128, j * 512:(j + 1) * 512],
                             after=wldr.guard + guard_w)
                  g2 = p.v(lambda v, c=c, dstb=dstb, usl=usl: v.tensor_scalar(
                      out=usl[c][:], in0=dstb[:, 0:512], scalar1=ln2v[:, c:c + 1],
                      scalar2=None, op0=ALU.mult), after=[l2])
                  wldr.set_guard([g2])
                  wg_t.append(g1)
                  wu_t.append(g2)
              for c in range(4):
                  dstb = wldr.next()
                  l3 = p.dma("scalar", dstb[:],
                             wd[j * 512 + c * 128:j * 512 + (c + 1) * 128, :],
                             after=wldr.guard + guard_w)
                  g3 = p.v(lambda v, c=c, dstb=dstb, dsl=dsl: v.tensor_copy(
                      dsl[c], dstb[:]), after=[l3] + wslots[1]["toks"])
                  wldr.set_guard([g3])
                  wd_t.append(g3)
              guard_w = []
              for i in range(g0, g0 + GRP):
                  gm = []
                  for c in range(NC):
                      mm = p.t(lambda t, c=c, i=i, xmsb=xmsb, gsl=gsl: t.matmul(
                          psB[:, 0:512], xmsb[:, c * S + i * 128:c * S + i * 128 + 128],
                          gsl[c][:], start=(c == 0), stop=(c == NC - 1)),
                          after=[lxm] + wg_t + atoks, sync=(c == NC - 1))
                  gm = mm
                  for c in range(NC):
                      mm = p.t(lambda t, c=c, i=i, xmsb=xmsb, usl=usl: t.matmul(
                          psB[:, 512:1024], xmsb[:, c * S + i * 128:c * S + i * 128 + 128],
                          usl[c][:], start=(c == 0), stop=(c == NC - 1)),
                          after=wu_t, sync=(c == NC - 1))
                  um = mm
                  ga = gact.next()
                  sil = p.s(lambda s, ga=ga, i=i: s.activation(
                      ga[:], psB[:, 0:512], AF.Silu, scale=rstd[:, 2 * i + 1:2 * i + 2]),
                      after=[gm, rtoks2[i]] + gact.guard)
                  ff = ffb.next()
                  st2 = p.v(lambda v, ff=ff, ga=ga, i=i: v.scalar_tensor_tensor(
                      out=ff[:], in0=psB[:, 512:1024],
                      scalar=rstd[:, 2 * i + 1:2 * i + 2], in1=ga[:],
                      op0=ALU.mult, op1=ALU.mult), after=[um, sil] + ffb.guard)
                  gact.set_guard([st2])
                  p.wait("tensor", [st2])
                  fT = ffT.next()
                  p.wait("sync", ffT.guard)
                  tt = p.dma("sync", fT[:].rearrange("t (c p) -> t c p", c=4),
                             ff[:].rearrange("p (c t) -> p c t", c=4),
                             after=[st2], transpose=True)
                  ffb.set_guard([tt])
                  wdm = []
                  for c in range(4):
                      for half in range(2):
                          mm = p.t(lambda t, c=c, half=half, fT=fT, dsl=dsl: t.matmul(
                              psA[:, half * 512:(half + 1) * 512],
                              fT[:, c * 128:(c + 1) * 128],
                              dsl[c][:, half * 512:(half + 1) * 512],
                              start=(c == 0), stop=(c == 3)),
                              after=[tt] + wd_t, sync=(c == 3 and half == 1))
                          wdm.append(mm)
                  ac = p.v(lambda v, i=i, g0=g0: v.tensor_add(
                      out_acc[:, (i - g0) * D:(i - g0 + 1) * D],
                      out_acc[:, (i - g0) * D:(i - g0 + 1) * D],
                      psA[:, 0:1024]), after=[wdm[-1], zacc])
                  ffT.set_guard([ac])
                  atoks = [ac]
                  guard_w.append(ac)
                  p.wait("tensor", [ac])
          for i in range(g0, g0 + GRP):
            xti = xtf.next()
            lx = p.dma("scalar", xti[:], xm_tm[b, i * 128:(i + 1) * 128, :],
                       after=xtf.guard + xmtstores)
            ad = p.v(lambda v, xti=xti, i=i, g0=g0: v.tensor_add(
                xti[:], xti[:], out_acc[:, (i - g0) * D:(i - g0 + 1) * D]),
                after=[lx] + atoks)
            so = p.dma("gpsimd", out_tm[b, i * 128:(i + 1) * 128, :], xti[:],
                       after=[ad])
            xtf.set_guard([so])
            fin_toks.append(so)
            fin_toks = fin_toks + atoks
        bat_guard = fin_toks + tOr + tEr
    return


def build_dec(S, with_rope=True):
    p = Prog("dec")
    NTs = S // 128
    aps = {
        "x_fm": p.dram("x_fm", [B, NC, 128, S], BF16, kind="ExternalInput"),
        "x_tm": p.dram("x_tm", [B, S, D], F32, kind="ExternalInput"),
        "wq": p.dram("wq", [D, D], F32, kind="ExternalInput"),
        "wk": p.dram("wk", [D, D], F32, kind="ExternalInput"),
        "wv": p.dram("wv", [D, D], F32, kind="ExternalInput"),
        "wo": p.dram("wo", [D, D], F32, kind="ExternalInput"),
        "ln1": p.dram("ln1", [D], F32, kind="ExternalInput"),
        "ln2": p.dram("ln2", [D], F32, kind="ExternalInput"),
        "wg": p.dram("wg", [D, 4096], F32, kind="ExternalInput"),
        "wu": p.dram("wu", [D, 4096], F32, kind="ExternalInput"),
        "wd": p.dram("wd", [4096, D], F32, kind="ExternalInput"),
        "ropec": p.dram("ropec", [B, S, DH], F32, kind="ExternalInput"),
        "ropes": p.dram("ropes", [B, S, DH], F32, kind="ExternalInput"),
        "out_tm": p.dram("out_tm", [B, S, D], F32, kind="ExternalOutput"),
        "c_maskbias_causal": const_ap(p, "maskbias_causal"),
    }
    emit_dec(p, aps, "dc", S, aps["x_fm"], aps["x_tm"], aps["out_tm"],
             aps["wq"], aps["wk"], aps["wv"], aps["wo"], aps["ln1"], aps["ln2"],
             aps["wg"], aps["wu"], aps["wd"], aps["ropec"], aps["ropes"])
    return p



def emit_router1(p, aps):
    xn_tm, xn_fm, proc_tm = aps["xn_tm"], aps["xn_fm"], aps["proc_tm"]
    dd, partial = aps["dd"], aps["partial"]
    psA = p.ps("r1_psA", [128, 2048])
    gv, tgv = load_vec_fm(p, "r1_pnw", aps["prior_norm_w"], D)
    onescol = p.sb("r1_onec", [128, 1], F32)
    t_oc = p.v(lambda v: v.memset(onescol[:], 1.0))
    p.init_toks += [tgv, t_oc]
    wslots = ([p.sb(f"r1_ws{i}", [128, 2048], BF16) for i in range(NC)],
              {"toks": []})
    wring = Ring(p, "r1_wld", [128, 2048], F32, n=2)
    W = FMWeight(p, "r1_pw", aps["prior_w"], D, 2 * D, gamma=gv, ring=wring,
                 slots=wslots)
    xsb = p.sb("r1_xfm", [128, NC * T], BF16)
    xr = Ring(p, "r1_x", [128, D], F32, n=2)
    pr = Ring(p, "r1_p", [128, D], F32, n=2)
    s1b = Ring(p, "r1_s1", [128, D], F32, n=2)
    s2b = Ring(p, "r1_s2", [128, D], F32, n=2)
    st = Ring(p, "r1_st", [128, 8], F32, n=3)
    acc = p.sb("r1_acc", [128, 2], F32)
    psb = p.sb("r1_psb", [1, 2], F32)
    accp = p.ps("r1_accp", [128, 8])
    zacc = p.v(lambda v: v.memset(acc[:], 0.0))
    atok = zacc
    xload = []
    for b in range(B):
        t_x = p.dma("sync", xsb[:].rearrange("p (c t) -> p c t", c=NC),
                    xn_fm[b].rearrange("c p t -> p c t"), after=xload)
        for i in range(NT):
            xt = xr.next()
            lx = p.dma("sync", xt[:], xn_tm[b, i * 128:(i + 1) * 128, :],
                       after=xr.guard)
            pt = pr.next()
            lp = p.dma("scalar", pt[:], proc_tm[b, i * 128:(i + 1) * 128, :],
                       after=pr.guard)
            stb = st.next()
            s1 = s1b.next()
            s2 = s2b.next()
            p.wait("vector", st.guard)
            t1 = p.v(lambda v, pt=pt, xt=xt, s1=s1: v.tensor_sub(s1[:], pt[:], xt[:]),
                     after=[lx, lp] + s1b.guard)
            t2 = p.s(lambda s, s1=s1, s2=s2: s.activation(s2[:], s1[:], AF.Square),
                     after=[t1] + s2b.guard)
            t3 = p.v(lambda v, s2=s2, stb=stb: v.reduce_sum(stb[:, 0:1], s2[:],
                                                            axis=AX.X), after=[t2])
            pr.set_guard([t1])
            t4 = p.s(lambda s, xt=xt, s2=s2: s.activation(s2[:], xt[:], AF.Square),
                     after=[t3, lx])
            t5 = p.v(lambda v, s2=s2, stb=stb: v.reduce_sum(stb[:, 1:2], s2[:],
                                                            axis=AX.X), after=[t4])
            s2b.set_guard([t5])
            cc = p.constcol(1e-6)
            t6 = p.s(lambda s, stb=stb, cc=cc: s.activation(
                stb[:, 2:3], stb[:, 1:2], AF.Sqrt, scale=1.0 / D, bias=cc),
                after=[t5])
            t7 = p.v(lambda v, stb=stb: v.reciprocal(stb[:, 2:3], stb[:, 2:3]),
                     after=[t6])
            mms = []
            for half in range(4):
                for c in range(NC):
                    mm = p.t(lambda t, half=half, c=c, i=i, W=W, xsb=xsb: t.matmul(
                        psA[:, half * 512:(half + 1) * 512],
                        xsb[:, c * T + i * 128:c * T + i * 128 + 128],
                        W.chunks[c][:, half * 512:(half + 1) * 512],
                        start=(c == 0), stop=(c == NC - 1)),
                        after=[t_x] + (W.tok if b == 0 and i == 0 else []) + [atok],
                        sync=(c == NC - 1))
                mms.append(mm)
            rm = p.v(lambda v, s1=s1, s2=s2, stb=stb: v.scalar_tensor_tensor(
                out=s2[:], in0=psA[:, 0:1024], scalar=stb[:, 2:3], in1=s1[:],
                op0=ALU.mult, op1=ALU.subtract_rev if hasattr(ALU, "subtract_rev")
                else ALU.subtract), after=mms[0:2] + [t7, t3])
            sq = p.s(lambda s, s2=s2: s.activation(s2[:], s2[:], AF.Square),
                     after=[rm])
            lvt = p.v(lambda v, s1=s1, stb=stb: v.tensor_scalar(
                out=s1[:], in0=psA[:, 1024:2048], scalar1=stb[:, 2:3], scalar2=None,
                op0=ALU.mult), after=mms[2:4] + [t7, rm])
            s1b.set_guard([lvt])
            ex = p.s(lambda s, s1=s1, s2=s2, stb=stb: s.activation(
                s2[:], s2[:], AF.Copy), after=[sq]) if False else None
            eb = p.v(lambda v, s1=s1: None, after=[]) if False else None
            te = p.s(lambda s, xt=xt, s1=s1: s.activation(xt[:], s1[:], AF.Exp,
                                                          scale=-1.0),
                     after=[lvt, t4])
            u1 = p.v(lambda v, s2=s2, xt=xt: v.tensor_scalar(
                out=s2[:], in0=s2[:], scalar1=1.0, scalar2=None, op0=ALU.add),
                after=[sq, te])
            u2 = p.v(lambda v, s2=s2, xt=xt: v.tensor_mul(s2[:], s2[:], xt[:]),
                     after=[u1])
            xr.set_guard([te, u2])
            u3 = p.v(lambda v, s2=s2, s1=s1: v.tensor_add(s2[:], s2[:], s1[:]),
                     after=[u2, lvt])
            u4 = p.v(lambda v, s2=s2, stb=stb: v.reduce_sum(stb[:, 3:4], s2[:],
                                                            axis=AX.X), after=[u3])
            d1 = p.v(lambda v, stb=stb: v.tensor_scalar_mul(
                stb[:, 4:5], stb[:, 0:1], 1.0 / D), after=[u4])
            d2 = p.v(lambda v, stb=stb: v.tensor_scalar(
                out=stb[:, 5:6], in0=stb[:, 3:4], scalar1=0.5 / D, scalar2=-0.5,
                op0=ALU.mult, op1=ALU.add), after=[d1])
            sd = p.dma("gpsimd", dd[b, 0, i, :].unsqueeze(1), stb[:, 4:5],
                       after=[d2])
            sd2 = p.dma("gpsimd", dd[b, 1, i, :].unsqueeze(1), stb[:, 5:6],
                        after=[d2])
            a1 = p.v(lambda v, stb=stb: v.tensor_add(acc[:], acc[:],
                                                     stb[:, 4:6]), after=[d2, atok])
            atok = a1
            st.set_guard([sd, sd2, a1])
        xload = [mms[-1]]
    pm = p.t(lambda t: t.matmul(accp[0:1, 0:2], onescol[:], acc[:, 0:2],
                                start=True, stop=True), after=[atok, t_oc])
    pe = p.s(lambda s: s.activation(partial_sb[0:1, 0:2], accp[0:1, 0:2], AF.Copy),
             after=[pm]) if False else None
    pev = p.s(lambda s: s.activation(psb[0:1, 0:2], accp[0:1, 0:2], AF.Copy),
              after=[pm])
    p.dma("gpsimd", partial, psb[0:1, 0:2], after=[pev])
    return


def _hook_r1_psb(p):
    return p.sb("r1_psb", [1, 2], F32)



def emit_router2(p, aps):
    dd, allpart, proc_tm = aps["dd"], aps["allpart"], aps["proc_tm"]
    ST_d, S_d, scores_d = aps["ST"], aps["Smat"], aps["scores_sm"]
    ropec2, ropes2 = aps["ropec2"], aps["ropes2"]
    sel_tm, sel_fm = aps["sel_tm"], aps["sel_fm"]
    phi_d, plo_d = aps["phi"], aps["plo"]
    grow_d = p.dram("r2_grow", [B, 3, T], F32)
    psA = p.ps("r2_psA", [128, 2048])
    psB = p.ps("r2_psB", [128, 1024])
    identf = p.sb("r2_idf", [128, 128], F32)
    ld0 = p.dma("sync", identf[:], aps["c_ident_f32"])
    ones1f = p.sb("r2_ones1", [1, 128], F32)
    t1_ = p.v(lambda v: v.memset(ones1f[:], 1.0))
    ibc = p.sb("r2_ibc", [128, 512], F32)
    iota_r = p.sb("r2_iotar", [1, 512], F32)
    ld1 = p.dma("sync", iota_r[:], aps["c_iota512_row"])
    iota_c = p.sb("r2_iotac", [128, 1], F32)
    ld2 = p.dma("sync", iota_c[:], aps["c_iota128_col"])
    ctab = p.sb("r2_ctab", [128, NT * DH], BF16)
    stab = p.sb("r2_stab", [128, NT * DH], BF16)
    ld3 = p.dma("sync", ctab[:].rearrange("p (i e) -> p i e", i=NT),
                aps["c_ropeb_cos_bf"].rearrange("(i p) e -> p i e", p=128))
    ld4 = p.dma("sync", stab[:].rearrange("p (i e) -> p i e", i=NT),
                aps["c_ropeb_sin_bf"].rearrange("(i p) e -> p i e", p=128))
    p.init_toks += [ld0, t1_, ld1, ld2, ld3, ld4]
    mm = p.t(lambda t: t.matmul(psB[:, 0:512], ones1f[:], iota_r[:],
                                start=True, stop=True))
    tib = p.s(lambda s: s.activation(ibc[:], psB[:, 0:512], AF.Copy), after=[mm])
    p.init_toks.append(tib)
    if allpart.shape[0] == 8:
        ap8 = p.sb("r2_ap8", [8, 2], F32)
        lda = p.dma("sync", ap8[:], allpart)
        onc8 = p.sb("r2_onc8", [8, 1], F32)
        t8_ = p.v(lambda v: v.memset(onc8[:], 1.0))
        mm2 = p.t(lambda t: t.matmul(psB[0:1, 0:2], onc8[0:8, 0:1],
                                     ap8[0:8, 0:2], start=True, stop=True),
                  after=[lda, t8_, tib])
        nm1 = p.s(lambda s: s.activation(negm[0:1, 0:2], psB[0:1, 0:2], AF.Copy,
                                         scale=-1.0 / (16 * T)), after=[mm2])
    else:
        ap1 = p.sb("r2_ap1", [1, 2], F32)
        lda = p.dma("sync", ap1[:], allpart, after=aps.get("allpart_tok", ()))
        nm1 = p.v(lambda v: v.tensor_scalar_mul(
            negm[0:1, 0:2], ap1[0:1, 0:2], -1.0 / (16 * T)), after=[lda, tib])
    nm2 = p.v(lambda v: v.reduce_sum(negm[0:1, 2:3], negm[0:1, 0:2], axis=AX.X),
              after=[nm1])
    mm3 = p.t(lambda t: t.matmul(psB[0:16, 2:3], ones1f[0:1, 0:16],
                                 negm[0:1, 2:3], start=True, stop=True),
              after=[nm2])
    nb = p.s(lambda s: s.activation(bias16[0:16, 0:1], psB[0:16, 2:3], AF.Copy),
             after=[mm3])

    negm = p.sb("r2_negm", [1, 4], F32)
    bias16 = p.sb("r2_b16", [16, 1], F32)
    grown = p.sb("r2_grown", [1, T], F32)
    scr2b = p.sb("r2_scr2b", [128, 2048], BF16)
    scomb = p.sb("r2_scom", [128, 4], F32)
    rowb = p.sb("r2_rowb", [128, 64], F32)
    selbf = p.sb("r2_selbf", [128, D], BF16)
    seltp = p.sb("r2_seltp", [128, D], BF16)
    gcol = p.sb("r2_gcol", [128, NT], F32)
    mcol = p.sb("r2_mcol", [128, NT], F32)
    ccol = p.sb("r2_ccol", [128, NT], F32)
    gil = p.sb("r2_gil", [16, 128], F32)
    gbc = p.sb("r2_gbc", [128, 2048], F32)
    mbc = p.sb("r2_mbc", [128, 2048], F32)
    cbc = p.sb("r2_cbc", [128, 2048], F32)
    scr = p.sb("r2_scr", [128, 2048], F32)
    scr2 = p.sb("r2_scr2", [128, 2048], F32)
    STsb = [p.sb(f"r2_stt{i}", [128, 512], BF16) for i in range(NT)]
    phisb = p.sb("r2_phi", [128, NT * D], BF16)
    plosb = p.sb("r2_plo", [128, NT * D], BF16)
    prr = Ring(p, "r2_pr", [128, D], F32, n=2)
    stw = Ring(p, "r2_stw", [128, 4], F32, n=2)

    def row_bcast(dst, row_ap, after, tag2):
        toks = []
        for s0 in range(0, 2048, 512):
            m = p.t(lambda t, s0=s0: t.matmul(
                psA[:, s0:s0 + 512], ones1f[:], row_ap[0:1, s0:s0 + 512],
                start=True, stop=True), after=after)
            toks.append(p.s(lambda s, s0=s0: s.activation(
                dst[:, s0:s0 + 512], psA[:, s0:s0 + 512], AF.Copy), after=[m]))
        return toks

    bat_g = []
    for b in range(B):
        dd0 = p.sb(f"r2_dd{b}", [16, 2 * 128], F32)
        l1 = p.dma("sync", dd0[0:16, 0:128], dd[b, 0], after=bat_g)
        l2 = p.dma("sync", dd0[0:16, 128:256], dd[b, 1], after=bat_g)
        g1 = p.v(lambda v, dd0=dd0: v.tensor_add(
            gil[0:16, :], dd0[0:16, 0:128], dd0[0:16, 128:256]),
            after=[l1, l2, nb] + bat_g)
        g2 = p.s(lambda s: s.activation(gil[0:16, :], gil[0:16, :], AF.Sigmoid,
                                        bias=bias16[0:16, 0:1]), after=[g1])
        sg = p.dma("gpsimd", grow_d[b, 0].rearrange("(i p) -> i p", p=128),
                   gil[0:16, :], after=[g2])
        tm1 = p.t(lambda t: t.matmul(psB[:, 0:128], gil[0:16, 0:128],
                                     identf[0:16, :], start=True, stop=True,
                                     is_transpose=True), after=[g2]) \
            if False else p.t(lambda t: t.transpose(
                psB[:, 0:16], gil[0:16, 0:128], identf[0:16, 0:16]), after=[g2])
        ev1 = p.s(lambda s: s.activation(gcol[:, 0:NT], psB[:, 0:NT], AF.Copy),
                  after=[tm1])
        lrow = p.dma("sync", grown[0:1, :], grow_d[b, 0].unsqueeze(0), after=[sg])
        gb_t = row_bcast(gbc, grown, [lrow, ev1], "g")
        mtoks = []
        for i in range(NT):
            c1 = p.v(lambda v, i=i: v.tensor_scalar(
                out=scr[:], in0=gbc[:], scalar1=gcol[:, i:i + 1], scalar2=None,
                op0=ALU.is_gt), after=gb_t + [ev1] + mtoks[-1:])
            c2 = p.v(lambda v, i=i: v.tensor_scalar(
                out=scr2[:], in0=gbc[:], scalar1=gcol[:, i:i + 1], scalar2=None,
                op0=ALU.is_equal), after=gb_t + [c1])
            c3 = p.g(lambda g, i=i: g.affine_select(
                out=scr2[:], in_=scr2[:], pattern=[[-1, 2048]],
                compare_op=ALU.is_gt, fill=0.0, base=i * 128,
                channel_multiplier=1), after=[c2])
            c4 = p.v(lambda v: v.tensor_add(scr[:], scr[:], scr2[:]),
                     after=[c3])
            stb = stw.next()
            c5 = p.v(lambda v, stb=stb: v.reduce_sum(stb[:, 0:1], scr[:], axis=AX.X),
                     after=[c4] + stw.guard)
            c6 = p.v(lambda v, stb=stb, i=i: v.tensor_scalar(
                out=mcol[:, i:i + 1], in0=stb[:, 0:1], scalar1=512.0, scalar2=None,
                op0=ALU.is_lt), after=[c5])
            stw.set_guard([c6])
            mtoks.append(c6)
        tm2 = p.t(lambda t: t.transpose(psB[0:16, 0:128], mcol[:, 0:NT],
                                        identf[:]), after=mtoks)
        ev2 = p.s(lambda s: s.activation(gil[0:16, :], psB[0:16, 0:128], AF.Copy),
                  after=[tm2])
        sm = p.dma("gpsimd", grow_d[b, 1].rearrange("(i p) -> i p", p=128),
                   gil[0:16, :], after=[ev2])
        lrow2 = p.dma("sync", grown[0:1, :], grow_d[b, 1].unsqueeze(0), after=[sm])
        mb_t = row_bcast(mbc, grown, [lrow2], "m")
        ctoks = []
        for i in range(NT):
            c1 = p.g(lambda g, i=i: g.affine_select(
                out=scr[:], in_=mbc[:], pattern=[[-1, 2048]], compare_op=ALU.is_gt,
                fill=0.0, base=i * 128, channel_multiplier=1),
                after=mb_t + ctoks[-1:])
            stb = stw.next()
            c5 = p.v(lambda v, stb=stb: v.reduce_sum(stb[:, 1:2], scr[:], axis=AX.X),
                     after=[c1] + stw.guard)
            c6 = p.v(lambda v, stb=stb, i=i: v.tensor_copy(ccol[:, i:i + 1],
                                                           stb[:, 1:2]), after=[c5])
            stw.set_guard([c6])
            ctoks.append(c6)
        st_toks = []
        for i in range(NT):
            t1 = p.v(lambda v, i=i: v.tensor_scalar(
                out=STsb[i][:], in0=ibc[:], scalar1=ccol[:, i:i + 1],
                scalar2=mcol[:, i:i + 1], op0=ALU.is_equal, op1=ALU.mult),
                after=ctoks + [mtoks[i]] + bat_g)
            so = p.dma("gpsimd", ST_d[b, i], STsb[i][:], after=[t1])
            st_toks.append(t1)
            st_toks.append(so)
        tm3 = p.t(lambda t: t.transpose(psB[0:16, 0:128], ccol[:, 0:NT],
                                        identf[:]), after=ctoks)
        ev3 = p.s(lambda s: s.activation(gil[0:16, :], psB[0:16, 0:128], AF.Copy),
                  after=[tm3, sm])
        sc_ = p.dma("gpsimd", grow_d[b, 2].rearrange("(i p) -> i p", p=128),
                    gil[0:16, :], after=[ev3])
        lrow3 = p.dma("sync", grown[0:1, :], grow_d[b, 2].unsqueeze(0), after=[sc_])
        cb_t = row_bcast(cbc, grown, [lrow3], "c")
        s_toks = []
        for s4 in range(4):
            t1 = p.v(lambda v, s4=s4: v.tensor_scalar(
                out=scr[:], in0=cbc[:], scalar1=float(s4 * 128),
                scalar2=iota_c[:, 0:1], op0=ALU.subtract, op1=ALU.is_equal),
                after=cb_t + s_toks[-2:])
            t2 = p.v(lambda v, s4=s4: v.tensor_mul(scr2b[:], scr[:], mbc[:]),
                     after=[t1] + mb_t)
            so = p.dma("gpsimd", S_d[b, s4], scr2b[:], after=[t2])
            t3 = p.v(lambda v: v.tensor_mul(scr[:], scr2b[:], gbc[:]),
                     after=[t2] + gb_t)
            stb = stw.next()
            t4 = p.v(lambda v, stb=stb: v.reduce_sum(stb[:, 2:3], scr[:], axis=AX.X),
                     after=[t3] + stw.guard)
            t5 = p.v(lambda v, stb=stb, s4=s4: v.tensor_copy(
                scomb[:, s4:s4 + 1], stb[:, 2:3]), after=[t4])
            stw.set_guard([t5])
            s_toks += [so, t5]
        ssc = p.dma("gpsimd", scores_d[b], scomb[:, 0:4], after=s_toks)
        rg_toks = []
        for (tab, dst) in ((ctab, ropec2), (stab, ropes2)):
            for s4 in range(4):
                mms = []
                for c in range(NT):
                    mmx = p.t(lambda t, c=c, s4=s4, tab=tab: t.matmul(
                        psB[:, 0:64], STsb[c][:, s4 * 128:(s4 + 1) * 128],
                        tab[:, c * DH:(c + 1) * DH],
                        start=(c == 0), stop=(c == NT - 1)),
                        after=st_toks + [ev3], sync=(c == NT - 1))
                ev = p.s(lambda s, s4=s4, dst=dst: s.activation(
                    rowb[:, 0:64], psB[:, 0:64], AF.Copy),
                    after=[mmx] + rg_toks[-1:])
                so = p.dma("gpsimd", dst[b, s4 * 128:(s4 + 1) * 128, :],
                           rowb[:, 0:64], after=[ev])
                p.wait("tensor", [ev])
                rg_toks.append(so)
        ph_toks = []
        for i in range(NT):
            pt = prr.next()
            lp = p.dma("scalar", pt[:], proc_tm[b, i * 128:(i + 1) * 128, :],
                       after=prr.guard)
            h1 = p.s(lambda s, pt=pt, i=i: s.activation(
                phisb[:, i * D:(i + 1) * D], pt[:], AF.Copy), after=[lp] + bat_g)
            h2 = p.v(lambda v, pt=pt, i=i: v.tensor_sub(
                plosb[:, i * D:(i + 1) * D], pt[:], phisb[:, i * D:(i + 1) * D]),
                after=[h1])
            prr.set_guard([h2])
            so1 = p.dma("gpsimd", phi_d[b, i * 128:(i + 1) * 128, :],
                        phisb[:, i * D:(i + 1) * D], after=[h1])
            so2 = p.dma("gpsimd", plo_d[b, i * 128:(i + 1) * 128, :],
                        plosb[:, i * D:(i + 1) * D], after=[h2])
            ph_toks += [so1, so2, h2]
        selstores = []
        for s4 in range(4):
            mms = []
            for half in range(2):
                first = True
                for c in range(NT):
                    for (src_, last) in ((phisb, False), (plosb, c == NT - 1)):
                        mmx = p.t(lambda t, c=c, s4=s4, half=half, src_=src_,
                                  last=last: t.matmul(
                            psA[:, half * 512:(half + 1) * 512],
                            STsb[c][:, s4 * 128:(s4 + 1) * 128],
                            src_[:, c * D + half * 512:c * D + half * 512 + 512],
                            start=(c == 0 and src_ is phisb), stop=last),
                            after=ph_toks + st_toks, sync=last)
                mms.append(mmx)
            pt = prr.next()
            ev = p.v(lambda v, pt=pt: v.tensor_copy(pt[:], psA[:, 0:1024]),
                     after=mms + prr.guard)
            p.wait("tensor", [ev])
            so1 = p.dma("gpsimd", sel_tm[b, s4 * 128:(s4 + 1) * 128, :], pt[:],
                        after=[ev])
            cb2 = p.s(lambda s, pt=pt: s.activation(selbf[:], pt[:], AF.Copy),
                      after=[ev] + selstores[-1:])
            o2 = store_fm(p, selbf, sel_fm[b], s4, [cb2], seltp, nch=NC)
            prr.set_guard([so1, o2])
            selstores += [so1, o2]
        bat_g = selstores + rg_toks + [ssc] + st_toks
    return


def build_router1():
    p = Prog("r1")
    aps = {
        "xn_tm": p.dram("xn_tm", [B, T, D], F32, kind="ExternalInput"),
        "xn_fm": p.dram("xn_fm", [B, NC, 128, T], BF16, kind="ExternalInput"),
        "proc_tm": p.dram("proc_tm", [B, T, D], F32, kind="ExternalInput"),
        "prior_w": p.dram("prior_w", [D, 2 * D], F32, kind="ExternalInput"),
        "prior_norm_w": p.dram("prior_norm_w", [D], F32, kind="ExternalInput"),
        "dd": p.dram("dd", [B, 2, NT, 128], F32, kind="ExternalOutput"),
        "partial": p.dram("partial", [1, 2], F32, kind="ExternalOutput"),
    }
    emit_router1(p, aps)
    return p


def build_router2():
    p = Prog("r2")
    aps = {
        "dd": p.dram("dd", [B, 2, NT, 128], F32, kind="ExternalInput"),
        "allpart": p.dram("allpart", [8, 2], F32, kind="ExternalInput"),
        "proc_tm": p.dram("proc_tm", [B, T, D], F32, kind="ExternalInput"),
        "ST": p.dram("ST", [B, NT, 128, 512], BF16, kind="ExternalOutput"),
        "Smat": p.dram("Smat", [B, 4, 128, T], BF16, kind="ExternalOutput"),
        "scores_sm": p.dram("scores_sm", [B, 128, 4], F32, kind="ExternalOutput"),
        "ropec2": p.dram("ropec2", [B, 512, DH], F32, kind="ExternalOutput"),
        "ropes2": p.dram("ropes2", [B, 512, DH], F32, kind="ExternalOutput"),
        "sel_tm": p.dram("sel_tm", [B, 512, D], F32, kind="ExternalOutput"),
        "sel_fm": p.dram("sel_fm", [B, NC, 128, 512], BF16, kind="ExternalOutput"),
        "phi": p.dram("phi", [B, T, D], BF16, kind="ExternalOutput"),
        "plo": p.dram("plo", [B, T, D], BF16, kind="ExternalOutput"),
        "c_ident_f32": const_ap(p, "ident_f32"),
        "c_iota512_row": const_ap(p, "iota512_row"),
        "c_iota128_col": const_ap(p, "iota128_col"),
        "c_ropeb_cos_bf": const_ap(p, "ropeb_cos_bf"),
        "c_ropeb_sin_bf": const_ap(p, "ropeb_sin_bf"),
    }
    emit_router2(p, aps)
    return p



def emit_delta(p, aps):
    y_tm, sel_tm, scores_sm = aps["y_tm"], aps["sel_tm"], aps["scores_sm"]
    dhi, dlo = aps["dhi"], aps["dlo"]
    yr = Ring(p, "dl_y", [128, D], F32, n=2)
    sr = Ring(p, "dl_s", [128, D], F32, n=2)
    hb = Ring(p, "dl_h", [128, D], BF16, n=2)
    sc = p.sb("dl_sc", [128, 4], F32)
    for b in range(B):
        ls = p.dma("sync", sc[:], scores_sm[b])
        for s4 in range(4):
            yt = yr.next()
            l1 = p.dma("sync", yt[:], y_tm[b, s4 * 128:(s4 + 1) * 128, :],
                       after=yr.guard)
            st_ = sr.next()
            l2 = p.dma("scalar", st_[:], sel_tm[b, s4 * 128:(s4 + 1) * 128, :],
                       after=sr.guard)
            t1 = p.v(lambda v, yt=yt, st_=st_: v.tensor_sub(yt[:], yt[:], st_[:]),
                     after=[l1, l2])
            t2 = p.v(lambda v, yt=yt, s4=s4: v.tensor_scalar(
                out=yt[:], in0=yt[:], scalar1=sc[:, s4:s4 + 1], scalar2=None,
                op0=ALU.mult), after=[t1, ls])
            sr.set_guard([t1])
            h1b = hb.next()
            h1 = p.s(lambda s, yt=yt, h1b=h1b: s.activation(h1b[:], yt[:], AF.Copy),
                     after=[t2] + hb.guard)
            so1 = p.dma("gpsimd", dhi[b, s4 * 128:(s4 + 1) * 128, :], h1b[:],
                        after=[h1])
            t3 = p.v(lambda v, yt=yt, h1b=h1b: v.tensor_sub(yt[:], yt[:], h1b[:]),
                     after=[h1])
            yr.set_guard([t3, so1])
            hb.set_guard([so1, t3])
            h2b = hb.next()
            h2 = p.s(lambda s, yt=yt, h2b=h2b: s.activation(h2b[:], yt[:], AF.Copy),
                     after=[t3] + hb.guard)
            so2 = p.dma("gpsimd", dlo[b, s4 * 128:(s4 + 1) * 128, :], h2b[:],
                        after=[h2])
            hb.set_guard([so2])
    return



def emit_combine(p, aps):
    x0, o_fm, Wo_d = aps["x0"], aps["o_fm"], aps["Wo"]
    proc_tm, S_d = aps["proc_tm"], aps["Smat"]
    dhi, dlo = aps["dhi"], aps["dlo"]
    x1_tm, xn1_tm, xn1_fm = aps["x1_tm"], aps["xn1_tm"], aps["xn1_fm"]
    psA = p.ps("cb_psA", [128, 1024])
    wring = Ring(p, "cb_wld", [128, 1024], F32, n=2)
    Wo = FMWeight(p, "cb_wo", Wo_d, D, D, ring=wring)
    osb = p.sb("cb_osb", [128, NC * T], BF16)
    Ssb = [p.sb(f"cb_S{s4}", [128, T], BF16) for s4 in range(4)]
    dsb = [p.sb(f"cb_d{s4}", [128, 2 * D], BF16) for s4 in range(4)]
    xr = Ring(p, "cb_x", [128, D], F32, n=2)
    pr = Ring(p, "cb_p", [128, D], F32, n=2)
    st = Ring(p, "cb_st", [128, 4], F32, n=3)
    sqr = Ring(p, "cb_sq", [128, D], F32, n=2)
    cbb = Ring(p, "cb_bb", [128, D], BF16, n=2)
    tpb = Ring(p, "cb_tp", [128, D], BF16, n=2)
    bat_g = []
    for b in range(B):
        lo_ = p.dma("sync", osb[:].rearrange("p (c t) -> p c t", c=NC),
                    o_fm[b].rearrange("c p t -> p c t"), after=bat_g)
        lS = [p.dma("sync", Ssb[s4][:], S_d[b, s4], after=bat_g) for s4 in range(4)]
        lD = []
        for s4 in range(4):
            lD.append(p.dma("scalar", dsb[s4][:, 0:D],
                            dhi[b, s4 * 128:(s4 + 1) * 128, :], after=bat_g))
            lD.append(p.dma("scalar", dsb[s4][:, D:2 * D],
                            dlo[b, s4 * 128:(s4 + 1) * 128, :], after=bat_g))
        for i in range(NT):
            mms = []
            for half in range(2):
                for c in range(NC):
                    mm = p.t(lambda t, c=c, i=i, half=half, Wo=Wo: t.matmul(
                        psA[:, half * 512:(half + 1) * 512],
                        osb[:, c * T + i * 128:c * T + i * 128 + 128],
                        Wo.chunks[c][:, half * 512:(half + 1) * 512],
                        start=(c == 0), stop=False),
                        after=[lo_] + (Wo.tok if b == 0 and i == 0 else []),
                        sync=False)
                for s4 in range(4):
                    for (off, last) in ((0, False), (D, s4 == 3)):
                        mm = p.t(lambda t, s4=s4, i=i, half=half, off=off,
                                 last=last: t.matmul(
                            psA[:, half * 512:(half + 1) * 512],
                            Ssb[s4][:, i * 128:(i + 1) * 128],
                            dsb[s4][:, off + half * 512:off + half * 512 + 512],
                            start=False, stop=last),
                            after=lS + lD, sync=last)
                mms.append(mm)
            xt = xr.next()
            lx = p.dma("sync", xt[:], x0[b, i * 128:(i + 1) * 128, :],
                       after=xr.guard)
            pt = pr.next()
            lp = p.dma("scalar", pt[:], proc_tm[b, i * 128:(i + 1) * 128, :],
                       after=pr.guard)
            a1 = p.v(lambda v, xt=xt, pt=pt: v.tensor_add(xt[:], xt[:], pt[:]),
                     after=[lx, lp])
            a2 = p.v(lambda v, xt=xt: v.tensor_add(xt[:], xt[:], psA[:, 0:1024]),
                     after=[a1] + mms)
            pr.set_guard([a1])
            p.wait("tensor", [a2])
            sx = p.dma("gpsimd", x1_tm[b, i * 128:(i + 1) * 128, :], xt[:],
                       after=[a2])
            stb = st.next()
            sqb = sqr.next()
            p.wait("vector", st.guard + sqr.guard)
            tok, _ = ln_tile(p, xt, stb, sqb, after=[a2, sx])
            st.set_guard([tok])
            sqr.set_guard([tok])
            sx2 = p.dma("gpsimd", xn1_tm[b, i * 128:(i + 1) * 128, :], xt[:],
                        after=[tok])
            ob = cbb.next()
            cb1 = p.s(lambda s, ob=ob, xt=xt: s.activation(ob[:], xt[:], AF.Copy),
                      after=[tok] + cbb.guard)
            xr.set_guard([sx, sx2, cb1])
            tmp = tpb.next()
            p.wait("sync", tpb.guard)
            o2 = store_fm(p, ob, xn1_fm[b], i, [cb1], tmp, nch=NC)
            cbb.set_guard([o2])
            tpb.set_guard([o2])
            bat_g = [sx, sx2, o2, a2]
    return



def emit_l1attn(p, aps):
    xn1_fm, wqkv_d, wout_d, bout_d = aps["xn1_fm"], aps["w_qkv"], aps["w_out"], aps["b_out"]
    a1_tm = aps["a1_tm"]
    psA = p.ps("f_psA", [128, 2048])
    psB = p.ps("f_psB", [128, 2048])
    maskb = p.sb("f_maskb", [128, 128], F32)
    cmk = p.dma("sync", maskb[:], aps["c_maskbias_causal"])
    p.init_toks.append(cmk)
    q_fm = p.dram("f_qfm", [B, 8, 128, T], BF16)
    k_fm = p.dram("f_kfm", [B, 8, 128, T], BF16)
    v_tm = p.dram("f_vtm", [B, T, D], BF16)
    wslots = ([p.sb(f"f_ws{i}", [128, 1024], BF16) for i in range(NC)],
              {"toks": []})
    wring = Ring(p, "f_wld", [128, 1024], F32, n=2)
    xsb = p.sb("f_x", [128, NC * T], BF16)
    bb = Ring(p, "f_bb", [128, 1024], BF16, n=2)
    tpb = Ring(p, "f_tp", [128, 1024], BF16, n=2)
    xtf = Ring(p, "f_xt", [128, D], F32, n=2)
    ones1f = p.sb("f_ones1", [1, 128], F32)
    t_o1 = p.v(lambda v: v.memset(ones1f[:], 1.0))
    brow = p.sb("f_brow", [1, D], F32)
    lb = p.dma("sync", brow[:], bout_d.unsqueeze(0))
    bbc = p.sb("f_bbc", [128, D], F32)
    bts = []
    for s0 in range(0, D, 512):
        m = p.t(lambda t, s0=s0: t.matmul(psB[:, s0:s0 + 512], ones1f[:],
                                          brow[0:1, s0:s0 + 512], start=True,
                                          stop=True), after=[lb, t_o1])
        bts.append(p.s(lambda s, s0=s0: s.activation(
            bbc[:, s0:s0 + 512], psB[:, s0:s0 + 512], AF.Copy), after=[m]))
    p.init_toks += bts
    bat_g = []
    for b in range(B):
        t_x = p.dma("sync", xsb[:].rearrange("p (c t) -> p c t", c=NC),
                    xn1_fm[b].rearrange("c p t -> p c t"), after=bat_g)
        stores = {"q": [], "k": [], "v": []}
        for (nm, coff, dst) in (("q", 0, q_fm), ("k", D, k_fm), ("v", 2 * D, v_tm)):
            W = FMWeight(p, f"f_{nm}w{b}",
                         wqkv_d[:, coff:coff + D], D, D, ring=wring, slots=wslots)
            mlast = []
            evp = [None, None]
            for i in range(NT):
                po = (i % 2) * 1024
                mms = []
                for half in range(2):
                    for c in range(NC):
                        mm = p.t(lambda t, c=c, i=i, half=half, W=W, xsb=xsb,
                                 po=po: t.matmul(
                            psA[:, po + half * 512:po + (half + 1) * 512],
                            xsb[:, c * T + i * 128:c * T + i * 128 + 128],
                            W.chunks[c][:, half * 512:(half + 1) * 512],
                            start=(c == 0), stop=(c == NC - 1)),
                            after=[t_x] + (W.tok if i == 0 else []) +
                            ([evp[i % 2]] if evp[i % 2] else []),
                            sync=(c == NC - 1))
                    mms.append(mm)
                mlast.append(mms[-1])
                ob = bb.next()
                ev = p.s(lambda s, ob=ob, po=po: s.activation(
                    ob[:], psA[:, po:po + 1024], AF.Copy),
                    after=mms + bb.guard)
                evp[i % 2] = ev
                if nm == "v":
                    so = p.dma("gpsimd", v_tm[b, i * 128:(i + 1) * 128, :], ob[:],
                               after=[ev])
                else:
                    tmp = tpb.next()
                    p.wait("sync", tpb.guard)
                    so = store_fm(p, ob, dst[b], i, [ev], tmp, nch=NC)
                    tpb.set_guard([so])
                bb.set_guard([so])
                stores[nm].append(so)
            wslots[1]["toks"] = mlast
        qf = p.sb(f"f_qf", [128, 8 * T], BF16) if b == 0 else qf
        kf = p.sb(f"f_kf", [128, 8 * T], BF16) if b == 0 else kf
        Eexp = [p.sb(f"f_Ee", [128, 2048], BF16),
                p.sb(f"f_Ee2", [128, 2048], BF16)] if b == 0 else Eexp
        Fsb = [p.sb(f"f_F", [128, 2048], BF16),
               p.sb(f"f_F2", [128, 2048], BF16)] if b == 0 else Fsb
        opair = p.sb(f"f_op", [128, 128], BF16) if b == 0 else opair
        oTb = p.sb(f"f_oT", [128, 128], BF16) if b == 0 else oTb
        stt_ = p.sb(f"f_stt", [128, 8], F32) if b == 0 else stt_
        o_fm = p.dram("f_ofm", [B, NC, 128, T], BF16) if b == 0 else o_fm
        lq = p.dma("sync", qf[:].rearrange("p (h t) -> p h t", h=8),
                   q_fm[b].rearrange("h p t -> p h t"),
                   after=stores["q"] + bat_g)
        lk = p.dma("sync", kf[:].rearrange("p (h t) -> p h t", h=8),
                   k_fm[b].rearrange("h p t -> p h t"),
                   after=stores["k"] + bat_g)
        vf = xsb
        lv = p.dma("scalar", vf[:].rearrange("p (i d) -> p i d", i=NT),
                   v_tm[b].rearrange("(i p) d -> p i d", p=128),
                   after=[mlast[-1]] + stores["v"] + bat_g)
        pe0 = [t for t in evp if t] or [lq]
        tEr = [pe0[-1], pe0[-1]]
        tOr = [pe0[0], pe0[0]]
        tFr = [lq, lq]
        prevT = [lq]
        prevTT = lq
        ostores = []
        for i in range(NT):
            span = (i + 1) * 128
            for hp in range(8):
                otoks = []
                for hh in range(2):
                    pp = (hp * 2 + hh) % 2
                    R = psA if pp == 0 else psB
                    q0 = pp * 4
                    qsl = qf[hh * 64:(hh + 1) * 64,
                              hp * T + i * 128:hp * T + i * 128 + 128]
                    mms = []
                    for s0 in range(0, span, 512):
                        s1 = min(s0 + 512, span)
                        mm = p.t(lambda t, qsl=qsl, s0=s0, s1=s1, hp=hp, hh=hh,
                                 R=R: t.matmul(
                            R[:, s0:s1], qsl,
                            kf[hh * 64:(hh + 1) * 64, hp * T + s0:hp * T + s1],
                            start=True, stop=True),
                            after=[lq, lk, tEr[pp], tOr[pp]], sync=(s1 == span))
                        mms.append(mm)
                    dm = p.v(lambda v, i=i, R=R: v.tensor_add(
                        R[:, i * 128:(i + 1) * 128],
                        R[:, i * 128:(i + 1) * 128], maskb[:]), after=mms)
                    nmx = p.v(lambda v, span=span, R=R, q0=q0: v.reduce_max(
                        stt_[:, q0:q0 + 1], R[:, 0:span], axis=AX.X, negate=True),
                        after=[dm])
                    sc2 = p.v(lambda v, q0=q0: v.tensor_scalar_mul(
                        stt_[:, q0:q0 + 1], stt_[:, q0:q0 + 1], 0.125),
                        after=[nmx])
                    ex = p.s(lambda s, span=span, R=R, q0=q0, pp=pp: s.activation(
                        Eexp[pp][:, 0:span], R[:, 0:span], AF.Exp,
                        bias=stt_[:, q0:q0 + 1],
                        scale=0.125, accum_out=stt_[:, q0 + 1:q0 + 2]),
                        after=[sc2, tFr[pp]])
                    tEr[pp] = ex
                    rc = p.v(lambda v, q0=q0: v.reciprocal(
                        stt_[:, q0 + 2:q0 + 3], stt_[:, q0 + 1:q0 + 2]),
                        after=[ex])
                    tF = p.dma("sync",
                               Fsb[pp][:, 0:span].rearrange("t (c p) -> t c p",
                                                            c=i + 1),
                               Eexp[pp][:, 0:span].rearrange("p (c t) -> p c t",
                                                             c=i + 1),
                               after=[ex], transpose=True)
                    tFr[pp] = tF
                    pvs = []
                    for c in range(i + 1):
                        mm = p.t(lambda t, c=c, hp=hp, hh=hh, i=i, R=R,
                                 pp=pp: t.matmul(
                            R[:, 0:64], Fsb[pp][:, c * 128:(c + 1) * 128],
                            vf[:, c * D + (hp * 2 + hh) * 64:
                               c * D + (hp * 2 + hh) * 64 + 64],
                            start=(c == 0), stop=(c == i)),
                            after=[tF, lv], sync=(c == i))
                        pvs.append(mm)
                    oe = p.v(lambda v, hh=hh, R=R, q0=q0: v.tensor_scalar(
                        out=opair[:, hh * 64:(hh + 1) * 64], in0=R[:, 0:64],
                        scalar1=stt_[:, q0 + 2:q0 + 3], scalar2=None,
                        op0=ALU.mult),
                        after=[pvs[-1], rc] + ([prevTT] if hh == 0 else []))
                    tOr[pp] = oe
                    otoks.append(oe)
                tT = p.dma("sync", oTb[:], opair[:], after=otoks + prevT,
                           transpose=True)
                prevTT = tT
                so = p.dma("gpsimd", o_fm[b][hp][:, i * 128:(i + 1) * 128], oTb[:],
                           after=[tT])
                prevT = [so]
                ostores.append(so)
        Wo = FMWeight(p, f"f_wo{b}", wout_d, D, D, ring=wring, slots=wslots)
        osb = qf
        lo_ = p.dma("sync", osb[:, 0:NC * T].rearrange("p (c t) -> p c t", c=NC),
                    o_fm[b].rearrange("c p t -> p c t"),
                    after=ostores + tOr + tEr + tFr)
        wom = []
        evp2 = [None, None]
        for i in range(NT):
            po = (i % 2) * 1024
            mms = []
            for half in range(2):
                for c in range(NC):
                    mm = p.t(lambda t, c=c, i=i, half=half, Wo=Wo, osb=osb,
                             po=po: t.matmul(
                        psA[:, po + half * 512:po + (half + 1) * 512],
                        osb[:, c * T + i * 128:c * T + i * 128 + 128],
                        Wo.chunks[c][:, half * 512:(half + 1) * 512],
                        start=(c == 0), stop=(c == NC - 1)),
                        after=[lo_] + Wo.tok +
                        ([evp2[i % 2]] if evp2[i % 2] else []),
                        sync=(c == NC - 1))
                mms.append(mm)
            wom.append(mms[-1])
            xt = xtf.next()
            ad = p.v(lambda v, xt=xt, po=po: v.tensor_add(
                xt[:], bbc[:], psA[:, po:po + 1024]), after=mms + xtf.guard)
            evp2[i % 2] = ad
            so = p.dma("gpsimd", a1_tm[b, i * 128:(i + 1) * 128, :], xt[:],
                       after=[ad])
            xtf.set_guard([so])
        wslots[1]["toks"] = wom
        bat_g = [so] + tOr + tEr
    return



L1H = 8
L1I = 512
L1HID = 2730


def emit_l1layer(p, aps, tag, li, x_tm, x_fm, out_tm, out_fm, final=None):
    wqkv, wo_d = aps["wqkv"], aps["wo"]
    qs_d, ks_d = aps["q_scale"], aps["k_scale"]
    lng, lnb = aps["ln_g"], aps["ln_b"]
    w1_d, b1_d, w2_d, b2_d = aps["w1"], aps["b1"], aps["w2"], aps["b2"]
    psA = p.ps(f"{tag}_psA", [128, 2048])
    psB = p.ps(f"{tag}_psB", [128, 2048])
    maskb = p.sb(f"{tag}_mkb", [128, 128], F32)
    maska = p.sb(f"{tag}_mka", [128, 128], F32)
    c1_ = p.dma("sync", maskb[:], aps["c_maskbias_causal"])
    c2_ = p.dma("sync", maska[:], aps["c_maskbias_anticausal"])
    xq = p.sb(f"{tag}_xqc", [128, NT * DH], BF16)
    xq2 = p.sb(f"{tag}_xqs", [128, NT * DH], BF16)
    xk = p.sb(f"{tag}_xkc", [128, NT * DH], BF16)
    xk2 = p.sb(f"{tag}_xks", [128, NT * DH], BF16)
    c3_ = p.dma("sync", xq[:].rearrange("p (i e) -> p i e", i=NT),
                aps["c_xpos_qc_bf"].rearrange("(i p) e -> p i e", p=128))
    c4_ = p.dma("sync", xq2[:].rearrange("p (i e) -> p i e", i=NT),
                aps["c_xpos_qs_bf"].rearrange("(i p) e -> p i e", p=128))
    c5_ = p.dma("sync", xk[:].rearrange("p (i e) -> p i e", i=NT),
                aps["c_xpos_kc_bf"].rearrange("(i p) e -> p i e", p=128))
    c6_ = p.dma("sync", xk2[:].rearrange("p (i e) -> p i e", i=NT),
                aps["c_xpos_ks_bf"].rearrange("(i p) e -> p i e", p=128))
    ones1f = p.sb(f"{tag}_ones1", [1, 128], F32)
    c7_ = p.v(lambda v: v.memset(ones1f[:], 1.0))
    p.init_toks += [c1_, c2_, c3_, c4_, c5_, c6_, c7_]

    def brow(name, dram_ap, n, reps):
        row = p.sb(f"{tag}_{name}r", [1, reps * n], F32)
        toks = [p.dma("sync", row[0:1, r * n:(r + 1) * n], dram_ap.unsqueeze(0))
                for r in range(reps)]
        return row, toks

    qsr, t1 = brow("qs", qs_d[li], DH, L1H)
    ksr, t2 = brow("ks", ks_d[li], DH, L1H)
    lgr, t3 = brow("lg", lng[li], D, 1)
    lbr, t4 = brow("lb", lnb[li], D, 1)
    qsb = p.sb(f"{tag}_qsb", [128, L1I], F32)
    ksb = p.sb(f"{tag}_ksb", [128, L1I], F32)
    lgb = p.sb(f"{tag}_lgb", [128, D], F32)
    lbb = p.sb(f"{tag}_lbb", [128, D], F32)
    bct = []
    for (dst, row, toks) in ((qsb, qsr, t1), (ksb, ksr, t2), (lgb, lgr, t3),
                             (lbb, lbr, t4)):
        for s0 in range(0, dst[:].shape[1], 512):
            m = p.t(lambda t, s0=s0, row=row: t.matmul(
                psB[:, s0 % 2048:s0 % 2048 + 512], ones1f[:],
                row[0:1, s0:s0 + 512], start=True, stop=True),
                after=toks + [c7_] + bct[-4:])
            bct.append(p.s(lambda s, s0=s0, dst=dst: s.activation(
                dst[:, s0:s0 + 512], psB[:, s0 % 2048:s0 % 2048 + 512], AF.Copy),
                after=[m]))
    p.init_toks += bct

    b1rowa = p.sb(f"{tag}_b1ra", [1, 512], F32)
    b1rowg = p.sb(f"{tag}_b1rg", [1, 512], F32)
    b2row = p.sb(f"{tag}_b2r", [1, D], F32)
    t6 = p.dma("sync", b2row[:], b2_d[li].unsqueeze(0))
    p.init_toks += [t6]

    wslots = ([p.sb(f"{tag}_ws{i}", [128, 1536], BF16) for i in range(NC)],
              {"toks": []})
    wring = Ring(p, f"{tag}_wld", [128, 1536], F32, n=2)
    xsb = p.sb(f"{tag}_x", [128, NC * T], BF16)
    q_fm = p.dram(f"{tag}_qfm", [B, 4, 128, T], BF16)
    k_fm = p.dram(f"{tag}_kfm", [B, 4, 128, T], BF16)
    v_tm = p.dram(f"{tag}_vtm", [B, T, L1I], BF16)
    o_fm = p.dram(f"{tag}_ofm", [B, 4, 128, T], BF16)
    xm_tm = p.dram(f"{tag}_xm", [B, T, D], F32)
    xm_fm = p.dram(f"{tag}_xmfm", [B, NC, 128, T], BF16)
    h_fm = p.dram(f"{tag}_hfm", [B, NC, 128, T], BF16)
    ff_d = p.dram(f"{tag}_ffd", [B, NT, 128, 2816], BF16)
    bb = Ring(p, f"{tag}_bb", [128, 1536], BF16, n=2)
    tpb = Ring(p, f"{tag}_tp", [128, 1024], BF16, n=2)
    xtf = Ring(p, f"{tag}_xt", [128, D], F32, n=2)
    sqr = Ring(p, f"{tag}_sq", [128, D], F32, n=1)
    st = Ring(p, f"{tag}_st", [128, 8], F32, n=2)
    qv = Ring(p, f"{tag}_qv", [128, L1I], F32, n=1)
    rv = Ring(p, f"{tag}_rv", [128, L1I], F32, n=1)
    sqtmp = p.sb(f"{tag}_sqt", [128, 512], F32)
    w1asl = [wslots[0][c][:, 0:512] for c in range(NC)]
    w1gsl = [wslots[0][c][:, 512:1024] for c in range(NC)]
    w2sl = [wslots[0][c][:, 0:1024] for c in range(4, NC)] +         [p.sb(f"{tag}_w2x{c}", [128, 1024], BF16) for c in range(2)][0:0]
    ffTg = [p.sb(f"{tag}_ffT{i}", [128, 2816], BF16) for i in range(1)]
    zpadb = p.sb(f"{tag}_zp", [128, 86], BF16)
    gctb = Ring(p, f"{tag}_ga", [128, 512], BF16, n=2)
    ffb2 = Ring(p, f"{tag}_ffb2", [128, 512], BF16, n=2)
    out_acc = p.sb(f"{tag}_oacc", [128, 1 * D], F32)
    xex = Ring(p, f"{tag}_xex", [128, D], F32, n=2)
    fin_toks = []
    b2b = p.sb(f"{tag}_b2b", [128, D], F32)
    for s0 in range(0, D, 512):
        m = p.t(lambda t, s0=s0: t.matmul(
            psB[:, s0:s0 + 512], ones1f[:], b2row[0:1, s0:s0 + 512],
            start=True, stop=True), after=[t6, c7_] + p.init_toks[-4:] + bct[-4:])
        p.init_toks.append(p.s(lambda s, s0=s0: s.activation(
            b2b[:, s0:s0 + 512], psB[:, s0:s0 + 512], AF.Copy), after=[m]))

    def load_shift(dst, src_fm, after):
        toks = []
        for c in range(NC):
            if c < 4:
                toks.append(p.dma("sync", dst[:, c * T:(c + 1) * T],
                                  src_fm[c], after=after))
            else:
                toks.append(p.dma("sync", dst[:, c * T + 1:(c + 1) * T],
                                  src_fm[c][:, 0:T - 1], after=after))
        z = p.v(lambda v: v.memset(
            dst[:].rearrange("p (c t) -> p c t", c=NC)[:, 4:8, 0:1], 0.0),
            after=after)
        return toks + [z]

    bat_g = []
    for b in range(B):
        lsh = load_shift(xsb, x_fm[b], bat_g)
        stores = {"q": [], "k": [], "v": []}
        W = FMWeight(p, f"{tag}_qkv{b}", wqkv[li], D, 3 * L1I, ring=wring,
                     slots=wslots)
        mlast = []
        for i in range(NT):
            mms = []
            for sp in range(3):
                for c in range(NC):
                    mm = p.t(lambda t, c=c, i=i, sp=sp, W=W, xsb=xsb: t.matmul(
                        psA[:, sp * 512:(sp + 1) * 512],
                        xsb[:, c * T + i * 128:c * T + i * 128 + 128],
                        W.chunks[c][:, sp * 512:(sp + 1) * 512],
                        start=(c == 0), stop=(c == NC - 1)),
                        after=lsh + (W.tok if i == 0 else []), sync=(c == NC - 1))
                mms.append(mm)
            mlast.append(mms[-1])
            for (nm, off, tabc, tabs, scb) in (("q", 0, xq, xq2, qsb),
                                               ("k", 512, xk, xk2, ksb)):
                qt = qv.next()
                ev = p.v(lambda v, qt=qt, off=off: v.tensor_copy(
                    qt[:], psA[:, off:off + 512]), after=mms + qv.guard)
                stb = st.next()
                p.wait("scalar", st.guard)
                sq2 = p.s(lambda s, qt=qt: s.activation(
                    sqtmp[:, 0:512], qt[:], AF.Square), after=[ev])
                r1b = p.v(lambda v, stb=stb: v.reduce_sum(
                    stb[:, 0:8], sqtmp[:, 0:512].rearrange("p (h d) -> p h d",
                                                           h=L1H),
                    axis=AX.X), after=[sq2])
                r2 = p.s(lambda s, stb=stb: s.activation(stb[:, 0:8], stb[:, 0:8],
                                                         AF.Sqrt), after=[r1b])
                r3 = p.v(lambda v, stb=stb: v.tensor_scalar_max(
                    stb[:, 0:8], stb[:, 0:8], 1e-12), after=[r2])
                r4 = p.v(lambda v, stb=stb: v.reciprocal(stb[:, 0:8], stb[:, 0:8]),
                         after=[r3])
                n1 = p.v(lambda v, qt=qt, stb=stb: v.tensor_mul(
                    qt[:].rearrange("p (h d) -> p h d", h=L1H),
                    qt[:].rearrange("p (h d) -> p h d", h=L1H),
                    stb[:, 0:8].unsqueeze(2).to_broadcast([128, L1H, DH])),
                    after=[r4])
                n2 = p.v(lambda v, qt=qt, scb=scb: v.tensor_mul(qt[:], qt[:],
                                                                scb[:]),
                         after=[n1])
                st.set_guard([n2])
                rb2 = rv.next()
                vi = qt[:].rearrange("p (h two i2) -> p h two i2", two=2, i2=32)
                vo = rb2[:].rearrange("p (h two i2) -> p h two i2", two=2, i2=32)
                svi = tabs[:, i * DH:(i + 1) * DH].rearrange(
                    "p (two i2) -> p two i2", two=2)
                x1_ = p.v(lambda v, vo=vo, vi=vi, svi=svi: v.tensor_mul(
                    vo[:, :, 0, :], vi[:, :, 1, :],
                    svi[:, 0, :].unsqueeze(1).to_broadcast([128, L1H, 32])),
                    after=[n2] + rv.guard)
                x2_ = p.v(lambda v, vo=vo, vi=vi, svi=svi: v.tensor_mul(
                    vo[:, :, 1, :], vi[:, :, 0, :],
                    svi[:, 1, :].unsqueeze(1).to_broadcast([128, L1H, 32])),
                    after=[n2])
                x3_ = p.v(lambda v, qt=qt, tabc=tabc, i=i: v.tensor_mul(
                    qt[:].rearrange("p (h e) -> p h e", e=DH),
                    qt[:].rearrange("p (h e) -> p h e", e=DH),
                    tabc[:, i * DH:(i + 1) * DH].unsqueeze(1).to_broadcast(
                        [128, L1H, DH])), after=[x1_, x2_])
                ob = bb.next()
                x4_ = p.v(lambda v, ob=ob, qt=qt, rb2=rb2: v.tensor_add(
                    ob[:, 0:512], qt[:], rb2[:]), after=[x3_] + bb.guard)
                rv.set_guard([x4_])
                qv.set_guard([x4_])
                tmp = tpb.next()
                p.wait("sync", tpb.guard)
                dst = q_fm if nm == "q" else k_fm
                so = store_fm(p, ob[:, 0:512], dst[b], i, [x4_], tmp, nch=4)
                tpb.set_guard([so])
                bb.set_guard([so])
                stores[nm].append(so)
            ob = bb.next()
            ev = p.s(lambda s, ob=ob: s.activation(ob[:, 0:512],
                                                   psA[:, 1024:1536], AF.Copy),
                     after=mms + bb.guard)
            p.wait("tensor", [ev])
            so = p.dma("gpsimd", v_tm[b, i * 128:(i + 1) * 128, :], ob[:, 0:512],
                       after=[ev])
            bb.set_guard([so])
            stores["v"].append(so)
        wslots[1]["toks"] = mlast
        qf = p.sb(f"{tag}_qf", [128, 4 * T], BF16) if b == 0 else qf
        kf = p.sb(f"{tag}_kf", [128, 4 * T], BF16) if b == 0 else kf
        Eexp = [p.sb(f"{tag}_Ee", [128, 640], BF16),
                p.sb(f"{tag}_Ee2", [128, 640], BF16)] if b == 0 else Eexp
        Fsb = [p.sb(f"{tag}_F", [128, 640], BF16),
               p.sb(f"{tag}_F2", [128, 640], BF16)] if b == 0 else Fsb
        opair = p.sb(f"{tag}_op", [128, 128], BF16) if b == 0 else opair
        oTb = p.sb(f"{tag}_oT", [128, 128], BF16) if b == 0 else oTb
        stt_ = p.sb(f"{tag}_stt", [128, 8], F32) if b == 0 else stt_
        lq = p.dma("sync", qf[:].rearrange("p (h t) -> p h t", h=4),
                   q_fm[b].rearrange("h p t -> p h t"), after=stores["q"] + bat_g)
        lk = p.dma("sync", kf[:].rearrange("p (h t) -> p h t", h=4),
                   k_fm[b].rearrange("h p t -> p h t"), after=stores["k"] + bat_g)
        vf = xsb
        lv = p.dma("scalar", vf[:, 0:NT * L1I].rearrange("p (i d) -> p i d", i=NT),
                   v_tm[b].rearrange("(i p) d -> p i d", p=128),
                   after=[mlast[-1]] + stores["v"] + bat_g)
        tEr = [lq, lq]
        tOr = [lq, lq]
        tFr = [lq, lq]
        prevT = [lq]
        prevTT = lq
        ostores = []
        for i in range(NT):
            w = i // 4
            kt0 = max(0, i - 4)
            nkt = i - kt0 + 1
            span = nkt * 128
            for hp in range(4):
                otoks = []
                for hh in range(2):
                    pp = (hp * 2 + hh) % 2
                    R = psA if pp == 0 else psB
                    q0 = pp * 4
                    qsl = qf[hh * 64:(hh + 1) * 64,
                              hp * T + i * 128:hp * T + i * 128 + 128]
                    mms = []
                    for s0 in range(0, span, 512):
                        s1 = min(s0 + 512, span)
                        mm = p.t(lambda t, qsl=qsl, s0=s0, s1=s1, hp=hp, hh=hh,
                                 kt0=kt0, R=R: t.matmul(
                            R[:, s0:s1], qsl,
                            kf[hh * 64:(hh + 1) * 64,
                               hp * T + kt0 * 128 + s0:hp * T + kt0 * 128 + s1],
                            start=True, stop=True),
                            after=[lq, lk, tEr[pp], tOr[pp]], sync=(s1 == span))
                        mms.append(mm)
                    dm = p.v(lambda v, span=span, R=R: v.tensor_add(
                        R[:, span - 128:span],
                        R[:, span - 128:span], maskb[:]), after=mms)
                    if i >= 4:
                        dm = p.v(lambda v, R=R: v.tensor_add(
                            R[:, 0:128], R[:, 0:128], maska[:]), after=[dm])
                    nmx = p.v(lambda v, span=span, R=R, q0=q0: v.reduce_max(
                        stt_[:, q0:q0 + 1], R[:, 0:span], axis=AX.X, negate=True),
                        after=[dm])
                    sc2 = p.v(lambda v, q0=q0: v.tensor_scalar_mul(
                        stt_[:, q0:q0 + 1], stt_[:, q0:q0 + 1], 8.0), after=[nmx])
                    ex = p.s(lambda s, span=span, R=R, q0=q0, pp=pp: s.activation(
                        Eexp[pp][:, 0:span], R[:, 0:span], AF.Exp,
                        bias=stt_[:, q0:q0 + 1],
                        scale=8.0, accum_out=stt_[:, q0 + 1:q0 + 2]),
                        after=[sc2, tFr[pp]])
                    tEr[pp] = ex
                    rc = p.v(lambda v, q0=q0: v.reciprocal(
                        stt_[:, q0 + 2:q0 + 3], stt_[:, q0 + 1:q0 + 2]),
                        after=[ex])
                    tF = p.dma("sync",
                               Fsb[pp][:, 0:span].rearrange("t (c p) -> t c p",
                                                            c=nkt),
                               Eexp[pp][:, 0:span].rearrange("p (c t) -> p c t",
                                                             c=nkt),
                               after=[ex], transpose=True)
                    tFr[pp] = tF
                    pvs = []
                    for c in range(nkt):
                        mm = p.t(lambda t, c=c, hp=hp, hh=hh, kt0=kt0, nkt=nkt,
                                 R=R, pp=pp: t.matmul(
                            R[:, 0:64], Fsb[pp][:, c * 128:(c + 1) * 128],
                            vf[:, (kt0 + c) * L1I + (hp * 2 + hh) * 64:
                               (kt0 + c) * L1I + (hp * 2 + hh) * 64 + 64],
                            start=(c == 0), stop=(c == nkt - 1)),
                            after=[tF, lv], sync=(c == nkt - 1))
                        pvs.append(mm)
                    oe = p.v(lambda v, hh=hh, R=R, q0=q0: v.tensor_scalar(
                        out=opair[:, hh * 64:(hh + 1) * 64], in0=R[:, 0:64],
                        scalar1=stt_[:, q0 + 2:q0 + 3], scalar2=None,
                        op0=ALU.mult),
                        after=[pvs[-1], rc] + ([prevTT] if hh == 0 else []))
                    tOr[pp] = oe
                    otoks.append(oe)
                tT = p.dma("sync", oTb[:], opair[:], after=otoks + prevT,
                           transpose=True)
                prevTT = tT
                so = p.dma("gpsimd", o_fm[b][hp][:, i * 128:(i + 1) * 128], oTb[:],
                           after=[tT])
                prevT = [so]
                ostores.append(so)
        Wo = FMWeight(p, f"{tag}_wo{b}", wo_d[li], L1I, D, ring=wring,
                      slots=wslots)
        osb = qf
        lo_ = p.dma("sync", osb[:, 0:4 * T].rearrange("p (c t) -> p c t", c=4),
                    o_fm[b].rearrange("c p t -> p c t"),
                    after=ostores + tOr + tEr + tFr)
        wom = []
        xmstores = []
        for i in range(NT):
            mms = []
            for half in range(2):
                for c in range(4):
                    mm = p.t(lambda t, c=c, i=i, half=half, Wo=Wo, osb=osb: t.matmul(
                        psA[:, half * 512:(half + 1) * 512],
                        osb[:, c * T + i * 128:c * T + i * 128 + 128],
                        Wo.chunks[c][:, half * 512:(half + 1) * 512],
                        start=(c == 0), stop=(c == 3)),
                        after=[lo_] + Wo.tok, sync=(c == 3))
                mms.append(mm)
            wom.append(mms[-1])
            xt = xtf.next()
            lx = p.dma("scalar", xt[:], x_tm[b, i * 128:(i + 1) * 128, :],
                       after=xtf.guard)
            ad = p.v(lambda v, xt=xt: v.tensor_add(xt[:], xt[:], psA[:, 0:1024]),
                     after=mms + [lx])
            p.wait("tensor", [ad])
            sx = p.dma("gpsimd", xm_tm[b, i * 128:(i + 1) * 128, :], xt[:],
                       after=[ad])
            ob = bb.next()
            cb = p.s(lambda s, ob=ob, xt=xt: s.activation(ob[:, 0:1024], xt[:],
                                                          AF.Copy),
                     after=[ad] + bb.guard)
            xtf.set_guard([sx, cb])
            tmp = tpb.next()
            p.wait("sync", tpb.guard)
            o2 = store_fm(p, ob[:, 0:1024], xm_fm[b], i, [cb], tmp, nch=NC)
            bb.set_guard([o2])
            tpb.set_guard([o2])
            xmstores += [sx, o2]
        wslots[1]["toks"] = wom
        hstores = []
        for i in range(NT):
            xt = xtf.next()
            p.wait("sync", xtf.guard)
            l_a = p.dma("sync", xt[:, 0:512],
                        xm_tm[b, i * 128:(i + 1) * 128, 0:512],
                        after=xtf.guard + xmstores)
            if i == 0:
                zr = p.v(lambda v, xt=xt: v.memset(xt[0:1, 512:1024], 0.0),
                         after=xtf.guard)
                l_b = p.dma("sync", xt[1:128, 512:1024],
                            xm_tm[b, 0:127, 512:1024],
                            after=xtf.guard + xmstores + [zr])
            else:
                zr = None
                l_b = p.dma("sync", xt[:, 512:1024],
                            xm_tm[b, i * 128 - 1:(i + 1) * 128 - 1, 512:1024],
                            after=xtf.guard + xmstores)
            up = [l_a, l_b] + ([zr] if zr else [])
            stb = st.next()
            sqb = sqr.next()
            p.wait("vector", st.guard + sqr.guard)
            tok, _ = ln_tile(p, xt, stb, sqb, after=up, eps=1e-5)
            g1 = p.v(lambda v, xt=xt: v.tensor_mul(xt[:], xt[:], lgb[:]),
                     after=[tok])
            g2 = p.v(lambda v, xt=xt: v.tensor_add(xt[:], xt[:], lbb[:]),
                     after=[g1])
            st.set_guard([g2])
            sqr.set_guard([g2])
            ob = bb.next()
            cb = p.s(lambda s, ob=ob, xt=xt: s.activation(ob[:, 0:1024], xt[:],
                                                          AF.Copy),
                     after=[g2] + bb.guard)
            xtf.set_guard([cb])
            tmp2 = tpb.next()
            p.wait("sync", tpb.guard)
            o2 = store_fm(p, ob[:, 0:1024], h_fm[b], i, [cb], tmp2, nch=NC)
            bb.set_guard([o2])
            tpb.set_guard([o2])
            hstores.append(o2)
        lh = p.dma("sync", xsb[:].rearrange("p (c t) -> p c t", c=NC),
                   h_fm[b].rearrange("c p t -> p c t"), after=hstores)
        ffstores = []
        prevj = []
        for jb in range(6):
            a0_ = jb * 512
            a1_ = min(a0_ + 512, L1HID)
            wid = a1_ - a0_
            wa_t, wg_t = [], []
            lb1a = p.dma("sync", b1rowa[0:1, 0:wid],
                         b1_d[li][a0_:a1_].unsqueeze(0), after=prevj)
            lb1g = p.dma("sync", b1rowg[0:1, 0:wid],
                         b1_d[li][L1HID + a0_:L1HID + a1_].unsqueeze(0),
                         after=prevj)
            for (sl, off, toks) in ((w1asl, a0_, wa_t), (w1gsl, L1HID + a0_, wg_t)):
                for c in range(NC):
                    dstb = wring.next()
                    l1 = p.dma("scalar", dstb[:, 0:wid],
                               w1_d[li][c * 128:(c + 1) * 128, off:off + wid],
                               after=wring.guard + prevj)
                    gg = p.v(lambda v, c=c, dstb=dstb, sl=sl, wid=wid: v.tensor_copy(
                        sl[c][:, 0:wid], dstb[:, 0:wid]), after=[l1])
                    wring.set_guard([gg])
                    toks.append(gg)
            jtoks = []
            for i in range(NT):
                mma = []
                for c in range(NC):
                    mm = p.t(lambda t, c=c, i=i, wid=wid: t.matmul(
                        psB[:, 0:wid],
                        xsb[:, c * T + i * 128:c * T + i * 128 + 128],
                        w1asl[c][:, 0:wid], start=(c == 0), stop=False),
                        after=[lh] + wa_t + ffstores[-2:], sync=False)
                mm1 = p.t(lambda t, wid=wid: t.matmul(
                    psB[:, 0:wid], ones1f[:], b1rowa[0:1, 0:wid],
                    start=False, stop=True), after=[lb1a], sync=True)
                for c in range(NC):
                    mm = p.t(lambda t, c=c, i=i, wid=wid: t.matmul(
                        psB[:, 512:512 + wid],
                        xsb[:, c * T + i * 128:c * T + i * 128 + 128],
                        w1gsl[c][:, 0:wid], start=(c == 0), stop=False),
                        after=wg_t, sync=False)
                mm2 = p.t(lambda t, wid=wid: t.matmul(
                    psB[:, 512:512 + wid], ones1f[:], b1rowg[0:1, 0:wid],
                    start=False, stop=True), after=[lb1g], sync=True)
                gb_ = gctb.next()
                ge = p.s(lambda s, gb_=gb_, wid=wid: s.activation(
                    gb_[:, 0:wid], psB[:, 512:512 + wid], AF.Gelu),
                    after=[mm2] + gctb.guard)
                fb_ = ffb2.next()
                fm_ = p.v(lambda v, gb_=gb_, fb_=fb_, wid=wid: v.tensor_mul(
                    fb_[:, 0:wid], psB[:, 0:wid], gb_[:, 0:wid]),
                    after=[mm1, ge] + ffb2.guard)
                gctb.set_guard([fm_])
                p.wait("tensor", [fm_])
                so = p.dma("gpsimd", ff_d[b, i, :, a0_:a1_], fb_[:, 0:wid],
                           after=[fm_])
                ffb2.set_guard([so])
                jtoks.append(so)
                ffstores.append(so)
            prevj = jtoks
        zp = p.v(lambda v: v.memset(zpadb[:, 0:86], 0.0))
        zps = [p.dma("gpsimd", ff_d[b, i, :, 2730:2816], zpadb[:, 0:86],
                     after=[zp]) for i in range(NT)]
        for g0 in range(0, NT, 1):
            zacc = p.v(lambda v: v.memset(out_acc[:], 0.0), after=fin_toks)
            gts = []
            for ii in range(1):
                fT = ffTg[ii]
                lf = p.dma("sync", fT[:].rearrange("t (c p) -> t c p", c=22),
                           ff_d[b, g0 + ii].rearrange("p (c t) -> p c t", c=22),
                           after=ffstores + zps + fin_toks, transpose=True)
                gts.append(lf)
            atoks = [zacc]
            w2prev = []
            for c4 in range(0, 22, 4):
                ncc = min(4, 22 - c4)
                w2t = []
                for cc in range(c4, c4 + ncc):
                    rows = 42 if cc == 21 else 128
                    dstb = wring.next()
                    l2 = p.dma("scalar", dstb[0:rows, 0:1024],
                               w2_d[li][cc * 128:cc * 128 + rows, :],
                               after=wring.guard + w2prev)
                    gg = p.v(lambda v, cc=cc, dstb=dstb, rows=rows: v.tensor_copy(
                        w2sl[cc % 4][0:rows, 0:1024], dstb[0:rows, 0:1024]),
                        after=[l2])
                    wring.set_guard([gg])
                    w2t.append(gg)
                mlist = []
                for ii in range(1):
                    for cc in range(c4, c4 + ncc):
                        rows = 42 if cc == 21 else 128
                        for half in range(2):
                            lastcc = (cc == c4 + ncc - 1)
                            lastmm = (lastcc and half == 1)
                            mm = p.t(lambda t, cc=cc, half=half, rows=rows,
                                     ii=ii, lastcc=lastcc, c4=c4: t.matmul(
                                psA[:, (ii % 2) * 1024 + half * 512:
                                    (ii % 2) * 1024 + half * 512 + 512],
                                ffTg[ii][0:rows, cc * 128:(cc + 1) * 128],
                                w2sl[cc % 4][0:rows, half * 512:half * 512 + 512],
                                start=(cc == c4), stop=lastcc),
                                after=gts + w2t + atoks, sync=lastmm)
                        if lastmm:
                            mlist.append(mm)
                    if ii % 2 == 1 or ii == 3:
                        pass
                    ac = p.v(lambda v, ii=ii, g0=g0: v.tensor_add(
                        out_acc[:, ii * D:(ii + 1) * D],
                        out_acc[:, ii * D:(ii + 1) * D],
                        psA[:, (ii % 2) * 1024:(ii % 2) * 1024 + 1024]),
                        after=[mlist[-1], zacc] + atoks)
                    atoks = [ac]
                    p.wait("tensor", [ac])
                w2prev = [atoks[0]]
            for ii in range(1):
                i = g0 + ii
                xt = xtf.next()
                lx = p.dma("scalar", xt[:], xm_tm[b, i * 128:(i + 1) * 128, :],
                           after=xtf.guard + xmstores)
                ad = p.v(lambda v, xt=xt, ii=ii: v.tensor_add(
                    xt[:], xt[:], out_acc[:, ii * D:(ii + 1) * D]),
                    after=[lx] + atoks)
                ad = p.v(lambda v, xt=xt: v.tensor_add(xt[:], xt[:], b2b[:]),
                         after=[ad])
                if final is not None:
                    x1_tm, a1_tm, mod_out = final
                    e1b = xex.next()
                    l1_ = p.dma("sync", e1b[:], x1_tm[b, i * 128:(i + 1) * 128, :],
                                after=xex.guard)
                    ad = p.v(lambda v, xt=xt, e1b=e1b: v.tensor_add(
                        xt[:], xt[:], e1b[:]), after=[ad, l1_])
                    l2_ = p.dma("sync", e1b[:], a1_tm[b, i * 128:(i + 1) * 128, :],
                                after=[ad])
                    ad2 = p.v(lambda v, xt=xt, e1b=e1b: v.tensor_add(
                        xt[:], xt[:], e1b[:]), after=[ad, l2_])
                    xex.set_guard([ad2])
                    so = p.dma("gpsimd", mod_out[b, i * 128:(i + 1) * 128, :],
                               xt[:], after=[ad2])
                    xtf.set_guard([so])
                    fin_toks.append(so)
                else:
                    so = p.dma("gpsimd", out_tm[b, i * 128:(i + 1) * 128, :],
                               xt[:], after=[ad])
                    ob = bb.next()
                    cb = p.s(lambda s, ob=ob, xt=xt: s.activation(
                        ob[:, 0:1024], xt[:], AF.Copy), after=[ad] + bb.guard)
                    xtf.set_guard([so, cb])
                    tmp = tpb.next()
                    p.wait("sync", tpb.guard)
                    o2 = store_fm(p, ob[:, 0:1024], out_fm[b], i, [cb], tmp,
                                  nch=NC)
                    bb.set_guard([o2])
                    tpb.set_guard([o2])
                    fin_toks += [so, o2]
        bat_g = fin_toks + tOr + tEr
    return


def build_l1attn():
    p = Prog("l1a")
    aps = {
        "xn1_fm": p.dram("xn1_fm", [B, NC, 128, T], BF16, kind="ExternalInput"),
        "w_qkv": p.dram("w_qkv", [D, 3 * D], F32, kind="ExternalInput"),
        "w_out": p.dram("w_out", [D, D], F32, kind="ExternalInput"),
        "b_out": p.dram("b_out", [D], F32, kind="ExternalInput"),
        "a1_tm": p.dram("a1_tm", [B, T, D], F32, kind="ExternalOutput"),
        "c_maskbias_causal": const_ap(p, "maskbias_causal"),
    }
    emit_l1attn(p, aps)
    return p


def build_delta():
    p = Prog("dl")
    aps = {
        "y_tm": p.dram("y_tm", [B, 512, D], F32, kind="ExternalInput"),
        "sel_tm": p.dram("sel_tm", [B, 512, D], F32, kind="ExternalInput"),
        "scores_sm": p.dram("scores_sm", [B, 128, 4], F32, kind="ExternalInput"),
        "dhi": p.dram("dhi", [B, 512, D], BF16, kind="ExternalOutput"),
        "dlo": p.dram("dlo", [B, 512, D], BF16, kind="ExternalOutput"),
    }
    emit_delta(p, aps)
    return p


def build_combine():
    p = Prog("cb")
    aps = {
        "x0": p.dram("x0", [B, T, D], F32, kind="ExternalInput"),
        "o_fm": p.dram("o_fm", [B, NH, 128, T], BF16, kind="ExternalInput"),
        "Wo": p.dram("Wo", [D, D], F32, kind="ExternalInput"),
        "proc_tm": p.dram("proc_tm", [B, T, D], F32, kind="ExternalInput"),
        "Smat": p.dram("Smat", [B, 4, 128, T], BF16, kind="ExternalInput"),
        "dhi": p.dram("dhi", [B, 512, D], BF16, kind="ExternalInput"),
        "dlo": p.dram("dlo", [B, 512, D], BF16, kind="ExternalInput"),
        "x1_tm": p.dram("x1_tm", [B, T, D], F32, kind="ExternalOutput"),
        "xn1_tm": p.dram("xn1_tm", [B, T, D], F32, kind="ExternalOutput"),
        "xn1_fm": p.dram("xn1_fm", [B, NC, 128, T], BF16, kind="ExternalOutput"),
    }
    emit_combine(p, aps)
    return p


def build_l1layer(li, final_stage=False):
    p = Prog(f"l1f{li}")
    aps = {
        "x_tm": p.dram("x_tm", [B, T, D], F32, kind="ExternalInput"),
        "x_fm": p.dram("x_fm", [B, NC, 128, T], BF16, kind="ExternalInput"),
        "wqkv": p.dram("wqkv", [2, D, 3 * L1I], F32, kind="ExternalInput"),
        "q_scale": p.dram("q_scale", [2, DH], F32, kind="ExternalInput"),
        "k_scale": p.dram("k_scale", [2, DH], F32, kind="ExternalInput"),
        "wo": p.dram("wo", [2, L1I, D], F32, kind="ExternalInput"),
        "ln_g": p.dram("ln_g", [2, D], F32, kind="ExternalInput"),
        "ln_b": p.dram("ln_b", [2, D], F32, kind="ExternalInput"),
        "w1": p.dram("w1", [2, D, 2 * L1HID], F32, kind="ExternalInput"),
        "b1": p.dram("b1", [2, 2 * L1HID], F32, kind="ExternalInput"),
        "w2": p.dram("w2", [2, L1HID, D], F32, kind="ExternalInput"),
        "b2": p.dram("b2", [2, D], F32, kind="ExternalInput"),
        "c_maskbias_causal": const_ap(p, "maskbias_causal"),
        "c_maskbias_anticausal": const_ap(p, "maskbias_anticausal"),
        "c_xpos_qc_bf": const_ap(p, "xpos_qc_bf"),
        "c_xpos_qs_bf": const_ap(p, "xpos_qs_bf"),
        "c_xpos_kc_bf": const_ap(p, "xpos_kc_bf"),
        "c_xpos_ks_bf": const_ap(p, "xpos_ks_bf"),
    }
    if final_stage:
        aps["x1_tm"] = p.dram("x1_tm", [B, T, D], F32, kind="ExternalInput")
        aps["a1_tm"] = p.dram("a1_tm", [B, T, D], F32, kind="ExternalInput")
        aps["mod_out"] = p.dram("mod_out", [B, T, D], F32, kind="ExternalOutput")
        final = (aps["x1_tm"], aps["a1_tm"], aps["mod_out"])
        out_tm = out_fm = None
    else:
        aps["out_tm"] = p.dram("out_tm", [B, T, D], F32, kind="ExternalOutput")
        aps["out_fm"] = p.dram("out_fm", [B, NC, 128, T], BF16,
                               kind="ExternalOutput")
        final = None
        out_tm, out_fm = aps["out_tm"], aps["out_fm"]
    emit_l1layer(p, aps, f"l1f{li}", li, aps["x_tm"], aps["x_fm"],
                 out_tm, out_fm, final=final)
    return p


def build_s0():
    p = Prog("s0")
    x = p.dram("x", [B, T, D], F32, kind="ExternalInput")
    xn_tm = p.dram("xn_tm", [B, T, D], F32, kind="ExternalOutput")
    xn_fm = p.dram("xn_fm", [B, NC, 128, T], BF16, kind="ExternalOutput")
    emit_s0(p, x, xn_tm, xn_fm)
    return p



_ROPE1C = None
_ROPE1S = None


def _dec1_tables():
    global _ROPE1C, _ROPE1S
    if _ROPE1C is None:
        half = 32
        inv = 1.0 / (10000.0 ** (np.arange(half, dtype=np.float32) / half))
        ang = np.arange(T, dtype=np.float32)[:, None] * inv[None, :]
        c = np.concatenate([np.cos(ang), np.cos(ang)], 1).astype(np.float32)
        s = np.concatenate([-np.sin(ang), np.sin(ang)], 1).astype(np.float32)
        _ROPE1C = np.ascontiguousarray(np.broadcast_to(c[None], (B, T, DH)))
        _ROPE1S = np.ascontiguousarray(np.broadcast_to(s[None], (B, T, DH)))
    return _ROPE1C, _ROPE1S


_PCACHE = {}


def _prog(name, fn):
    if name not in _PCACHE:
        p = fn()
        finish(p)
        _PCACHE[name] = p.finalize()
    return _PCACHE[name]


def _run(name, fn, in_maps, outs):
    nc = _prog(name, fn)
    res = run_bass_kernel_spmd(nc, in_maps, core_ids=list(range(NCORE)))
    return [{k: r[k] for k in outs} for r in res.results]


def kernel(**inputs):
    if FUSED:
        return kernel_fused(**inputs)
    inp = {k: np.ascontiguousarray(v) for k, v in inputs.items()}
    x_full = inp["L0_attn_x"].astype(np.float32, copy=False)
    shards = [np.ascontiguousarray(x_full[2 * c:2 * c + 2]) for c in range(NCORE)]

    def rep(arrnames):
        return {k: inp[k] for k in arrnames}

    r = _run("s0", build_s0, [{"x": s} for s in shards], ["xn_tm", "xn_fm"])
    xn_tm = [c["xn_tm"] for c in r]
    xn_fm = [c["xn_fm"] for c in r]

    cst = const_feed([])
    im = [{"xn_fm": xn_fm[c], "Wq": inp["L0_attn_Wq"], "Wk": inp["L0_attn_Wk"],
           "Wv": inp["L0_attn_Wv"], "Ww": inp["L0_attn_Ww"],
           "Wbeta": inp["L0_attn_Wbeta"], "Wg": inp["L0_attn_Wg"],
           "conv_w": inp["L0_attn_conv_w"], "qn_w": inp["L0_attn_qn_w"],
           "kn_w": inp["L0_attn_kn_w"]} for c in range(NCORE)]
    r = _run("l0ap", build_l0a_proj, im,
             ["q_fm", "k_fm", "w_fm", "v_tm", "w_tm", "beta_tm", "negG_hm"])

    cst = const_feed(["ident_bf", "ident2_bf", "maskbias_causal"])
    im = [dict(r[c], **cst) for c in range(NCORE)]
    r2 = _run("l0ai", build_l0a_inner, im, ["o_fm"])
    o_fm_l0 = [c["o_fm"] for c in r2]

    rc1, rs1 = _dec1_tables()
    cst = const_feed(["maskbias_causal"])
    im = [{"x_fm": xn_fm[c], "x_tm": xn_tm[c], "wq": inp["L0_ffn_wq1"],
           "wk": inp["L0_ffn_wk1"], "wv": inp["L0_ffn_wv1"],
           "wo": inp["L0_ffn_wo1"], "ln1": inp["L0_ffn_ln1_1"],
           "ln2": inp["L0_ffn_ln2_1"], "wg": inp["L0_ffn_wg1"],
           "wu": inp["L0_ffn_wu1"], "wd": inp["L0_ffn_wd1"],
           "ropec": rc1, "ropes": rs1, **cst} for c in range(NCORE)]
    r3 = _run("dec1", lambda: build_dec(T), im, ["out_tm"])
    proc = [c["out_tm"] for c in r3]

    im = [{"xn_tm": xn_tm[c], "xn_fm": xn_fm[c], "proc_tm": proc[c],
           "prior_w": inp["L0_ffn_prior_w"],
           "prior_norm_w": inp["L0_ffn_prior_norm_w"]} for c in range(NCORE)]
    r4 = _run("r1", build_router1, im, ["dd", "partial"])
    allpart = np.ascontiguousarray(
        np.concatenate([c["partial"] for c in r4], 0))

    cst = const_feed(["ident_f32", "iota512_row", "iota128_col",
                      "ropeb_cos_bf", "ropeb_sin_bf"])
    im = [{"dd": r4[c]["dd"], "allpart": allpart, "proc_tm": proc[c], **cst}
          for c in range(NCORE)]
    r5 = _run("r2", build_router2, im,
              ["ST", "Smat", "scores_sm", "ropec2", "ropes2", "sel_tm",
               "sel_fm", "phi", "plo"])

    cst = const_feed(["maskbias_causal"])
    im = [{"x_fm": r5[c]["sel_fm"], "x_tm": r5[c]["sel_tm"],
           "wq": inp["L0_ffn_wq2"], "wk": inp["L0_ffn_wk2"],
           "wv": inp["L0_ffn_wv2"], "wo": inp["L0_ffn_wo2"],
           "ln1": inp["L0_ffn_ln1_2"], "ln2": inp["L0_ffn_ln2_2"],
           "wg": inp["L0_ffn_wg2"], "wu": inp["L0_ffn_wu2"],
           "wd": inp["L0_ffn_wd2"], "ropec": r5[c]["ropec2"],
           "ropes": r5[c]["ropes2"], **cst} for c in range(NCORE)]
    r6 = _run("dec2", lambda: build_dec(512), im, ["out_tm"])

    im = [{"y_tm": r6[c]["out_tm"], "sel_tm": r5[c]["sel_tm"],
           "scores_sm": r5[c]["scores_sm"]} for c in range(NCORE)]
    r7 = _run("dl", build_delta, im, ["dhi", "dlo"])

    im = [{"x0": shards[c], "o_fm": o_fm_l0[c], "Wo": inp["L0_attn_Wo"],
           "proc_tm": proc[c], "Smat": r5[c]["Smat"], "dhi": r7[c]["dhi"],
           "dlo": r7[c]["dlo"]} for c in range(NCORE)]
    r8 = _run("cb", build_combine, im, ["x1_tm", "xn1_tm", "xn1_fm"])

    cst = const_feed(["maskbias_causal"])
    im = [{"xn1_fm": r8[c]["xn1_fm"], "w_qkv": inp["L1_attn_w_qkv"],
           "w_out": inp["L1_attn_w_out"], "b_out": inp["L1_attn_b_out"], **cst}
          for c in range(NCORE)]
    r9 = _run("l1a", build_l1attn, im, ["a1_tm"])

    cst = const_feed(["maskbias_causal", "maskbias_anticausal", "xpos_qc_bf",
                      "xpos_qs_bf", "xpos_kc_bf", "xpos_ks_bf"])
    wl1 = {"wqkv": inp["L1_ffn_wqkv"], "q_scale": inp["L1_ffn_q_scale"],
           "k_scale": inp["L1_ffn_k_scale"], "wo": inp["L1_ffn_wo"],
           "ln_g": inp["L1_ffn_ln_g"], "ln_b": inp["L1_ffn_ln_b"],
           "w1": inp["L1_ffn_w1"], "b1": inp["L1_ffn_b1"],
           "w2": inp["L1_ffn_w2"], "b2": inp["L1_ffn_b2"]}
    im = [{"x_tm": r8[c]["xn1_tm"], "x_fm": r8[c]["xn1_fm"], **wl1, **cst}
          for c in range(NCORE)]
    r10 = _run("l1f0", lambda: build_l1layer(0), im, ["out_tm", "out_fm"])

    im = [{"x_tm": r10[c]["out_tm"], "x_fm": r10[c]["out_fm"], **wl1, **cst,
           "x1_tm": r8[c]["x1_tm"], "a1_tm": r9[c]["a1_tm"]}
          for c in range(NCORE)]
    r11 = _run("l1f1", lambda: build_l1layer(1, final_stage=True), im,
               ["mod_out"])

    out = np.concatenate([r11[c]["mod_out"] for c in range(NCORE)], 0)
    return out.astype(np.float32)



def build_fused():
    p = Prog("mk")
    E = "ExternalInput"
    g = {}
    g["x"] = p.dram("L0_attn_x", [B, T, D], F32, kind=E)
    wnames = [
        ("L0_attn_Wq", [D, D]), ("L0_attn_Wk", [D, D]), ("L0_attn_Wv", [D, D]),
        ("L0_attn_Ww", [D, D]), ("L0_attn_Wbeta", [D, 8]), ("L0_attn_Wg", [D, 8]),
        ("L0_attn_Wo", [D, D]), ("L0_attn_conv_w", [D, 4]),
        ("L0_attn_qn_w", [128]), ("L0_attn_kn_w", [128]),
        ("L0_ffn_ln1_1", [D]), ("L0_ffn_wq1", [D, D]), ("L0_ffn_wk1", [D, D]),
        ("L0_ffn_wv1", [D, D]), ("L0_ffn_wo1", [D, D]), ("L0_ffn_ln2_1", [D]),
        ("L0_ffn_wg1", [D, 4096]), ("L0_ffn_wu1", [D, 4096]),
        ("L0_ffn_wd1", [4096, D]), ("L0_ffn_ln1_2", [D]), ("L0_ffn_wq2", [D, D]),
        ("L0_ffn_wk2", [D, D]), ("L0_ffn_wv2", [D, D]), ("L0_ffn_wo2", [D, D]),
        ("L0_ffn_ln2_2", [D]), ("L0_ffn_wg2", [D, 4096]),
        ("L0_ffn_wu2", [D, 4096]), ("L0_ffn_wd2", [4096, D]),
        ("L0_ffn_prior_norm_w", [D]), ("L0_ffn_prior_w", [D, 2 * D]),
        ("L1_attn_w_qkv", [D, 3 * D]), ("L1_attn_w_out", [D, D]),
        ("L1_attn_b_out", [D]),
        ("L1_ffn_wqkv", [2, D, 3 * L1I]), ("L1_ffn_q_scale", [2, DH]),
        ("L1_ffn_k_scale", [2, DH]), ("L1_ffn_wo", [2, L1I, D]),
        ("L1_ffn_ln_g", [2, D]), ("L1_ffn_ln_b", [2, D]),
        ("L1_ffn_w1", [2, D, 2 * L1HID]), ("L1_ffn_b1", [2, 2 * L1HID]),
        ("L1_ffn_w2", [2, L1HID, D]), ("L1_ffn_b2", [2, D]),
    ]
    for (nm, sh) in wnames:
        g[nm] = p.dram(nm, sh, F32, kind=E)
    CN = ["ident_bf", "ident2_bf", "maskbias_causal", "maskbias_anticausal",
          "ident_f32", "iota512_row", "iota128_col", "ropeb_cos_bf",
          "ropeb_sin_bf", "xpos_qc_bf", "xpos_qs_bf", "xpos_kc_bf",
          "xpos_ks_bf", "rope1c", "rope1s"]
    cap = {("c_" + n): const_ap(p, n) for n in CN}
    mod_out = p.dram("mod_out", [B, T, D], F32, kind="ExternalOutput")

    I = "Internal"
    xn_tm = p.dram("xn_tm", [B, T, D], F32, kind=I)
    xn_fm = p.dram("xn_fm", [B, NC, 128, T], BF16, kind=I)
    q_fm = p.dram("q_fm", [B, NH, 128, T], BF16, kind=I)
    k_fm = p.dram("k_fm", [B, NH, 128, T], BF16, kind=I)
    w_fm = p.dram("w_fm", [B, NH, 128, T], BF16, kind=I)
    v_tm = p.dram("v_tm", [B, T, D], BF16, kind=I)
    w_tm = p.dram("w_tm", [B, T, D], BF16, kind=I)
    beta_tm = p.dram("beta_tm", [B, T, 8], F32, kind=I)
    negG_hm = p.dram("negG_hm", [B, NH, T], F32, kind=I)
    o_fm0 = p.dram("o_fm0", [B, NH, 128, T], BF16, kind=I)
    proc_tm = p.dram("proc_tm", [B, T, D], F32, kind=I)
    dd = p.dram("dd", [B, 2, NT, 128], F32, kind=I)
    partial = p.dram("partial", [1, 2], F32, kind=I)
    allsum = p.dram("allsum", [1, 2], F32, kind=I)
    ST_d = p.dram("ST", [B, NT, 128, 512], BF16, kind=I)
    S_d = p.dram("Smat", [B, 4, 128, T], BF16, kind=I)
    scores_sm = p.dram("scores_sm", [B, 128, 4], F32, kind=I)
    ropec2 = p.dram("ropec2", [B, 512, DH], F32, kind=I)
    ropes2 = p.dram("ropes2", [B, 512, DH], F32, kind=I)
    sel_tm = p.dram("sel_tm", [B, 512, D], F32, kind=I)
    sel_fm = p.dram("sel_fm", [B, NC, 128, 512], BF16, kind=I)
    phi = p.dram("phi", [B, T, D], BF16, kind=I)
    plo = p.dram("plo", [B, T, D], BF16, kind=I)
    y_tm = p.dram("y_tm", [B, 512, D], F32, kind=I)
    dhi = p.dram("dhi", [B, 512, D], BF16, kind=I)
    dlo = p.dram("dlo", [B, 512, D], BF16, kind=I)
    x1_tm = p.dram("x1_tm", [B, T, D], F32, kind=I)
    xn1_tm = p.dram("xn1_tm", [B, T, D], F32, kind=I)
    xn1_fm = p.dram("xn1_fm", [B, NC, 128, T], BF16, kind=I)
    a1_tm = p.dram("a1_tm", [B, T, D], F32, kind=I)
    l1x_tm = p.dram("l1x_tm", [B, T, D], F32, kind=I)
    l1x_fm = p.dram("l1x_fm", [B, NC, 128, T], BF16, kind=I)

    p.constcol(1e-5)
    p.constcol(1e-6)
    p.constcol(1.0)

    def stage(fn):
        p.push_scope()
        fn()
        p.barrier()
        p.pop_scope()

    stage(lambda: emit_s0(p, g["x"], xn_tm, xn_fm))
    stage(lambda: emit_l0a_proj(p, {
        "xn_fm": xn_fm, "q_fm": q_fm, "k_fm": k_fm, "w_fm": w_fm,
        "v_tm": v_tm, "w_tm": w_tm, "beta_tm": beta_tm, "negG_hm": negG_hm,
        "Wq": g["L0_attn_Wq"], "Wk": g["L0_attn_Wk"], "Wv": g["L0_attn_Wv"],
        "Ww": g["L0_attn_Ww"], "Wbeta": g["L0_attn_Wbeta"],
        "Wg": g["L0_attn_Wg"], "conv_w": g["L0_attn_conv_w"],
        "qn_w": g["L0_attn_qn_w"], "kn_w": g["L0_attn_kn_w"]}))
    stage(lambda: emit_l0a_inner(p, {
        "q_fm": q_fm, "k_fm": k_fm, "w_fm": w_fm, "v_tm": v_tm, "w_tm": w_tm,
        "beta_tm": beta_tm, "negG_hm": negG_hm, "o_fm": o_fm0,
        "c_ident_bf": cap["c_ident_bf"], "c_ident2_bf": cap["c_ident2_bf"],
        "c_maskbias_causal": cap["c_maskbias_causal"]}))
    stage(lambda: emit_dec(p, {"c_maskbias_causal": cap["c_maskbias_causal"]},
                           "dc1", T, xn_fm, xn_tm, proc_tm,
                           g["L0_ffn_wq1"], g["L0_ffn_wk1"], g["L0_ffn_wv1"],
                           g["L0_ffn_wo1"], g["L0_ffn_ln1_1"], g["L0_ffn_ln2_1"],
                           g["L0_ffn_wg1"], g["L0_ffn_wu1"], g["L0_ffn_wd1"],
                           cap["c_rope1c"], cap["c_rope1s"]))
    stage(lambda: emit_router1(p, {
        "xn_tm": xn_tm, "xn_fm": xn_fm, "proc_tm": proc_tm,
        "prior_w": g["L0_ffn_prior_w"],
        "prior_norm_w": g["L0_ffn_prior_norm_w"], "dd": dd, "partial": partial}))

    def do_cc():
        ccs = p.sem("ccs")
        dt = [(s, s.n) for q in p.dpool for s in p.dpool[q] if s.n > 0]

        def fn(e):
            return e.collective_compute(
                "AllReduce", ALU.add, replica_groups=[list(range(NCORE))],
                ins=[partial.opt()], outs=[allsum.opt()])
        p._emit("gpsimd", fn, dt, [(ccs, 1)])
        p._cc_tok = (ccs, 1)
    stage(do_cc)
    stage(lambda: emit_router2(p, {
        "dd": dd, "allpart": allsum, "allpart_tok": [p._cc_tok],
        "proc_tm": proc_tm, "ST": ST_d, "Smat": S_d, "scores_sm": scores_sm,
        "ropec2": ropec2, "ropes2": ropes2, "sel_tm": sel_tm, "sel_fm": sel_fm,
        "phi": phi, "plo": plo, "c_ident_f32": cap["c_ident_f32"],
        "c_iota512_row": cap["c_iota512_row"],
        "c_iota128_col": cap["c_iota128_col"],
        "c_ropeb_cos_bf": cap["c_ropeb_cos_bf"],
        "c_ropeb_sin_bf": cap["c_ropeb_sin_bf"]}))
    stage(lambda: emit_dec(p, {"c_maskbias_causal": cap["c_maskbias_causal"]},
                           "dc2", 512, sel_fm, sel_tm, y_tm,
                           g["L0_ffn_wq2"], g["L0_ffn_wk2"], g["L0_ffn_wv2"],
                           g["L0_ffn_wo2"], g["L0_ffn_ln1_2"], g["L0_ffn_ln2_2"],
                           g["L0_ffn_wg2"], g["L0_ffn_wu2"], g["L0_ffn_wd2"],
                           ropec2, ropes2))
    stage(lambda: emit_delta(p, {"y_tm": y_tm, "sel_tm": sel_tm,
                                 "scores_sm": scores_sm, "dhi": dhi,
                                 "dlo": dlo}))
    stage(lambda: emit_combine(p, {
        "x0": g["x"], "o_fm": o_fm0, "Wo": g["L0_attn_Wo"], "proc_tm": proc_tm,
        "Smat": S_d, "dhi": dhi, "dlo": dlo, "x1_tm": x1_tm, "xn1_tm": xn1_tm,
        "xn1_fm": xn1_fm}))
    stage(lambda: emit_l1attn(p, {
        "xn1_fm": xn1_fm, "w_qkv": g["L1_attn_w_qkv"],
        "w_out": g["L1_attn_w_out"], "b_out": g["L1_attn_b_out"],
        "a1_tm": a1_tm, "c_maskbias_causal": cap["c_maskbias_causal"]}))
    l1aps = {"wqkv": g["L1_ffn_wqkv"], "q_scale": g["L1_ffn_q_scale"],
             "k_scale": g["L1_ffn_k_scale"], "wo": g["L1_ffn_wo"],
             "ln_g": g["L1_ffn_ln_g"], "ln_b": g["L1_ffn_ln_b"],
             "w1": g["L1_ffn_w1"], "b1": g["L1_ffn_b1"], "w2": g["L1_ffn_w2"],
             "b2": g["L1_ffn_b2"],
             "c_maskbias_causal": cap["c_maskbias_causal"],
             "c_maskbias_anticausal": cap["c_maskbias_anticausal"],
             "c_xpos_qc_bf": cap["c_xpos_qc_bf"],
             "c_xpos_qs_bf": cap["c_xpos_qs_bf"],
             "c_xpos_kc_bf": cap["c_xpos_kc_bf"],
             "c_xpos_ks_bf": cap["c_xpos_ks_bf"]}
    stage(lambda: emit_l1layer(p, l1aps, "l1f0", 0, xn1_tm, xn1_fm,
                               l1x_tm, l1x_fm))
    stage(lambda: emit_l1layer(p, l1aps, "l1f1", 1, l1x_tm, l1x_fm,
                               None, None, final=(x1_tm, a1_tm, mod_out)))
    return p


FUSED = True


def kernel_fused(**inputs):
    inp = {k: np.ascontiguousarray(np.asarray(v)) for k, v in inputs.items()}
    x_full = inp["L0_attn_x"].astype(np.float32, copy=False)
    CN = ["ident_bf", "ident2_bf", "maskbias_causal", "maskbias_anticausal",
          "ident_f32", "iota512_row", "iota128_col", "ropeb_cos_bf",
          "ropeb_sin_bf", "xpos_qc_bf", "xpos_qs_bf", "xpos_kc_bf",
          "xpos_ks_bf", "rope1c", "rope1s"]
    cst = const_feed(CN)
    base = {k: inp[k] for k in inp if k != "L0_attn_x"}
    base.update(cst)
    nc = _prog("fused", build_fused)
    ims = [dict(base, L0_attn_x=np.ascontiguousarray(x_full[2 * c:2 * c + 2]))
           for c in range(NCORE)]
    res = run_bass_kernel_spmd(nc, ims, core_ids=list(range(NCORE)))
    out = np.concatenate([res.results[c]["mod_out"] for c in range(NCORE)], 0)
    return out.astype(np.float32)
